# version 13
# baseline (speedup 1.0000x reference)
import numpy as np
import concourse.bass as bass
import concourse.mybir as mybir

F32 = mybir.dt.float32
BF16 = mybir.dt.bfloat16
I32 = mybir.dt.int32
AF = mybir.ActivationFunctionType
ALU = mybir.AluOpType
AX = mybir.AxisListType

NT = 16
L1_SKIP_GLOBAL = False
L1_SKIP_LOCAL = False
L1_SKIP_NORM = False
L1_SKIP_GAV = False
L1_SKIP_LAV = False
L1_SKIP_ROW = False
L1_SKIP_TRANS = False
ND = 8
T = 2048
D = 1024
LN_EPS = 1e-5


def emit_consts(nc, ident, ident_f32=None):
    with nc.Block() as blk:
        @blk.gpsimd
        def _(g):
            g.memset(ident[:, :], 1.0)
            g.drain()
            g.affine_select(ident[:, :], ident[:, :], [[1, 128]], ALU.is_equal, 0.0,
                            channel_multiplier=-1)
            if ident_f32 is not None:
                g.memset(ident_f32[:, :], 1.0)
                g.drain()
                g.affine_select(ident_f32[:, :], ident_f32[:, :], [[1, 128]],
                                ALU.is_equal, 0.0, channel_multiplier=-1)


def emit_ln_stats(nc, p, xs, ws, shared=None):
    sums, sumsq, scale, bias = ws["sums"], ws["sumsq"], ws["scale"], ws["bias"]
    sq_scr = ws["sq_scr"]

    with nc.Block() as blk:
        @blk.scalar
        def _(s):
            for ti in range(NT):
                if ti:
                    s.drain()
                s.activation(sq_scr[:, :], xs[:, ti, :], AF.Square,
                             accum_out=sumsq[:, ti:ti + 1])

        @blk.vector
        def _(v):
            for ti in range(NT):
                v.reduce_sum(sums[:, ti:ti + 1], xs[:, ti, :], AX.X)

    import contextlib as _ctl
    with _ctl.ExitStack() as _e:
        if shared is None:
            c1 = _e.enter_context(nc.semaphore(p + "c1"))
            c2 = _e.enter_context(nc.semaphore(p + "c2"))
            Bc = 0
        else:
            (c1, c2) = shared[0]
            Bc = shared[1]
        with nc.Block() as blk:
            @blk.vector
            def _(v):
                iv = 1.0 / D
                v.tensor_scalar(sums[:, :], sums[:, :], iv, None, ALU.mult)
                v.tensor_scalar(sumsq[:, :], sumsq[:, :], iv, None, ALU.mult)
                v.drain()
                v.tensor_tensor(bias[:, :], sums[:, :], sums[:, :], ALU.mult)
                v.drain()
                v.scalar_tensor_tensor(scale[:, :], sumsq[:, :], LN_EPS, bias[:, :],
                                       ALU.add, ALU.subtract).then_inc(c1, 1)
                v.wait_ge(c2, Bc + 1)
                v.scalar_tensor_tensor(bias[:, :], sums[:, :], -1.0, scale[:, :],
                                       ALU.mult, ALU.mult)

            @blk.scalar
            def _(s):
                s.wait_ge(c1, Bc + 1)
                s.activation(sumsq[:, :], scale[:, :], AF.Ln)
                s.drain()
                s.activation(scale[:, :], sumsq[:, :], AF.Exp,
                             scale=-0.5).then_inc(c2, 1)


def emit_ln_to_lnT(nc, p, xs, ident, lnT, ws, shared=None):
    if shared is None:
        emit_ln_stats(nc, p + "s", xs, ws)
    else:
        emit_ln_stats(nc, p + "s", xs, ws, shared=(shared[0][0:2], shared[1]))
    scale, bias, ln_scr = ws["scale"], ws["bias"], ws["ln_scr"]
    import contextlib as _ctl
    with _ctl.ExitStack() as _e:
        if shared is None:
            ap_s = _e.enter_context(nc.semaphore(p + "ap"))
            tp_s = _e.enter_context(nc.semaphore(p + "tp"))
            ev_s = _e.enter_context(nc.semaphore(p + "ev"))
            Ba = Bt = Be = 0
        else:
            ap_s, tp_s, ev_s = shared[0][2:5]
            k = shared[1]
            Ba, Bt, Be = NT * k, 2 * NT * k, 2 * NT * k
        with (nc.psum_tensor(p + "tps", [128, 2, 4, 128], F32) as tps,
              nc.Block() as blk):
            @blk.vector
            def _(v):
                for ti in range(NT):
                    if ti >= 2:
                        v.wait_ge(tp_s, Bt + 2 * (ti - 1))
                    v.tensor_scalar(ln_scr[:, ti % 2, :], xs[:, ti, :],
                                    scale[:, ti:ti + 1], bias[:, ti:ti + 1],
                                    ALU.mult, ALU.add).then_inc(ap_s, 1)
                    if ti >= 1:
                        tj = ti - 1
                        v.wait_ge(tp_s, Bt + 2 * ti)
                        for h in range(2):
                            v.tensor_copy(
                                lnT[:, 4 * h:4 * h + 4, tj * 128:(tj + 1) * 128],
                                tps[:, h, :, :]).then_inc(ev_s, 1)
                tj = NT - 1
                v.wait_ge(tp_s, Bt + 2 * NT)
                for h in range(2):
                    v.tensor_copy(lnT[:, 4 * h:4 * h + 4, tj * 128:(tj + 1) * 128],
                                  tps[:, h, :, :]).then_inc(ev_s, 1)

            @blk.tensor
            def _(t):
                for ti in range(NT):
                    t.wait_ge(ap_s, Ba + ti + 1)
                    if ti >= 1:
                        t.wait_ge(ev_s, Be + 2 * ti)
                    for h in range(2):
                        for j in range(4):
                            dj = 4 * h + j
                            mm = t.matmul(tps[:, h, j, :],
                                          ln_scr[:, ti % 2, dj * 128:(dj + 1) * 128],
                                          ident[:, :], start=True, stop=True)
                        mm.then_inc(tp_s, 1)


def emit_l0ffn(nc, p, xs, ident, wg_d, wu_d, wd_d, ws, lnsh=None):
    lnT = ws["lnT"]
    emit_ln_to_lnT(nc, p + "ln", xs, ident, lnT, ws, shared=lnsh)

    wd_sb = ws["wd_sb"]
    wg_sb = ws["wg_sb"]
    wu_sb = ws["wu_sb"]
    hT = ws["hT"]
    silu_scr = ws["silu_scr"]

    NC_ = 24
    inv127 = float(1.0 / 127.0)
    inv127sq = float(1.0 / (127.0 * 127.0))

    with (nc.semaphore(p + "wd") as wd_s, nc.Block() as blk):
        @blk.gpsimd
        def _(g):
            for c in range(NC_):
                g.dma_start(wd_sb[:, c, :],
                            wd_d[c, :, :]).then_inc(wd_s, 16)
            g.wait_ge(wd_s, 16 * NC_)

    import contextlib as _ctl
    _es0 = _ctl.ExitStack()
    lda_s = _es0.enter_context(nc.semaphore(p + "lda"))
    ldb_s = _es0.enter_context(nc.semaphore(p + "ldb"))
    mm_s = _es0.enter_context(nc.semaphore(p + "mm"))
    si_s = _es0.enter_context(nc.semaphore(p + "si"))
    mu_s = _es0.enter_context(nc.semaphore(p + "mu"))
    dn_s = _es0.enter_context(nc.semaphore(p + "dn"))
    ev_s = _es0.enter_context(nc.semaphore(p + "ev"))
    for half in range(2):
        t0 = half * 1024
        B_ld, B_mm, B_si, B_mu = 384 * half, 48 * half, 24 * half, 24 * half
        B_dn, B_ev = 8 * half, 8 * half
        ld_par = [lda_s, ldb_s]
        with (nc.psum_tensor(p + f"gps{half}", [128, 2, 1024], F32) as gps,
              nc.psum_tensor(p + f"ups{half}", [128, 2, 1024], F32) as ups,
              nc.Block() as blk):
            @blk.gpsimd
            def _(g):
                for c in range(NC_):
                    if c >= 2:
                        g.wait_ge(mm_s, B_mm + 2 * (c - 1))
                    g.dma_start(wg_sb[:, c % 2, :, :],
                                wg_d[c, :, :].rearrange("(dc pp) ci -> pp dc ci", pp=128)
                                ).then_inc(ld_par[c % 2], 16)
                    g.dma_start(wu_sb[:, c % 2, :, :],
                                wu_d[c, :, :].rearrange("(dc pp) ci -> pp dc ci", pp=128)
                                ).then_inc(ld_par[c % 2], 16)

            @blk.tensor
            def _(t):
                for c in range(NC_):
                    t.wait_ge(ld_par[c % 2], B_ld + 32 * (c // 2 + 1))
                    if c >= 2:
                        t.wait_ge(si_s, B_si + c - 1)
                        t.wait_ge(mu_s, B_mu + c - 1)
                    for j in range(ND):
                        for n2 in range(2):
                            mm = t.matmul(gps[:, c % 2, n2 * 512:(n2 + 1) * 512],
                                          wg_sb[:, c % 2, j, :],
                                          lnT[:, j, t0 + n2 * 512:t0 + (n2 + 1) * 512],
                                          start=(j == 0), stop=(j == ND - 1))
                    mm.then_inc(mm_s, 1)
                    for j in range(ND):
                        for n2 in range(2):
                            mm = t.matmul(ups[:, c % 2, n2 * 512:(n2 + 1) * 512],
                                          wu_sb[:, c % 2, j, :],
                                          lnT[:, j, t0 + n2 * 512:t0 + (n2 + 1) * 512],
                                          start=(j == 0), stop=(j == ND - 1))
                    mm.then_inc(mm_s, 1)

            @blk.scalar
            def _(s):
                for c in range(NC_):
                    s.wait_ge(mm_s, B_mm + 2 * c + 1)
                    if c >= 1:
                        s.wait_ge(mu_s, B_mu + c)
                    s.activation(silu_scr[:, :], gps[:, c % 2, :], AF.Silu,
                                 scale=inv127).then_inc(si_s, 1)

            @blk.vector
            def _(v):
                for c in range(NC_):
                    v.wait_ge(mm_s, B_mm + 2 * c + 2)
                    v.wait_ge(si_s, B_si + c + 1)
                    v.tensor_tensor(hT[:, c, :], silu_scr[:, :],
                                    ups[:, c % 2, :], ALU.mult).then_inc(mu_s, 1)

        with (nc.psum_tensor(p + f"dps{half}", [128, 2, 1024], F32) as dps,
              nc.Block() as blk):
            @blk.tensor
            def _(t):
                for tl in range(8):
                    ti = 8 * half + tl
                    if tl >= 2:
                        t.wait_ge(ev_s, B_ev + tl - 1)
                    for c in range(NC_):
                        for n2 in range(2):
                            mm = t.matmul(dps[:, tl % 2, n2 * 512:(n2 + 1) * 512],
                                          hT[:, c, tl * 128:(tl + 1) * 128],
                                          wd_sb[:, c, n2 * 512:(n2 + 1) * 512],
                                          start=(c == 0), stop=(c == NC_ - 1))
                    mm.then_inc(dn_s, 1)

            @blk.vector
            def _(v):
                for tl in range(8):
                    ti = 8 * half + tl
                    v.wait_ge(dn_s, B_dn + tl + 1)
                    v.scalar_tensor_tensor(xs[:, ti, :], dps[:, tl % 2, :],
                                           inv127sq, xs[:, ti, :],
                                           ALU.mult, ALU.add).then_inc(ev_s, 1)
    _es0.close()


def build_l0ffn_standalone():
    nc = bass.Bass("TRN2", target_bir_lowering=False)
    x_in = nc.dram_tensor("x", [T, D], F32, kind="ExternalInput").ap()
    wg_d = nc.dram_tensor("wg", [24, 1024, 128], I32, kind="ExternalInput").ap()
    wu_d = nc.dram_tensor("wu", [24, 1024, 128], I32, kind="ExternalInput").ap()
    wd_d = nc.dram_tensor("wd", [24, 128, 1024], I32, kind="ExternalInput").ap()
    x_out = nc.dram_tensor("xo", [T, D], F32, kind="ExternalOutput").ap()

    with (
        nc.sbuf_tensor("xs", [128, NT, D], F32) as xs,
        nc.sbuf_tensor("ident", [128, 128], BF16) as ident,
        nc.sbuf_tensor("lnT", [128, ND, T], BF16) as lnT,
        nc.sbuf_tensor("wd_sb", [128, 24, 1024], BF16) as wd_sb,
        nc.sbuf_tensor("wg_sb", [128, 2, 8, 128], BF16) as wg_sb,
        nc.sbuf_tensor("wu_sb", [128, 2, 8, 128], BF16) as wu_sb,
        nc.sbuf_tensor("hT", [128, 24, 1024], BF16) as hT,
        nc.sbuf_tensor("silu_scr", [128, 1024], BF16) as silu_scr,
        nc.sbuf_tensor("sums", [128, NT], F32) as sums,
        nc.sbuf_tensor("sumsq", [128, NT], F32) as sumsq,
        nc.sbuf_tensor("scale", [128, NT], F32) as scale,
        nc.sbuf_tensor("bias", [128, NT], F32) as bias,
    ):
        sq_scr = silu_scr
        ln_scr = hT[:, 0:2, :].rearrange("p a b -> p a b") if False else None
        ws = dict(lnT=lnT, wd_sb=wd_sb, wg_sb=wg_sb, wu_sb=wu_sb, hT=hT,
                  silu_scr=silu_scr, sums=sums, sumsq=sumsq,
                  scale=scale, bias=bias, sq_scr=sq_scr, ln_scr=hT)
        with (nc.semaphore("xin") as xin_s, nc.Block() as blk):
            @blk.sync
            def _(s):
                s.dma_start(xs[:, :, :],
                            x_in.rearrange("(n pp) m -> pp n m", pp=128)
                            ).then_inc(xin_s, 16)
                s.wait_ge(xin_s, 16)
        emit_consts(nc, ident)
        emit_l0ffn(nc, "f0", xs, ident, wg_d, wu_d, wd_d, ws)
        with (nc.semaphore("xout") as xo_s, nc.Block() as blk):
            @blk.sync
            def _(s):
                s.dma_start(x_out.rearrange("(n pp) m -> pp n m", pp=128),
                            xs[:, :, :]).then_inc(xo_s, 16)
                s.wait_ge(xo_s, 16)
    return nc


def host_l0ffn_weights(w_gate, w_up, w_down):
    wg = np.ascontiguousarray(
        w_gate.T.reshape(1024, 24, 128).transpose(1, 0, 2))
    wu = np.ascontiguousarray(
        w_up.T.reshape(1024, 24, 128).transpose(1, 0, 2))
    wd = np.ascontiguousarray(
        w_down.T.reshape(24, 128, 1024))
    return wg, wu, wd


def emit_tern(nc, p, w_d, ntiles, W, tern, wtile, scr_f32, acc, smalls,
              onesf, scale_bc, wmean_bc, shared=None):
    count = float(ntiles * 128 * W)
    import contextlib as _ctl
    _et = _ctl.ExitStack()
    if shared is None:
        l1a = _et.enter_context(nc.semaphore(p + "l1a"))
        l1b = _et.enter_context(nc.semaphore(p + "l1b"))
        r1 = _et.enter_context(nc.semaphore(p + "r1"))
        t1 = _et.enter_context(nc.semaphore(p + "t1"))
        t2 = _et.enter_context(nc.semaphore(p + "t2"))
        l3a = _et.enter_context(nc.semaphore(p + "l3a"))
        l3b = _et.enter_context(nc.semaphore(p + "l3b"))
        r3 = _et.enter_context(nc.semaphore(p + "r3"))
        Bl1a = Bl1b = Br1 = Bt1 = Bt2 = Bl3a = Bl3b = Br3 = 0
    else:
        (l1a, l1b, r1, t1, t2, l3a, l3b, r3) = shared[0]
        (Bl1a, Bl1b, Br1, Bt1, Bt2, Bl3a, Bl3b, Br3) = shared[1]
    with nc.Block() as blk:
        ls1 = [(l1a, Bl1a), (l1b, Bl1b)]
        @blk.gpsimd
        def _(g):
            for i in range(ntiles):
                if i >= 2:
                    g.wait_ge(r1, Br1 + i - 1)
                g.dma_start(wtile[:, i % 2, 0:W],
                            w_d[i, :, :]).then_inc(ls1[i % 2][0], 16)

        @blk.vector
        def _(v):
            for i in range(ntiles):
                v.wait_ge(ls1[i % 2][0], ls1[i % 2][1] + 16 * (i // 2 + 1))
                v.tensor_reduce(acc[:, i:i + 1], wtile[:, i % 2, 0:W], AX.X,
                                ALU.add, apply_absolute_value=True).then_inc(r1, 1)
    with (nc.psum_tensor(p + "tp1", [128, 8], F32) as tp1,
          nc.Block() as blk):
        @blk.vector
        def _(v):
            v.tensor_reduce(smalls[:, 0:1], acc[:, 0:ntiles], AX.X, ALU.add)
            v.drain()
            v.nop().then_inc(t1, 1)
            v.wait_ge(t2, Bt2 + 1)
            v.tensor_copy(smalls[0:1, 1:2], tp1[0:1, 1:2])
            v.drain()
            v.tensor_scalar(smalls[0:1, 2:3], smalls[0:1, 1:2], 1.0 / count, None,
                            ALU.mult)
            v.drain()
            v.tensor_scalar(smalls[0:1, 3:4], smalls[0:1, 2:3], 1e-5, None, ALU.max)
            v.drain()
            v.reciprocal(smalls[0:1, 4:5], smalls[0:1, 3:4])
            v.drain()
            v.nop().then_inc(t1, 2)
            v.wait_ge(t2, Bt2 + 2)
            v.tensor_copy(wmean_bc[:, 0:1], tp1[:, 2:3])
            v.tensor_copy(scale_bc[:, 0:1], tp1[:, 3:4])

        @blk.tensor
        def _(t):
            t.wait_ge(t1, Bt1 + 1)
            t.matmul(tp1[0:1, 1:2], smalls[:, 0:1], onesf[:, 0:1],
                     start=True, stop=True).then_inc(t2, 1)
            t.wait_ge(t1, Bt1 + 3)
            t.matmul(tp1[:, 2:3], onesf[0:1, :], smalls[0:1, 3:4],
                     start=True, stop=True)
            mm = t.matmul(tp1[:, 3:4], onesf[0:1, :], smalls[0:1, 4:5],
                          start=True, stop=True)
            mm.then_inc(t2, 1)
    with nc.Block() as blk:
        ls3 = [(l3a, Bl3a), (l3b, Bl3b)]
        @blk.gpsimd
        def _(g):
            for i in range(ntiles):
                if i >= 2:
                    g.wait_ge(r3, Br3 + i - 1)
                g.dma_start(wtile[:, i % 2, 0:W],
                            w_d[i, :, :]).then_inc(ls3[i % 2][0], 16)

        @blk.vector
        def _(v):
            scr_i32 = scr_f32.bitcast(I32)
            for i in range(ntiles):
                v.wait_ge(ls3[i % 2][0], ls3[i % 2][1] + 16 * (i // 2 + 1))
                v.tensor_scalar(scr_f32[:, 0:W], wtile[:, i % 2, 0:W],
                                scale_bc[:, 0:1], None, ALU.mult)
                v.drain()
                v.tensor_copy(scr_i32[:, 0:W], scr_f32[:, 0:W])
                v.drain()
                v.tensor_copy(wtile[:, i % 2, 0:W], scr_i32[:, 0:W])
                v.drain()
                v.tensor_scalar(tern[:, i, 0:W], wtile[:, i % 2, 0:W],
                                1.0, -1.0, ALU.min, ALU.max).then_inc(r3, 1)


def emit_q1_to_q1T(nc, p, xs, ident, q1T, ws, shared=None):
    if shared is None:
        emit_ln_stats(nc, p + "s", xs, ws)
    else:
        emit_ln_stats(nc, p + "s", xs, ws, shared=(shared[0][0:2], shared[1]))
    scale, bias = ws["scale"], ws["bias"]
    lnf, lq_i32, lq_bf, q1c, smalls = (ws["lnf"], ws["lq_i32"], ws["lq_bf"],
                                       ws["q1c"], ws["smalls"])
    lq_f32 = ws["lq_f32"]
    import contextlib as _ctl
    _eq = _ctl.ExitStack()
    if shared is None:
        ap_s = _eq.enter_context(nc.semaphore(p + "ap"))
        tp_s = _eq.enter_context(nc.semaphore(p + "tp"))
        ev_s = _eq.enter_context(nc.semaphore(p + "ev"))
        Bqa = Bqt = Bqe = 0
    else:
        ap_s, tp_s, ev_s = shared[0][2:5]
        _k = shared[1]
        Bqa, Bqt, Bqe = NT * _k, 2 * NT * _k, 2 * NT * _k
    qa_s = _eq.enter_context(nc.semaphore(p + "qa"))
    qb_s = _eq.enter_context(nc.semaphore(p + "qb"))
    with (_eq, nc.psum_tensor(p + "tps", [128, 2, 4, 128], F32) as tps,
          nc.Block() as blk):
        @blk.vector
        def _(v):
            for ti in range(NT):
                if ti >= 2:
                    v.wait_ge(qb_s, ti - 1)
                v.tensor_scalar(lnf[:, ti % 2, :], xs[:, ti, :],
                                scale[:, ti:ti + 1], bias[:, ti:ti + 1],
                                ALU.mult, ALU.add)
                v.drain()
                v.tensor_reduce(smalls[:, 0:1], lnf[:, ti % 2, :], AX.X,
                                ALU.max, apply_absolute_value=True)
                v.drain()
                v.tensor_scalar(smalls[:, 1:2], smalls[:, 0:1], 1e-5, None, ALU.max)
                v.drain()
                v.tensor_scalar(smalls[:, 2:3], smalls[:, 1:2], 1.0 / 127.0,
                                None, ALU.mult)
                v.reciprocal(smalls[:, 3:4], smalls[:, 1:2])
                v.drain()
                v.tensor_scalar(smalls[:, 4:5], smalls[:, 3:4], 127.0, None,
                                ALU.mult).then_inc(qa_s, 1)
                v.wait_ge(qb_s, ti + 1)
                v.tensor_copy(lq_bf[:, :], lq_i32[:, :])
                if ti >= 1:
                    tj = ti - 1
                    v.wait_ge(tp_s, Bqt + 2 * ti)
                    for h in range(2):
                        v.tensor_copy(q1T[:, 4 * h:4 * h + 4, tj * 128:(tj + 1) * 128],
                                      tps[:, h, :, :]).then_inc(ev_s, 1)
                v.drain()
                v.tensor_scalar(q1c[:, :], lq_bf[:, :], smalls[:, 2:3],
                                None, ALU.mult).then_inc(ap_s, 1)
            tj = NT - 1
            v.wait_ge(tp_s, Bqt + 2 * NT)
            for h in range(2):
                v.tensor_copy(q1T[:, 4 * h:4 * h + 4, tj * 128:(tj + 1) * 128],
                              tps[:, h, :, :]).then_inc(ev_s, 1)

        @blk.scalar
        def _(s):
            for ti in range(NT):
                s.wait_ge(qa_s, ti + 1)
                s.activation(lq_i32[:, :], lnf[:, ti % 2, :], AF.Identity,
                             scale=smalls[:, 4:5]).then_inc(qb_s, 1)

        @blk.tensor
        def _(t):
            for ti in range(NT):
                t.wait_ge(ap_s, Bqa + ti + 1)
                if ti >= 1:
                    t.wait_ge(ev_s, Bqe + 2 * ti)
                for h in range(2):
                    for j in range(4):
                        dj = 4 * h + j
                        mm = t.matmul(tps[:, h, j, :],
                                      q1c[:, dj * 128:(dj + 1) * 128],
                                      ident[:, :], start=True, stop=True)
                    mm.then_inc(tp_s, 1)


def emit_l1ffn(nc, p, xs, ident, identf, onesf, w1_d, w2_d, convw_d, convb_d,
               hs_d, ws, lnsh=None):
    q1T = ws["q1T"]
    tern = ws["tern"]
    t1v = tern.rearrange("pp (a b) -> pp a b", a=8)
    t2v = tern.rearrange("pp (a b) -> pp a b", a=32)
    big16 = ws["big16"]
    wtile = big16.rearrange("pp (a b) -> pp a b", a=2)
    hs_sb = big16[:, 0:8192].rearrange("pp (a b) -> pp a b", a=32)
    scr = ws["scr16k"]
    acc, smalls = ws["acc"], ws["smalls"]
    M = ws["M"]
    s2, inv_s2, comb = ws["s2"], ws["inv_s2"], ws["comb"]
    cw_sb, cb_sb = ws["cw_sb"], ws["cb_sb"]
    s0c, smc, spc = ws["s0c"], ws["smc"], ws["spc"]
    sc1, wm1, sc2, wm2 = ws["sc1"], ws["wm1"], ws["sc2"], ws["wm2"]
    S2bc, I2bc = ws["S2bc"], ws["I2bc"]
    r1x = ws["r1x"]

    NC_ = 32

    import contextlib as _ctl
    _etn = _ctl.ExitStack()
    tsems = tuple(_etn.enter_context(nc.semaphore(p + "tn" + s))
                  for s in ("la", "lb", "r1", "t1", "t2", "lc", "ld", "r3"))
    tbase1 = (0, 0, 0, 0, 0, 0, 0, 0)
    tbase2 = (64, 64, 8, 3, 2, 64, 64, 8)

    emit_q1_to_q1T(nc, p + "q", xs, ident, q1T, ws, shared=lnsh)

    with (nc.semaphore(p + "cw") as cw_s, nc.Block() as blk):
        @blk.sync
        def _(s):
            s.dma_start(cw_sb[:, :, :], convw_d).then_inc(cw_s, 16)
            s.dma_start(cb_sb[:, :], convb_d).then_inc(cw_s, 16)
            s.wait_ge(cw_s, 32)
    emit_tern(nc, p + "t1", w1_d, 8, 4096, t1v, wtile,
              scr, acc, smalls, onesf, sc1, wm1, shared=(tsems, tbase1))
    with nc.Block() as blk:
        @blk.vector
        def _(v):
            for k, dst in ((0, smc), (1, s0c), (2, spc)):
                v.tensor_scalar(dst[:, :], cw_sb[:, :, k], wm1[:, 0:1],
                                None, ALU.mult)
        @blk.gpsimd
        def _(g):
            g.memset(M[:, :], -1e30)
            g.memset(ws["Mn"][:, :], 1e30)

    a2 = scr.rearrange("pp (a b) -> pp a b", a=2)
    hsp = ws["hsp"]
    Mn = ws["Mn"]
    with (nc.semaphore(p + "cm") as cm_s, nc.semaphore(p + "c1") as c1_s,
          nc.semaphore(p + "c3") as c3_s, nc.semaphore(p + "cs") as cs_s,
          nc.semaphore(p + "cg") as cg_s, nc.semaphore(p + "ch") as ch_s,
          nc.semaphore(p + "c5") as cm2_s,
          nc.psum_tensor(p + "hp", [128, 2, 2048], F32) as hp,
          nc.Block() as blk):
        @blk.tensor
        def _(t):
            for c in range(NC_):
                if c >= 2:
                    t.wait_ge(c3_s, c - 1)
                    t.wait_ge(cs_s, c - 1)
                for j in range(ND):
                    for n4 in range(4):
                        mm = t.matmul(hp[:, c % 2, n4 * 512:(n4 + 1) * 512],
                                      t1v[:, j, c * 128:(c + 1) * 128],
                                      q1T[:, j, n4 * 512:(n4 + 1) * 512],
                                      start=(j == 0), stop=(j == ND - 1))
                mm.then_inc(cm_s, 1)

        @blk.scalar
        def _(s):
            for c in range(NC_):
                s.wait_ge(cm_s, c + 1)
                if c >= 2:
                    s.wait_ge(c3_s, c - 1)
                s.activation(a2[:, c % 2, :], hp[:, c % 2, :], AF.Identity,
                             scale=s0c[:, c:c + 1],
                             bias=cb_sb[:, c:c + 1]).then_inc(c1_s, 1)
                if c >= 1:
                    s.wait_ge(c3_s, c)
                    s.wait_ge(cg_s, max(0, c - 1))
                    s.activation(hsp[:, :], a2[:, (c - 1) % 2, :],
                                 AF.Silu).then_inc(cs_s, 1)
            s.wait_ge(c3_s, NC_)
            s.wait_ge(cg_s, NC_ - 1)
            s.activation(hsp[:, :], a2[:, (NC_ - 1) % 2, :],
                         AF.Silu).then_inc(cs_s, 1)

        @blk.vector
        def _(v):
            for c in range(NC_):
                v.wait_ge(c1_s, c + 1)
                v.scalar_tensor_tensor(a2[:, c % 2, 1:2048], hp[:, c % 2, 0:2047],
                                       smc[:, c:c + 1], a2[:, c % 2, 1:2048],
                                       ALU.mult, ALU.add)
                v.drain()
                v.scalar_tensor_tensor(a2[:, c % 2, 0:2047], hp[:, c % 2, 1:2048],
                                       spc[:, c:c + 1], a2[:, c % 2, 0:2047],
                                       ALU.mult, ALU.add).then_inc(c3_s, 1)
                v.wait_ge(cs_s, c + 1)
                v.tensor_tensor(M[:, :], M[:, :], hsp[:, :], ALU.max)
                v.drain()
                v.tensor_tensor(Mn[:, :], Mn[:, :], hsp[:, :], ALU.min
                                ).then_inc(cm2_s, 1)
                v.drain()

        @blk.gpsimd
        def _(g):
            for c in range(NC_):
                g.wait_ge(cs_s, c + 1)
                g.dma_start(hs_d[c, :, :], hsp[:, :]).then_inc(ch_s, 16)
                g.wait_ge(ch_s, 16 * (c + 1))
                g.wait_ge(cm2_s, c + 1)
                g.nop().then_inc(cg_s, 1)

    with nc.Block() as blk:
        @blk.vector
        def _(v):
            v.tensor_scalar(Mn[:, :], Mn[:, :], -1.0, None, ALU.mult)
            v.drain()
            v.tensor_tensor(M[:, :], M[:, :], Mn[:, :], ALU.max)
    with (nc.semaphore(p + "dm") as dm_s, nc.semaphore(p + "dv") as dv_s,
          nc.psum_tensor(p + "mt", [128, 2, 512], F32) as mt,
          nc.Block() as blk):
        @blk.tensor
        def _(t):
            for tc in range(NT):
                if tc >= 2:
                    t.wait_ge(dv_s, tc - 1)
                t.matmul(mt[:, tc % 2, 0:128], M[:, tc * 128:(tc + 1) * 128],
                         ident[:, :], start=True, stop=True).then_inc(dm_s, 1)

        @blk.vector
        def _(v):
            for tc in range(NT):
                sc5 = smalls[:, 5 + tc % 2:6 + tc % 2]
                v.wait_ge(dm_s, tc + 1)
                v.tensor_reduce(sc5, mt[:, tc % 2, 0:128], AX.X, ALU.max)
                v.drain()
                v.tensor_scalar(s2[:, tc:tc + 1], sc5, 1e-5, None,
                                ALU.max).then_inc(dv_s, 1)
            v.drain()
            v.tensor_scalar(inv_s2[:, :], s2[:, :], 1.0 / 127.0, None, ALU.mult)
            v.drain()
            v.reciprocal(s2[:, :], s2[:, :])
            v.drain()
            v.tensor_scalar(s2[:, :], s2[:, :], 127.0, None, ALU.mult)

    emit_tern(nc, p + "t2", w2_d, 32, 1024, t2v, wtile,
              scr, acc, smalls, onesf, sc2, wm2, shared=(tsems, tbase2))
    with nc.Block() as blk:
        @blk.vector
        def _(v):
            v.tensor_scalar(comb[:, :], inv_s2[:, :], wm2[:, 0:1], None, ALU.mult)

    i32p = scr.bitcast(I32).rearrange("pp (a b) -> pp a b", a=2)
    import contextlib as _ctl
    _esf = _ctl.ExitStack()
    fl_s = _esf.enter_context(nc.semaphore(p + "fl"))
    fa_s = _esf.enter_context(nc.semaphore(p + "fa"))
    fb_s = _esf.enter_context(nc.semaphore(p + "fb"))
    fc_s = _esf.enter_context(nc.semaphore(p + "fc"))
    fm_s = _esf.enter_context(nc.semaphore(p + "fm"))
    fr_s = _esf.enter_context(nc.semaphore(p + "fr"))
    for e in range(8):
        Bl, Ba, Bb, Bc, Bm, Br = 128 * e, 2 * e, e, 32 * e, 2 * e, 2 * e
        with (nc.psum_tensor(p + f"f{e}p", [128, 2, 1024], F32) as dps,
              nc.psum_tensor(p + f"f{e}s", [128, 4, 256], F32) as sps,
              nc.Block() as blk):
            @blk.sync
            def _(s):
                for cg in range(8):
                    s.dma_start(
                        hs_sb[:, cg * 4:(cg + 1) * 4, :],
                        hs_d[cg * 4:(cg + 1) * 4, :, e * 256:(e + 1) * 256]
                        .rearrange("c pp t -> pp c t")).then_inc(fl_s, 16)

            @blk.tensor
            def _(t):
                for k in range(2):
                    t.matmul(sps[0:1, 0, k * 128:(k + 1) * 128],
                             s2[:, 2 * e + k:2 * e + k + 1], identf[:, :],
                             start=True, stop=True)
                for k in range(2):
                    mm = t.matmul(sps[0:1, 1, k * 128:(k + 1) * 128],
                                  inv_s2[:, 2 * e + k:2 * e + k + 1], identf[:, :],
                                  start=True, stop=True)
                mm.then_inc(fa_s, 1)
                t.wait_ge(fb_s, Bb + 1)
                t.matmul(sps[:, 2, :], onesf[0:1, :], r1x[0:1, 0:256],
                         start=True, stop=True).then_inc(fa_s, 1)
                for tc in range(2):
                    t.wait_ge(fc_s, Bc + NC_)
                    if tc >= 1:
                        t.wait_ge(fr_s, Br + tc)
                    for c in range(NC_):
                        for n2 in range(2):
                            mm = t.matmul(dps[:, tc % 2, n2 * 512:(n2 + 1) * 512],
                                          hs_sb[:, c, tc * 128:(tc + 1) * 128],
                                          t2v[:, c, n2 * 512:(n2 + 1) * 512],
                                          start=(c == 0), stop=(c == NC_ - 1))
                    mm.then_inc(fm_s, 1)

            @blk.vector
            def _(v):
                v.wait_ge(fa_s, Ba + 1)
                v.tensor_copy(r1x[0:1, 0:256], sps[0:1, 0, 0:256])
                v.drain()
                v.nop().then_inc(fb_s, 1)
                v.wait_ge(fa_s, Ba + 2)
                v.tensor_copy(S2bc[:, :], sps[:, 2, :])
                v.drain()
                v.wait_ge(fl_s, Bl + 16 * 8)
                for c in range(NC_):
                    v.tensor_tensor(i32p[:, c % 2, 0:256], hs_sb[:, c, :],
                                    S2bc[:, :], ALU.mult)
                    v.drain()
                    v.tensor_copy(hs_sb[:, c, :], i32p[:, c % 2, 0:256]
                                  ).then_inc(fc_s, 1)
                for tc in range(2):
                    ti = 2 * e + tc
                    v.wait_ge(fm_s, Bm + tc + 1)
                    v.scalar_tensor_tensor(xs[:, ti, :], dps[:, tc % 2, :],
                                           comb[:, ti:ti + 1], xs[:, ti, :],
                                           ALU.mult, ALU.add).then_inc(fr_s, 1)


def build_l1ffn_standalone():
    nc = bass.Bass("TRN2", target_bir_lowering=False)
    x_in = nc.dram_tensor("x", [T, D], F32, kind="ExternalInput").ap()
    w1_d = nc.dram_tensor("w1", [8, 128, 4096], F32, kind="ExternalInput").ap()
    w2_d = nc.dram_tensor("w2", [32, 128, 1024], F32, kind="ExternalInput").ap()
    convw_d = nc.dram_tensor("cw", [128, 32, 3], F32, kind="ExternalInput").ap()
    convb_d = nc.dram_tensor("cb", [128, 32], F32, kind="ExternalInput").ap()
    hs_d = nc.dram_tensor("hs_scr", [32, 128, 2048], BF16).ap()
    x_out = nc.dram_tensor("xo", [T, D], F32, kind="ExternalOutput").ap()

    import contextlib
    with contextlib.ExitStack() as _es:
        def _al(name, shape, dt):
            return _es.enter_context(nc.sbuf_tensor(name, shape, dt))
        xs = _al("xs", [128, NT, D], F32)
        ident = _al("ident", [128, 128], BF16)
        identf = _al("identf", [128, 128], F32)
        onesf = _al("onesf", [128, 128], F32)
        q1T = _al("q1T", [128, ND, T], BF16)
        tern = _al("tern", [128, 32768], BF16)
        big16 = _al("big16", [128, 8192], BF16)
        scr16k = _al("scr16k", [128, 4096], F32)
        hsp = _al("hsp", [128, 2048], BF16)
        M = _al("Mt", [128, 2048], BF16)
        Mn = _al("Mn", [128, 2048], BF16)
        sums = _al("sums", [128, NT], F32)
        sumsq = _al("sumsq", [128, NT], F32)
        scale = _al("scale", [128, NT], F32)
        bias = _al("bias", [128, NT], F32)
        s2 = _al("s2", [128, NT], F32)
        inv_s2 = _al("inv_s2", [128, NT], F32)
        comb = _al("comb", [128, NT], F32)
        acc = _al("acc", [128, 32], F32)
        smalls = _al("smalls", [128, 8], F32)
        cw_sb = _al("cw_sb", [128, 32, 3], F32)
        cb_sb = _al("cb_sb", [128, 32], F32)
        s0c = _al("s0c", [128, 32], F32)
        smc = _al("smc", [128, 32], F32)
        spc = _al("spc", [128, 32], F32)
        sc1 = _al("sc1", [128, 1], F32)
        wm1 = _al("wm1", [128, 1], F32)
        sc2 = _al("sc2", [128, 1], F32)
        wm2 = _al("wm2", [128, 1], F32)
        S2bc = scr16k[:, 3584:3840]
        r1x = _al("r1x", [1, 256], F32)
        lnf = scr16k[:, 0:2048].rearrange("pp (a b) -> pp a b", a=2)
        lq_f32 = scr16k[:, 2048:3072]
        lq_i32 = lq_f32.bitcast(I32)
        lq_bf = scr16k[:, 3072:3584].bitcast(BF16)
        q1c = scr16k[:, 3584:4096].bitcast(BF16)
        ws = dict(sums=sums, sumsq=sumsq, scale=scale, bias=bias, sq_scr=lq_bf,
                  lnf=lnf, lq_i32=lq_i32, lq_f32=lq_f32, lq_bf=lq_bf, q1c=q1c,
                  smalls=smalls,
                  q1T=q1T, tern=tern, big16=big16, scr16k=scr16k, hsp=hsp,
                  M=M, Mn=Mn, s2=s2, inv_s2=inv_s2, comb=comb, acc=acc,
                  cw_sb=cw_sb, cb_sb=cb_sb, s0c=s0c, smc=smc, spc=spc,
                  sc1=sc1, wm1=wm1, sc2=sc2, wm2=wm2, S2bc=S2bc, I2bc=None,
                  r1x=r1x)
        with (nc.semaphore("xin") as xin_s, nc.Block() as blk):
            @blk.sync
            def _(s):
                s.dma_start(xs[:, :, :],
                            x_in.rearrange("(n pp) m -> pp n m", pp=128)
                            ).then_inc(xin_s, 16)
                s.wait_ge(xin_s, 16)
            @blk.gpsimd
            def _(g):
                g.memset(onesf[:, :], 1.0)
        emit_consts(nc, ident, identf)
        emit_l1ffn(nc, "g1", xs, ident, identf, onesf, w1_d, w2_d,
                   convw_d, convb_d, hs_d, ws)
        with (nc.semaphore("xout") as xo_s, nc.Block() as blk):
            @blk.sync
            def _(s):
                s.dma_start(x_out.rearrange("(n pp) m -> pp n m", pp=128),
                            xs[:, :, :]).then_inc(xo_s, 16)
                s.wait_ge(xo_s, 16)
    return nc


def host_l1ffn_weights(w1, conv_w, conv_b, w2):
    w1p = np.ascontiguousarray(w1.T.reshape(8, 128, 4096))
    w2p = np.ascontiguousarray(w2.T.reshape(32, 128, 1024))
    cwp = np.ascontiguousarray(
        conv_w.reshape(4096, 3).reshape(32, 128, 3).transpose(1, 0, 2))
    cbp = np.ascontiguousarray(conv_b.reshape(32, 128).T)
    return w1p, w2p, cwp, cbp



def _l1_window_pieces(j):
    blocks = sorted({(j - 1) % 32, j, (j + 1) % 32})
    out = []
    for b in blocks:
        s, l = 64 * b, 64
        if out and out[-1][0] + out[-1][1] == s:
            out[-1] = (out[-1][0], out[-1][1] + l)
        else:
            out.append((s, l))
    return out


def emit_l1attn(nc, p, xs, ident, onesb, wq_d, wk_d, wv_d, wo_d,
                bq_d, bk_d, brow_d, ws, stop=None, lnsh=None):
    lnT = ws["lnT"]
    qT, kT = ws["qT"], ws["kT"]
    v_aug = ws["v_aug"]
    wbuf = ws["wbuf"]
    scr9 = ws["scr9"]
    vT_scr = scr9[:, 0:4096].rearrange("pp (a b) -> pp a b", a=2)
    pl = scr9[:, 0:1536].rearrange("pp (a b) -> pp a b", a=4)
    pg = scr9[:, 0:1024].rearrange("pp (a b) -> pp a b", a=2)
    pr = scr9[:, 0:128].rearrange("pp (a b) -> pp a b", a=2)
    y_tok = scr9[:, 1536:3584].rearrange("pp (a b) -> pp a b", a=16)
    ytmp = scr9[:, 3584:4096].rearrange("pp (a b c) -> pp a b c", a=4, b=2)
    wb = ws["wbuf"]
    wbuf = wb[:, 0:2048].rearrange("pp (a b) -> pp a b", a=2)
    ysb = wb[:, 0:1040].rearrange("pp (a b c d) -> pp a b c d", a=4, b=2, c=2)
    gvp = ws["gvp"]
    bq_sb, bk_sb = ws["bq_sb"], ws["bk_sb"]
    smalls = ws["smalls"]
    zsc = smalls[:, 8:24]
    yT = lnT

    emit_ln_to_lnT(nc, p + "ln", xs, ident, lnT, ws, shared=lnsh)

    with (nc.semaphore(p + "b") as b_s, nc.Block() as blk):
        @blk.sync
        def _(s):
            s.dma_start(bq_sb[:, :], bq_d).then_inc(b_s, 16)
            s.dma_start(bk_sb[:, :], bk_d).then_inc(b_s, 16)
            s.dma_start(ws["bv128"][:, :], ws["bv128_d"]).then_inc(b_s, 16)
            s.wait_ge(b_s, 48)
        @blk.gpsimd
        def _(g):
            vh = v_aug.rearrange("pp tc (h e) -> pp tc h e", e=65)
            g.memset(vh[:, :, :, 64:65], 1.0)
            g.drain()
            g.memset(wb[0:1, 2048:2176], 1.0)

    if stop == "ln":
        return
    for which, w_d, dst, b_sb in (("q", wq_d, qT, bq_sb), ("k", wk_d, kT, bk_sb)):
        with (nc.semaphore(p + which + "w") as w_s,
              nc.semaphore(p + which + "m") as m_s,
              nc.semaphore(p + which + "e") as e_s,
              nc.psum_tensor(p + which + "ps", [128, 2, 2048], F32) as pps,
              nc.Block() as blk):
            @blk.gpsimd
            def _(g):
                for ocp in range(4):
                    for wave in range(4):
                        gw = 4 * ocp + wave
                        if gw >= 1:
                            g.wait_ge(m_s, gw)
                        for dj in range(2):
                            d = 2 * wave + dj
                            g.dma_start(wbuf[:, dj, 0:256],
                                        w_d[d * 128:(d + 1) * 128,
                                            256 * ocp:256 * (ocp + 1)]
                                        ).then_inc(w_s, 16)

            @blk.tensor
            def _(t):
                for ocp in range(4):
                    if ocp >= 1:
                        t.wait_ge(e_s, 2 * ocp)
                    for wave in range(4):
                        gw = 4 * ocp + wave
                        t.wait_ge(w_s, 32 * (gw + 1))
                        for dj in range(2):
                            d = 2 * wave + dj
                            for oc2 in range(2):
                                for n4 in range(4):
                                    mm = t.matmul(
                                        pps[:, oc2, n4 * 512:(n4 + 1) * 512],
                                        wbuf[:, dj, oc2 * 128:(oc2 + 1) * 128],
                                        lnT[:, d, n4 * 512:(n4 + 1) * 512],
                                        start=(d == 0), stop=(d == 7))
                        mm.then_inc(m_s, 1)

            @blk.vector
            def _(v):
                for ocp in range(4):
                    v.wait_ge(m_s, 4 * ocp + 4)
                    for oc2 in range(2):
                        oc = 2 * ocp + oc2
                        v.tensor_scalar(dst[:, oc, :], pps[:, oc2, :],
                                        b_sb[:, oc:oc + 1], None,
                                        ALU.add).then_inc(e_s, 1)

    if stop == "qk":
        return
    import contextlib as _ctl
    _esv = _ctl.ExitStack()
    vw_s = _esv.enter_context(nc.semaphore(p + "vw"))
    vm_s = _esv.enter_context(nc.semaphore(p + "vm"))
    vt_s = _esv.enter_context(nc.semaphore(p + "vt"))
    vv_s = _esv.enter_context(nc.semaphore(p + "vv"))
    for ocp in range(4):
        Bw, Bm, Bt, Bv = 128 * ocp, 2 * ocp, 8 * ocp, 8 * ocp
        w_s, m_s = vw_s, vm_s
        with (nc.psum_tensor(p + f"vps{ocp}", [128, 2, 2048], F32) as pps,
              nc.Block() as blk):
            @blk.gpsimd
            def _(g):
                for wave in range(4):
                    if wave >= 1:
                        g.wait_ge(m_s, 4 * ocp + wave)
                    for dj in range(2):
                        d = 2 * wave + dj
                        g.dma_start(wbuf[:, dj, 0:256],
                                    wv_d[d * 128:(d + 1) * 128,
                                         256 * ocp:256 * (ocp + 1)]
                                    ).then_inc(w_s, 16)

            @blk.tensor
            def _(t):
                for wave in range(4):
                    t.wait_ge(w_s, 128 * ocp + 32 * (wave + 1))
                    for dj in range(2):
                        d = 2 * wave + dj
                        for oc2 in range(2):
                            for n4 in range(4):
                                mm = t.matmul(
                                    pps[:, oc2, n4 * 512:(n4 + 1) * 512],
                                    wbuf[:, dj, oc2 * 128:(oc2 + 1) * 128],
                                    lnT[:, d, n4 * 512:(n4 + 1) * 512],
                                    start=(d == 0), stop=(d == 7))
                    mm.then_inc(m_s, 1)

            @blk.vector
            def _(v):
                v.wait_ge(m_s, 4 * ocp + 4)
                for oc2 in range(2):
                    oc = 2 * ocp + oc2
                    v.tensor_scalar(vT_scr[:, oc2, :], pps[:, oc2, :],
                                    ws["bv128"][:, oc:oc + 1], None, ALU.add)
        with (nc.psum_tensor(p + f"vtp{ocp}", [128, 2, 512], F32) as tps,
              nc.Block() as blk):
            @blk.tensor
            def _(t):
                for i in range(8):
                    oc2, s4 = i // 4, i % 4
                    if i >= 2:
                        t.wait_ge(vv_s, Bv + i - 1)
                    for t4 in range(4):
                        tc = 4 * s4 + t4
                        mm = t.matmul(tps[:, i % 2, 128 * t4:128 * (t4 + 1)],
                                      vT_scr[:, oc2, tc * 128:(tc + 1) * 128],
                                      ident[:, :], start=True, stop=True)
                    mm.then_inc(vt_s, 1)

            @blk.vector
            def _(v):
                for i in range(8):
                    oc2, s4 = i // 4, i % 4
                    oc = 2 * ocp + oc2
                    v.wait_ge(vt_s, Bt + i + 1)
                    dst = v_aug.rearrange("pp tc (h e) -> pp tc h e", e=65)
                    v.tensor_copy(
                        dst[:, 4 * s4:4 * s4 + 4, 2 * oc:2 * oc + 2, 0:64],
                        tps[:, i % 2, :].rearrange(
                            "pp (tc h e) -> pp tc h e", tc=4, h=2)
                        ).then_inc(vv_s, 1)

    _esv.close()
    if stop == "proj":
        return
    with nc.Block() as blk:
        @blk.vector
        def _(v):
            v.tensor_copy(gvp[0:64, :], v_aug[0:64, 0, :])
            v.tensor_copy(gvp[64:128, :], v_aug[64:128, 15, :])
    import contextlib as _ctl
    _esc = _ctl.ExitStack()
    sl_s = _esc.enter_context(nc.semaphore(p + "csl"))
    pe_s = _esc.enter_context(nc.semaphore(p + "cpe"))
    av_s = _esc.enter_context(nc.semaphore(p + "cav"))
    sg_s = _esc.enter_context(nc.semaphore(p + "csg"))
    pg_s = _esc.enter_context(nc.semaphore(p + "cpg"))
    ev2_s = _esc.enter_context(nc.semaphore(p + "cev"))
    rs_s = _esc.enter_context(nc.semaphore(p + "crs"))
    rp_s = _esc.enter_context(nc.semaphore(p + "crp"))
    ra_s = _esc.enter_context(nc.semaphore(p + "cra"))
    tt_s = _esc.enter_context(nc.semaphore(p + "ctt"))
    tv_s = _esc.enter_context(nc.semaphore(p + "ctv"))
    _gidx = [0]
    _np = 8
    if (stop or "").startswith("core1"):
        _np = 1
    elif (stop or "").startswith("core2"):
        _np = 2
    elif (stop or "").startswith("core4"):
        _np = 4
    for g_ in range(_np):
        hA, hB = 2 * g_, 2 * g_ + 1
        for G in range(4):
            if stop == "core1g" and G >= 1:
                continue
            q0, q1_ = 512 * G, 512 * (G + 1)
            jlist = [(8 * G - 1) % 32] + [8 * G + r for r in range(8)] \
                + [(8 * G + 8) % 32]
            jinfo = []
            for j in jlist:
                pieces = []
                off = 0
                for (s0, l0) in _l1_window_pieces(j):
                    lo, hi = max(s0, q0), min(s0 + l0, q1_)
                    if lo < hi:
                        pieces.append((lo, hi - lo, off))
                        off += hi - lo
                cons = {}
                for bb in ((j - 1) % 32, j, (j + 1) % 32):
                    ts = 64 * bb
                    if q0 <= ts < q1_:
                        for (lo, ln_, o2) in pieces:
                            if lo <= ts < lo + ln_:
                                cons[bb] = o2 + (ts - lo)
                jinfo.append((j, pieces, cons, off))

            BSL, BPE, BAV = 20 * _gidx[0], 20 * _gidx[0], 8 * _gidx[0]
            BSG, BPG, BEV = 4 * _gidx[0], 2 * _gidx[0], 16 * _gidx[0]
            _gidx[0] += 1
            with (nc.psum_tensor(p + f"a{g_}g{G}sl", [128, 2, 512], F32) as slp,
                  nc.psum_tensor(p + f"a{g_}g{G}yl", [128, 4, 512], F32) as ylp,
                  nc.psum_tensor(p + f"a{g_}g{G}yg", [128, 4, 2, 128], F32) as ygp,
                  nc.Block() as blk):
                def _nb(jj):
                    return jj - 2

                @blk.tensor
                def _(t):
                    for h2 in range(2 if not L1_SKIP_GLOBAL else 0):
                        base = 64 * h2
                        mm = None
                        for (gi, (kts, kl)) in enumerate(((0, 64), (1984, 64))):
                            mm = t.matmul(
                                slp[gi * 64:(gi + 1) * 64, h2, 0:512],
                                kT[base:base + 64, g_, kts:kts + kl],
                                qT[base:base + 64, g_, q0:q1_],
                                start=True, stop=True)
                        mm.then_inc(sg_s, 1)
                    if not L1_SKIP_GLOBAL:
                        t.wait_ge(pg_s, BPG + 2)
                    for h2 in range(2 if not (L1_SKIP_GLOBAL or L1_SKIP_GAV) else 0):
                        h = 2 * g_ + h2
                        mm = None
                        for qc in range(4):
                            mm = t.matmul(
                                ygp[:, qc, h2, 0:65],
                                pg[:, h2, qc * 128:(qc + 1) * 128],
                                gvp[:, 65 * h:65 * h + 65],
                                start=True, stop=True)
                        mm.then_inc(sg_s, 1)
                    for jj, (j, pieces, cons, w) in enumerate(
                            jinfo if not L1_SKIP_LOCAL else []):
                        kr = (j % 2) * 64
                        if jj >= 2:
                            t.wait_ge(pe_s, BPE + 2 * (jj - 1))
                        for h2 in range(2):
                            base = 64 * h2
                            mm = None
                            for (lo, ln_, o2) in pieces:
                                mm = t.matmul(
                                    slp[kr:kr + 64, h2, o2:o2 + ln_],
                                    kT[base:base + 64, g_, 64 * j:64 * j + 64],
                                    qT[base:base + 64, g_, lo:lo + ln_],
                                    start=True, stop=True)
                            mm.then_inc(sl_s, 1)
                        if jj < 2:
                            continue
                        bblk = (jlist[jj] - 1) % 32
                        assert bblk == jlist[jj - 1], (jj, jlist)
                        t.wait_ge(pe_s, BPE + 2 * (jj + 1))
                        nb = _nb(jj)
                        if nb >= 1:
                            t.wait_ge(ev2_s, BEV + 2 * nb)
                        qrow = (bblk % 2) * 64
                        for h2 in range(2):
                            h = 2 * g_ + h2
                            for cp in range(2):
                                bank = 2 * h2 + cp
                                mms = []
                                for jj2 in (jj - 2, jj - 1, jj):
                                    j2 = jinfo[jj2][0]
                                    if j2 % 2 != cp:
                                        continue
                                    po = jinfo[jj2][2][bblk]
                                    kr2 = (j2 % 2) * 64
                                    ktc2 = j2 // 2
                                    mms.append((jj2, po, kr2, ktc2))
                                mm = None
                                for ci, (jj2, po, kr2, ktc2) in enumerate(mms):
                                    mm = t.matmul(
                                        ylp[qrow:qrow + 64, bank, 0:65],
                                        pl[kr2:kr2 + 64, jj2 % 4, 192 * h2 + po:
                                           192 * h2 + po + 64],
                                        v_aug[kr2:kr2 + 64, ktc2,
                                              65 * h:65 * h + 65],
                                        start=(ci == 0), stop=(ci == len(mms) - 1))
                        mm.then_inc(av_s, 1)

                @blk.scalar
                def _(s):
                    for h2 in range(2 if not L1_SKIP_GLOBAL else 0):
                        s.wait_ge(sg_s, BSG + h2 + 1)
                        s.activation(pg[:, h2, :], slp[:, h2, 0:512], AF.Exp,
                                     scale=0.125).then_inc(pg_s, 1)
                    first = True
                    for jj, (j, pieces, cons, w) in enumerate(
                            jinfo if not L1_SKIP_LOCAL else []):
                        kr = (j % 2) * 64
                        if first and not L1_SKIP_GLOBAL:
                            s.wait_ge(sg_s, BSG + 4)
                            first = False
                        if jj >= 4:
                            s.wait_ge(av_s, BAV + jj - 3)
                        s.wait_ge(sl_s, BSL + 2 * jj + 2)
                        s.activation(pl[kr:kr + 64, jj % 4, :]
                                     .rearrange("p (h2 c) -> p h2 c", h2=2)
                                     [:, :, 0:w],
                                     slp[kr:kr + 64, :, 0:w],
                                     AF.Exp, scale=0.125).then_inc(pe_s, 2)

                @blk.vector
                def _(v):
                    if L1_SKIP_GLOBAL:
                        v.nop().then_inc(sg_s, 4)
                    if L1_SKIP_LOCAL:
                        v.nop().then_inc(av_s, 8)
                        v.nop().then_inc(ev2_s, 16)
                    if L1_SKIP_GAV and not L1_SKIP_GLOBAL:
                        v.nop().then_inc(sg_s, 2)
                    for jj, (j, pieces, cons, w) in enumerate(
                            jinfo if not L1_SKIP_LOCAL else []):
                        if jj < 2:
                            continue
                        nb = _nb(jj)
                        bblk = jlist[jj - 1]
                        qrow = (bblk % 2) * 64
                        v.wait_ge(av_s, BAV + nb + 1)
                        for h2 in range(2):
                            v.tensor_copy(
                                ysb[qrow:qrow + 64, nb // 2, h2, :, 0:65],
                                ylp[qrow:qrow + 64, 2 * h2:2 * h2 + 2, 0:65]
                                ).then_inc(ev2_s, 1)
                    v.wait_ge(sg_s, BSG + 4)
                    v.wait_ge(ev2_s, BEV + 16)
                    if not L1_SKIP_NORM:
                        ysv = ysb
                        ytk = y_tok[:, 4 * G:4 * G + 4, :].rearrange(
                            "p a (h e) -> p a h e", h=2)
                        zl = zsc[:, 0:8].rearrange("p (a h) -> p a h", a=4
                                                   ).unsqueeze(3)
                        zg = zsc[:, 8:16].rearrange("p (a h) -> p a h", a=4
                                                    ).unsqueeze(3)
                        v.tensor_tensor(ysv[:, :, :, 0, :], ysv[:, :, :, 0, :],
                                        ysv[:, :, :, 1, :], ALU.add)
                        v.drain()
                        v.reciprocal(zl, ysv[:, :, :, 0, 64:65])
                        v.reciprocal(zg, ygp[:, 0:4, :, 64:65])
                        v.drain()
                        v.tensor_tensor(ytk, ysv[:, :, :, 0, 0:64],
                                        zl.broadcast_to([128, 4, 2, 64]), ALU.mult)
                        v.tensor_tensor(ytmp[:, :, :, :], ygp[:, 0:4, :, 0:64],
                                        zg.broadcast_to([128, 4, 2, 64]), ALU.mult)
                        v.drain()
                        v.tensor_tensor(ytk, ytk, ytmp[:, :, :, :], ALU.add)

        if stop == "core1g":
            return
        if L1_SKIP_ROW:
            continue
        BRS, BRP, BRA = 8 * g_, 8 * g_, 8 * g_
        pr4 = scr9[:, 0:1024].rearrange("pp (a b) -> pp a b", a=2)
        with (nc.psum_tensor(p + f"r{g_}sp", [128, 2, 512], F32) as rsp,
              nc.psum_tensor(p + f"r{g_}yp", [64, 2, 512], F32) as ryp,
              nc.Block() as blk):
            @blk.tensor
            def _(t):
                for idx in range(8):
                    h2, sup = idx // 4, idx % 4
                    base = 64 * h2
                    h = 2 * g_ + h2
                    if idx >= 2:
                        t.wait_ge(rp_s, BRP + idx - 1)
                    for t4 in range(4):
                        tc = 4 * sup + t4
                        mm = t.matmul(rsp[:, idx % 2, 64 * t4:64 * t4 + 64],
                                      kT[base:base + 64, g_,
                                         tc * 128:(tc + 1) * 128],
                                      qT[base:base + 64, g_, 0:64],
                                      start=True, stop=True)
                    mm.then_inc(rs_s, 1)
                    t.wait_ge(rp_s, BRP + idx + 1)
                    for t4 in range(4):
                        tc = 4 * sup + t4
                        mm = t.matmul(ryp[0:64, h2, 0:65],
                                      pr4[:, idx % 2, 64 * t4:64 * t4 + 64],
                                      v_aug[:, tc, 65 * h:65 * h + 65],
                                      start=(sup == 0 and t4 == 0),
                                      stop=(sup == 3 and t4 == 3))
                    mm.then_inc(ra_s, 1)

            @blk.scalar
            def _(s):
                for idx in range(8):
                    s.wait_ge(rs_s, BRS + idx + 1)
                    if idx >= 2:
                        s.wait_ge(ra_s, BRA + idx - 1)
                    s.activation(pr4[:, idx % 2, 0:256], rsp[:, idx % 2, 0:256],
                                 AF.Exp, scale=0.125).then_inc(rp_s, 1)

            @blk.vector
            def _(v):
                v.wait_ge(ra_s, BRA + 8)
                for h2 in range(2):
                    v.reciprocal(smalls[0:64, 0:1], ryp[0:64, h2, 64:65])
                    v.drain()
                    v.tensor_scalar(y_tok[0:64, 0, 64 * h2:64 * h2 + 64],
                                    ryp[0:64, h2, 0:64],
                                    smalls[0:64, 0:1], None, ALU.mult)
                    v.drain()

        if L1_SKIP_TRANS:
            continue
        BTT, BTV = 16 * g_, 16 * g_
        with (nc.psum_tensor(p + f"t{g_}p", [128, 2, 512], F32) as tps,
              nc.Block() as blk):
            @blk.tensor
            def _(t):
                for tc in range(16):
                    if tc >= 2:
                        t.wait_ge(tv_s, BTV + tc - 1)
                    t.matmul(tps[:, tc % 2, 0:128], y_tok[:, tc, :], ident[:, :],
                             start=True, stop=True).then_inc(tt_s, 1)

            @blk.vector
            def _(v):
                for tc in range(16):
                    v.wait_ge(tt_s, BTT + tc + 1)
                    v.tensor_copy(yT[:, g_, tc * 128:(tc + 1) * 128],
                                  tps[:, tc % 2, 0:128]).then_inc(tv_s, 1)

    wo_q = wb[:, 0:2048].rearrange("pp (a b) -> pp a b", a=8)
    with (nc.semaphore(p + "ow") as ow_s, nc.semaphore(p + "om") as om_s,
          nc.semaphore(p + "oe") as oe_s,
          nc.psum_tensor(p + "ops", [128, 2, 512], F32) as ops,
          nc.Block() as blk):
        @blk.gpsimd
        def _(g):
            for dq in range(4):
                if dq >= 1:
                    g.wait_ge(om_s, 16 * dq)
                g.dma_start(wb[0:1, 2176:2432],
                            brow_d[0:1, dq * 256:(dq + 1) * 256]).then_inc(ow_s, 16)
                for g2 in range(8):
                    g.dma_start(wo_q[:, g2, :],
                                wo_d[g2 * 128:(g2 + 1) * 128,
                                     dq * 256:(dq + 1) * 256]).then_inc(ow_s, 16)

        @blk.tensor
        def _(t):
            for dq in range(4):
                t.wait_ge(ow_s, 16 * 9 * (dq + 1))
                for ti in range(NT):
                    n = 16 * dq + ti
                    if n >= 2:
                        t.wait_ge(oe_s, n - 1)
                    for g2 in range(8):
                        t.matmul(ops[:, ti % 2, 0:256],
                                 yT[:, g2, ti * 128:(ti + 1) * 128],
                                 wo_q[:, g2, :],
                                 start=(g2 == 0), stop=False)
                    mm = t.matmul(ops[:, ti % 2, 0:256], wb[0:1, 2048:2176],
                                  wb[0:1, 2176:2432],
                                  start=False, stop=True)
                    mm.then_inc(om_s, 1)

        @blk.vector
        def _(v):
            for dq in range(4):
                for ti in range(NT):
                    n = 16 * dq + ti
                    v.wait_ge(om_s, n + 1)
                    v.scalar_tensor_tensor(
                        xs[:, ti, dq * 256:(dq + 1) * 256], ops[:, ti % 2, 0:256],
                        1.0, xs[:, ti, dq * 256:(dq + 1) * 256],
                        ALU.mult, ALU.add).then_inc(oe_s, 1)


def build_l1attn_standalone():
    nc = bass.Bass("TRN2", target_bir_lowering=False)
    x_in = nc.dram_tensor("x", [T, D], F32, kind="ExternalInput").ap()
    wq_d = nc.dram_tensor("wq", [1024, 1024], F32, kind="ExternalInput").ap()
    wk_d = nc.dram_tensor("wk", [1024, 1024], F32, kind="ExternalInput").ap()
    wv_d = nc.dram_tensor("wv", [1024, 1024], F32, kind="ExternalInput").ap()
    wo_d = nc.dram_tensor("wo", [1024, 1024], F32, kind="ExternalInput").ap()
    bq_d = nc.dram_tensor("bq", [128, 8], F32, kind="ExternalInput").ap()
    bk_d = nc.dram_tensor("bk", [128, 8], F32, kind="ExternalInput").ap()
    bv_d = nc.dram_tensor("bv", [128, 8], F32, kind="ExternalInput").ap()
    bo_d = nc.dram_tensor("bo", [1, 1024], F32, kind="ExternalInput").ap()
    x_out = nc.dram_tensor("xo", [T, D], F32, kind="ExternalOutput").ap()

    import contextlib
    with contextlib.ExitStack() as _es:
        def _al(name, shape, dt):
            return _es.enter_context(nc.sbuf_tensor(name, shape, dt))
        xs = _al("xs", [128, NT, D], F32)
        ident = _al("ident", [128, 128], BF16)
        lnT = _al("lnT", [128, ND, T], BF16)
        qT = _al("qT", [128, ND, T], BF16)
        kT = _al("kT", [128, ND, T], BF16)
        v_aug = _al("v_aug", [128, NT, 1040], BF16)
        wbuf = _al("wbuf", [128, 2432], BF16)
        scr9 = _al("scr9", [128, 4096], BF16)
        gvp = _al("gvp", [128, 1040], BF16)
        f32scr = _al("f32scr", [128, 64], F32)
        sums = f32scr[:, 0:16]
        sumsq = f32scr[:, 16:32]
        scale = f32scr[:, 32:48]
        bias = f32scr[:, 48:64]
        bq_sb = f32scr[:, 32:40]
        bk_sb = f32scr[:, 40:48]
        bv128 = f32scr[:, 48:56]
        smalls = f32scr[:, 0:24]
        ln_scr = scr9[:, 1536:3584].rearrange("pp (a b) -> pp a b", a=2)
        sq_scr = scr9[:, 0:1024]
        ws = dict(lnT=lnT, qT=qT, kT=kT, v_aug=v_aug, wbuf=wbuf, scr9=scr9,
                  bq_sb=bq_sb, bk_sb=bk_sb, bv128=bv128, bv128_d=bv_d,
                  gvp=gvp, smalls=smalls,
                  sums=sums, sumsq=sumsq, scale=scale, bias=bias,
                  sq_scr=sq_scr, ln_scr=ln_scr)
        with (nc.semaphore("xin") as xin_s, nc.Block() as blk):
            @blk.sync
            def _(s):
                s.dma_start(xs[:, :, :],
                            x_in.rearrange("(n pp) m -> pp n m", pp=128)
                            ).then_inc(xin_s, 16)
                s.wait_ge(xin_s, 16)
        emit_consts(nc, ident)
        emit_l1attn(nc, "a1", xs, ident, None, wq_d, wk_d, wv_d, wo_d,
                    bq_d, bk_d, bo_d, ws)
        with (nc.semaphore("xout") as xo_s, nc.Block() as blk):
            @blk.sync
            def _(s):
                s.dma_start(x_out.rearrange("(n pp) m -> pp n m", pp=128),
                            xs[:, :, :]).then_inc(xo_s, 16)
                s.wait_ge(xo_s, 16)
    return nc


def host_l1attn_weights(Wq, bq, Wk, bk, Wv, bv, Wo, bo):
    wq = np.ascontiguousarray(Wq.T)
    wk = np.ascontiguousarray(Wk.T)
    wv = np.ascontiguousarray(Wv.T)
    wo = np.ascontiguousarray(Wo.T)
    bq128 = np.ascontiguousarray(bq.reshape(8, 128).T)
    bk128 = np.ascontiguousarray(bk.reshape(8, 128).T)
    bv128 = np.ascontiguousarray(bv.reshape(8, 128).T)
    bo_row = np.ascontiguousarray(bo.reshape(1, 1024))
    return wq, wk, wv, wo, bq128, bk128, bv128, bo_row


emit_l1attn_phased = emit_l1attn


def emit_l0attn(nc, p, xs, ident, wq_d, wk_d, wv_d, wo_d, ones3_d, ws, lnsh=None):
    lnT = ws["lnT"]
    qT = ws["qTa"]
    kT = ws["kTa"]
    v_aug = ws["v_aug"]
    wpb = ws["wpb"]
    y_tok = ws["y_tok0"]
    Bp = ws["Bp"]
    tril = ws["tril"]
    onescol = ws["onescol"]
    C_sb = ws["C_sb"]
    smalls = ws["smalls"]
    zc = ws["zc0"]
    yT = lnT

    emit_ln_to_lnT(nc, p + "ln", xs, ident, lnT, ws, shared=lnsh)

    with nc.Block() as blk:
        @blk.gpsimd
        def _(g):
            g.memset(tril[:, :], 1.0)
            g.drain()
            g.affine_select(tril[:, :], tril[:, :], [[1, 128]], ALU.is_ge, 0.0,
                            channel_multiplier=-1)
            g.drain()
            g.memset(onescol[:, :], 1.0)
            g.drain()
            g.memset(C_sb[:, :], 0.0)
            vh = v_aug.rearrange("pp tc (h e) -> pp tc h e", e=65)
            g.memset(vh[:, :, :, 64:65], 1.0)

    for which, w_d, dst in (("q", wq_d, qT), ("k", wk_d, kT)):
        with (nc.semaphore(p + which + "w") as w_s,
              nc.semaphore(p + which + "m") as m_s,
              nc.semaphore(p + which + "e") as e_s,
              nc.psum_tensor(p + which + "ps", [128, 2, 2048], F32) as pps,
              nc.Block() as blk):
            @blk.gpsimd
            def _(g):
                for dj in range(8):
                    g.dma_start(wpb[:, dj, 0:768],
                                w_d[dj * 128:(dj + 1) * 128, :]).then_inc(w_s, 16)

            @blk.tensor
            def _(t):
                t.wait_ge(w_s, 128)
                for oc in range(6):
                    if oc >= 2:
                        t.wait_ge(e_s, oc - 1)
                    for dj in range(8):
                        for n4 in range(4):
                            mm = t.matmul(
                                pps[:, oc % 2, n4 * 512:(n4 + 1) * 512],
                                wpb[:, dj, oc * 128:(oc + 1) * 128],
                                lnT[:, dj, n4 * 512:(n4 + 1) * 512],
                                start=(dj == 0), stop=(dj == 7))
                    mm.then_inc(m_s, 1)

            @blk.vector
            def _(v):
                for oc in range(6):
                    v.wait_ge(m_s, oc + 1)
                    v.tensor_scalar(dst[:, oc, :], pps[:, oc % 2, :],
                                    0.5, None, ALU.mult).then_inc(e_s, 1)

        with (nc.semaphore(p + which + "o") as o_s, nc.Block() as blk):
            @blk.gpsimd
            def _(g):
                nd = 0
                for oc in range(6):
                    for a_ in range(3):
                        r = 16 + 32 * a_
                        g.dma_start(dst[r:r + 1, oc, :],
                                    ones3_d[a_:a_ + 1, :]).then_inc(o_s, 16)
                        nd += 1
                g.wait_ge(o_s, 16 * nd)

    with (nc.semaphore(p + "vw") as w_s, nc.semaphore(p + "vm") as m_s,
          nc.semaphore(p + "ve") as e_s,
          nc.psum_tensor(p + "vps", [128, 2, 1024], F32) as vps,
          nc.Block() as blk):
        @blk.gpsimd
        def _(g):
            for dj in range(8):
                g.dma_start(wpb[:, dj, :],
                            wv_d[dj * 128:(dj + 1) * 128, :]).then_inc(w_s, 16)

        @blk.tensor
        def _(t):
            t.wait_ge(w_s, 128)
            for ti in range(NT):
                if ti >= 2:
                    t.wait_ge(e_s, ti - 1)
                for dj in range(8):
                    for n2 in range(2):
                        mm = t.matmul(vps[:, ti % 2, n2 * 512:(n2 + 1) * 512],
                                      lnT[:, dj, ti * 128:(ti + 1) * 128],
                                      wpb[:, dj, n2 * 512:(n2 + 1) * 512],
                                      start=(dj == 0), stop=(dj == 7))
                mm.then_inc(m_s, 1)

        @blk.vector
        def _(v):
            vh = v_aug.rearrange("pp tc (h e) -> pp tc h e", e=65)
            for ti in range(NT):
                v.wait_ge(m_s, ti + 1)
                v.tensor_copy(vh[:, ti, :, 0:64],
                              vps[:, ti % 2, :].rearrange("pp (h e) -> pp h e",
                                                          h=16)).then_inc(e_s, 1)

    import contextlib as _ctl
    _esc = _ctl.ExitStack()
    ss_s = _esc.enter_context(nc.semaphore(p + "css"))
    ph_s = _esc.enter_context(nc.semaphore(p + "cph"))
    p2_s = _esc.enter_context(nc.semaphore(p + "cp2"))
    av_s = _esc.enter_context(nc.semaphore(p + "cav"))
    cs_s = _esc.enter_context(nc.semaphore(p + "ccs"))
    nm_s = _esc.enter_context(nc.semaphore(p + "cnm"))
    ca_s = _esc.enter_context(nc.semaphore(p + "cca"))
    tt_s = _esc.enter_context(nc.semaphore(p + "ctt"))
    tv_s = _esc.enter_context(nc.semaphore(p + "ctv"))
    ab_s = _esc.enter_context(nc.semaphore(p + "cab"))

    def _batches(i):
        out = []
        j0 = 0
        while j0 <= i:
            out.append((j0, min(4, i + 1 - j0)))
            j0 += 4
        return out

    _binfo = []
    _gb = 0
    _ss = 0
    for i in range(NT):
        bl = _batches(i)
        for h in range(16):
            for k, (j0, nj) in enumerate(bl):
                diag = (j0 + nj - 1 == i)
                _binfo.append((i, h, k, j0, nj, _gb, _ss, diag))
                _gb += 1
                _ss += nj
    _bykey = {(i, h, k): e for e in _binfo for (i, h, k) in [(e[0], e[1], e[2])]}

    with (nc.psum_tensor(p + "sb", [128, 2, 512], F32) as sbat,
          nc.psum_tensor(p + "yp", [128, 16, 128], F32) as yps,
          nc.psum_tensor(p + "tp", [128, 2, 512], F32) as tps,
          nc.Block() as blk):
        @blk.tensor
        def _(t):
            for i in range(NT):
                bl = _batches(i)
                nb_ = len(bl)

                def emit_s(h, k):
                    (j0, nj) = bl[k]
                    gb = _bykey[(i, h, k)][5]
                    hb = 32 * (h % 3)
                    oc = h // 3
                    if gb >= 2:
                        t.wait_ge(ph_s, gb - 1)
                    for jj in range(nj):
                        j = j0 + jj
                        t.matmul(sbat[:, gb % 2, jj * 128:(jj + 1) * 128],
                                 kT[hb:hb + 17, oc, j * 128:(j + 1) * 128],
                                 qT[hb:hb + 17, oc, i * 128:(i + 1) * 128],
                                 start=True, stop=True).then_inc(ss_s, 1)

                def emit_av(h, k):
                    (j0, nj) = bl[k]
                    gb = _bykey[(i, h, k)][5]
                    t.wait_ge(ph_s, gb + 1)
                    if i >= 1 and h == 0 and k == 0:
                        t.wait_ge(nm_s, i)
                        t.wait_ge(ca_s, i)
                    for jj in range(nj):
                        j = j0 + jj
                        mm = t.matmul(
                            yps[:, h, 0:65],
                            Bp[:, gb % 2, jj * 128:(jj + 1) * 128],
                            v_aug[:, j, 65 * h:65 * h + 65],
                            start=(k == 0 and jj == 0), stop=False)
                    mm.then_inc(ab_s, 1)
                    if k == nb_ - 1:
                        mm = t.matmul(yps[:, h, 0:65], tril[:, :],
                                      v_aug[:, i, 65 * h:65 * h + 65],
                                      start=False, stop=(i == 0))
                        if i >= 1:
                            mm = t.matmul(yps[:, h, 0:65], onescol[:, :],
                                          ws["C_bf"][0:1, 65 * h:65 * h + 65],
                                          start=False, stop=True)
                        mm.then_inc(av_s, 1)

                stream = [(h, k) for h in range(16) for k in range(nb_)]
                for n, (h, k) in enumerate(stream):
                    emit_s(h, k)
                    if n >= 1:
                        emit_av(*stream[n - 1])
                emit_av(*stream[-1])
                if i >= 1:
                    t.wait_ge(nm_s, i)
                    for occ in range(8):
                        n = 8 * (i - 1) + occ
                        if n >= 2:
                            t.wait_ge(tv_s, n - 1)
                        t.matmul(tps[:, n % 2, 0:128],
                                 y_tok[:, (i - 1) % 2, occ * 128:(occ + 1) * 128],
                                 ident[:, :], start=True, stop=True
                                 ).then_inc(tt_s, 1)
                t.wait_ge(nm_s, i + 1)
                for h in range(16):
                    mm = t.matmul(yps[0:1, h, 64:128],
                                  ws["onesc2"][:, 0:1],
                                  v_aug[:, i, 65 * h:65 * h + 64],
                                  start=True, stop=True)
                mm.then_inc(cs_s, 1)
            for occ in range(8):
                n = 8 * (NT - 1) + occ
                t.wait_ge(tv_s, n - 1)
                t.matmul(tps[:, n % 2, 0:128],
                         y_tok[:, (NT - 1) % 2, occ * 128:(occ + 1) * 128],
                         ident[:, :], start=True, stop=True).then_inc(tt_s, 1)

        @blk.scalar
        def _(s):
            for (i, h, k, j0, nj, gb, ssb, diag) in _binfo:
                s.wait_ge(ss_s, ssb + nj)
                if gb >= 2:
                    s.wait_ge(ab_s, gb - 1)
                if diag:
                    s.activation(Bp[:, gb % 2, 0:128 * nj],
                                 sbat[:, gb % 2, 0:128 * nj],
                                 AF.Square).then_inc(p2_s, 1)
                else:
                    s.wait_ge(ph_s, gb)
                    s.activation(Bp[:, gb % 2, 0:128 * nj],
                                 sbat[:, gb % 2, 0:128 * nj],
                                 AF.Square).then_inc(ph_s, 1)

        @blk.vector
        def _(v):
            ndiag = 0
            for i in range(NT):
                bl = _batches(i)
                for h in range(16):
                    k = len(bl) - 1
                    (_, _, _, j0, nj, gb, ssb, diag) = _bykey[(i, h, k)]
                    assert diag
                    ndiag += 1
                    v.wait_ge(p2_s, ndiag)
                    v.tensor_tensor(Bp[:, gb % 2, 128 * (nj - 1):128 * nj],
                                    Bp[:, gb % 2, 128 * (nj - 1):128 * nj],
                                    tril[:, :], ALU.mult).then_inc(ph_s, 1)
                if i >= 1:
                    for occ in range(8):
                        n = 8 * (i - 1) + occ
                        v.wait_ge(tt_s, n + 1)
                        v.tensor_copy(yT[:, occ, (i - 1) * 128:i * 128],
                                      tps[:, n % 2, 0:128]).then_inc(tv_s, 1)
                v.wait_ge(av_s, 16 * (i + 1))
                if i >= 1:
                    v.wait_ge(tv_s, 8 * i)
                v.reciprocal(zc[:, 0:16].unsqueeze(2), yps[:, :, 64:65])
                v.drain()
                v.tensor_tensor(
                    y_tok[:, i % 2, :].rearrange("pp (h e) -> pp h e", h=16),
                    yps[:, :, 0:64],
                    zc[:, 0:16].unsqueeze(2).broadcast_to([128, 16, 64]),
                    ALU.mult).then_inc(nm_s, 1)
                v.wait_ge(cs_s, i + 1)
                cview = C_sb[0:1, :].rearrange("a (h e) -> a h e", e=65)
                v.tensor_tensor(cview[:, :, 0:64], cview[:, :, 0:64],
                                yps[0:1, :, 64:128][:, :, 0:64], ALU.add)
                v.drain()
                for h in range(16):
                    v.memset(C_sb[0:1, 65 * h + 64:65 * h + 65],
                             float(128 * (i + 1)))
                v.drain()
                v.tensor_copy(ws["C_bf"][0:1, :], C_sb[0:1, :])
                v.drain()
                v.nop().then_inc(ca_s, 1)
            for occ in range(8):
                n = 8 * (NT - 1) + occ
                v.wait_ge(tt_s, n + 1)
                v.tensor_copy(yT[:, occ, (NT - 1) * 128:NT * 128],
                              tps[:, n % 2, 0:128]).then_inc(tv_s, 1)
    _esc.close()

    with (nc.semaphore(p + "ow") as ow_s, nc.semaphore(p + "om") as om_s,
          nc.semaphore(p + "oe") as oe_s,
          nc.psum_tensor(p + "ops", [128, 2, 512], F32) as ops,
          nc.Block() as blk):
        @blk.gpsimd
        def _(g):
            for dq in range(4):
                if dq >= 1:
                    g.wait_ge(om_s, 16 * dq)
                for g2 in range(8):
                    g.dma_start(wpb[:, g2, 0:256],
                                wo_d[g2 * 128:(g2 + 1) * 128,
                                     dq * 256:(dq + 1) * 256]).then_inc(ow_s, 16)

        @blk.tensor
        def _(t):
            for dq in range(4):
                t.wait_ge(ow_s, 16 * 8 * (dq + 1))
                for ti in range(NT):
                    n = 16 * dq + ti
                    if n >= 2:
                        t.wait_ge(oe_s, n - 1)
                    for g2 in range(8):
                        mm = t.matmul(ops[:, ti % 2, 0:256],
                                      yT[:, g2, ti * 128:(ti + 1) * 128],
                                      wpb[:, g2, 0:256],
                                      start=(g2 == 0), stop=(g2 == 7))
                    mm.then_inc(om_s, 1)

        @blk.vector
        def _(v):
            for dq in range(4):
                for ti in range(NT):
                    n = 16 * dq + ti
                    v.wait_ge(om_s, n + 1)
                    v.scalar_tensor_tensor(
                        xs[:, ti, dq * 256:(dq + 1) * 256], ops[:, ti % 2, 0:256],
                        1.0, xs[:, ti, dq * 256:(dq + 1) * 256],
                        ALU.mult, ALU.add).then_inc(oe_s, 1)


def build_l0attn_standalone():
    nc = bass.Bass("TRN2", target_bir_lowering=False)
    x_in = nc.dram_tensor("x", [T, D], F32, kind="ExternalInput").ap()
    wq_d = nc.dram_tensor("wq", [1024, 768], F32, kind="ExternalInput").ap()
    wk_d = nc.dram_tensor("wk", [1024, 768], F32, kind="ExternalInput").ap()
    wv_d = nc.dram_tensor("wv", [1024, 1024], F32, kind="ExternalInput").ap()
    wo_d = nc.dram_tensor("wo", [1024, 1024], F32, kind="ExternalInput").ap()
    ones3_d = nc.dram_tensor("ones3", [3, 2048], F32, kind="ExternalInput").ap()
    x_out = nc.dram_tensor("xo", [T, D], F32, kind="ExternalOutput").ap()

    import contextlib
    with contextlib.ExitStack() as _es:
        def _al(name, shape, dt):
            return _es.enter_context(nc.sbuf_tensor(name, shape, dt))
        xs = _al("xs", [128, NT, D], F32)
        ident = _al("ident", [128, 128], BF16)
        lnT = _al("lnT", [128, ND, T], BF16)
        qTa = _al("qTa", [128, 6, T], BF16)
        kTa = _al("kTa", [128, 6, T], BF16)
        v_aug = _al("v_aug", [128, NT, 1040], BF16)
        wpb = _al("wpb", [128, 8, 1024], BF16)
        y_tok0 = _al("y_tok0", [128, 2, 1024], BF16)
        Bp = _al("Bp", [128, 2, 512], BF16)
        tril = _al("tril", [128, 128], BF16)
        onescol = _al("onescol", [1, 128], BF16)
        onesc2 = _al("onesc2", [128, 1], BF16)
        C_sb = _al("C_sb", [1, 1040], F32)
        C_bf = _al("C_bf", [1, 1040], BF16)
        f32scr = _al("f32scr", [128, 80], F32)
        sums = f32scr[:, 0:16]
        sumsq = f32scr[:, 16:32]
        scale = f32scr[:, 32:48]
        bias = f32scr[:, 48:64]
        zc0 = f32scr[:, 64:80]
        smalls = f32scr[:, 0:24]
        ln_scr = y_tok0
        sq_scr = y_tok0[:, 0, :]
        ws = dict(lnT=lnT, qTa=qTa, kTa=kTa, v_aug=v_aug, wpb=wpb,
                  y_tok0=y_tok0, Bp=Bp, tril=tril, onescol=onescol,
                  onesc2=onesc2, C_sb=C_sb, C_bf=C_bf, zc0=zc0, smalls=smalls,
                  sums=sums, sumsq=sumsq, scale=scale, bias=bias,
                  sq_scr=sq_scr, ln_scr=ln_scr)
        with (nc.semaphore("xin") as xin_s, nc.Block() as blk):
            @blk.sync
            def _(s):
                s.dma_start(xs[:, :, :],
                            x_in.rearrange("(n pp) m -> pp n m", pp=128)
                            ).then_inc(xin_s, 16)
                s.wait_ge(xin_s, 16)
            @blk.gpsimd
            def _(g):
                g.memset(ws["onesc2"][:, :], 1.0)
        emit_consts(nc, ident)
        emit_l0attn(nc, "a0", xs, ident, wq_d, wk_d, wv_d, wo_d, ones3_d, ws)
        with (nc.semaphore("xout") as xo_s, nc.Block() as blk):
            @blk.sync
            def _(s):
                s.dma_start(x_out.rearrange("(n pp) m -> pp n m", pp=128),
                            xs[:, :, :]).then_inc(xo_s, 16)
                s.wait_ge(xo_s, 16)
    return nc


def host_l0attn_weights(Wq, Wk, Wv, Wo):
    def pad6(W):
        out = np.zeros((1024, 768), dtype=W.dtype)
        for oc in range(6):
            for a_ in range(3):
                h = 3 * oc + a_
                if h >= 16:
                    break
                out[:, oc * 128 + 32 * a_:oc * 128 + 32 * a_ + 16] = \
                    W[:, 16 * h:16 * h + 16]
        return np.ascontiguousarray(out)
    ones3 = np.ones((3, 2048), dtype=np.float32)
    return pad6(Wq), pad6(Wk), np.ascontiguousarray(Wv), \
        np.ascontiguousarray(Wo), ones3


def build_fused():
    nc = bass.Bass("TRN2", target_bir_lowering=False)
    x_in = nc.dram_tensor("x", [T, D], F32, kind="ExternalInput").ap()
    a0wq = nc.dram_tensor("a0wq", [1024, 768], F32, kind="ExternalInput").ap()
    a0wk = nc.dram_tensor("a0wk", [1024, 768], F32, kind="ExternalInput").ap()
    a0wv = nc.dram_tensor("a0wv", [1024, 1024], F32, kind="ExternalInput").ap()
    a0wo = nc.dram_tensor("a0wo", [1024, 1024], F32, kind="ExternalInput").ap()
    ones3_d = nc.dram_tensor("ones3", [3, 2048], F32, kind="ExternalInput").ap()
    f0wg = nc.dram_tensor("f0wg", [24, 1024, 128], I32, kind="ExternalInput").ap()
    f0wu = nc.dram_tensor("f0wu", [24, 1024, 128], I32, kind="ExternalInput").ap()
    f0wd = nc.dram_tensor("f0wd", [24, 128, 1024], I32, kind="ExternalInput").ap()
    a1wq = nc.dram_tensor("a1wq", [1024, 1024], F32, kind="ExternalInput").ap()
    a1wk = nc.dram_tensor("a1wk", [1024, 1024], F32, kind="ExternalInput").ap()
    a1wv = nc.dram_tensor("a1wv", [1024, 1024], F32, kind="ExternalInput").ap()
    a1wo = nc.dram_tensor("a1wo", [1024, 1024], F32, kind="ExternalInput").ap()
    a1bq = nc.dram_tensor("a1bq", [128, 8], F32, kind="ExternalInput").ap()
    a1bk = nc.dram_tensor("a1bk", [128, 8], F32, kind="ExternalInput").ap()
    a1bv = nc.dram_tensor("a1bv", [128, 8], F32, kind="ExternalInput").ap()
    a1bo = nc.dram_tensor("a1bo", [1, 1024], F32, kind="ExternalInput").ap()
    g1w1 = nc.dram_tensor("g1w1", [8, 128, 4096], F32, kind="ExternalInput").ap()
    g1w2 = nc.dram_tensor("g1w2", [32, 128, 1024], F32, kind="ExternalInput").ap()
    g1cw = nc.dram_tensor("g1cw", [128, 32, 3], F32, kind="ExternalInput").ap()
    g1cb = nc.dram_tensor("g1cb", [128, 32], F32, kind="ExternalInput").ap()
    hs_d = nc.dram_tensor("hs_scr", [32, 128, 2048], BF16).ap()
    x_out = nc.dram_tensor("xo", [T, D], F32, kind="ExternalOutput").ap()

    import contextlib
    with contextlib.ExitStack() as es0:
        def _al0(name, shape, dt):
            return es0.enter_context(nc.sbuf_tensor(name, shape, dt))
        xs = _al0("xs", [128, NT, D], F32)
        ident = _al0("ident", [128, 128], BF16)
        LN = tuple(es0.enter_context(nc.semaphore("LN" + s))
                   for s in ("c1", "c2", "ap", "tp", "ev"))

        with (nc.semaphore("xin") as xin_s, nc.Block() as blk):
            @blk.sync
            def _(s):
                s.dma_start(xs[:, :, :],
                            x_in.rearrange("(n pp) m -> pp n m", pp=128)
                            ).then_inc(xin_s, 16)
                s.wait_ge(xin_s, 16)
        emit_consts(nc, ident)

        with contextlib.ExitStack() as es1:
            def _al(name, shape, dt):
                return es1.enter_context(nc.sbuf_tensor(name, shape, dt))
            lnT = _al("b1lnT", [128, ND, T], BF16)
            qTa = _al("b1qTa", [128, 6, T], BF16)
            kTa = _al("b1kTa", [128, 6, T], BF16)
            v_aug = _al("b1vaug", [128, NT, 1040], BF16)
            wpb = _al("b1wpb", [128, 8, 1024], BF16)
            y_tok0 = _al("b1ytok0", [128, 2, 1024], BF16)
            Bp = _al("b1Bp", [128, 2, 512], BF16)
            tril = _al("b1tril", [128, 128], BF16)
            onescol = _al("b1onescol", [1, 128], BF16)
            C_sb = _al("b1Csb", [1, 1040], F32)
            C_bf = _al("b1Cbf", [1, 1040], BF16)
            f32scr = _al("b1f32scr", [128, 80], F32)
            onesc2 = _al("b1onesc2", [128, 1], BF16)
            with nc.Block() as blk:
                @blk.gpsimd
                def _(g):
                    g.memset(onesc2[:, :], 1.0)
            ws = dict(lnT=lnT, qTa=qTa, kTa=kTa, v_aug=v_aug, wpb=wpb,
                      y_tok0=y_tok0, Bp=Bp, tril=tril, onescol=onescol,
                      onesc2=onesc2, C_sb=C_sb, C_bf=C_bf,
                      zc0=f32scr[:, 64:80], smalls=f32scr[:, 0:24],
                      sums=f32scr[:, 0:16], sumsq=f32scr[:, 16:32],
                      scale=f32scr[:, 32:48], bias=f32scr[:, 48:64],
                      sq_scr=y_tok0[:, 0, :], ln_scr=y_tok0)
            emit_l0attn(nc, "a0", xs, ident, a0wq, a0wk, a0wv, a0wo, ones3_d,
                        ws, lnsh=(LN, 0))

        with contextlib.ExitStack() as es2:
            def _al(name, shape, dt):
                return es2.enter_context(nc.sbuf_tensor(name, shape, dt))
            lnT = _al("b2lnT", [128, ND, T], BF16)
            wd_sb = _al("b2wd", [128, 24, 1024], BF16)
            wg_sb = _al("b2wg", [128, 2, 8, 128], BF16)
            wu_sb = _al("b2wu", [128, 2, 8, 128], BF16)
            hT = _al("b2hT", [128, 24, 1024], BF16)
            silu_scr = _al("b2silu", [128, 1024], BF16)
            sums = _al("b2sums", [128, NT], F32)
            sumsq = _al("b2sumsq", [128, NT], F32)
            scale = _al("b2scale", [128, NT], F32)
            bias = _al("b2bias", [128, NT], F32)
            ws = dict(lnT=lnT, wd_sb=wd_sb, wg_sb=wg_sb, wu_sb=wu_sb, hT=hT,
                      silu_scr=silu_scr, sums=sums, sumsq=sumsq,
                      scale=scale, bias=bias, sq_scr=silu_scr, ln_scr=hT)
            emit_l0ffn(nc, "f0", xs, ident, f0wg, f0wu, f0wd, ws, lnsh=(LN, 1))

        with contextlib.ExitStack() as es3:
            def _al(name, shape, dt):
                return es3.enter_context(nc.sbuf_tensor(name, shape, dt))
            lnT = _al("b3lnT", [128, ND, T], BF16)
            qT = _al("b3qT", [128, ND, T], BF16)
            kT = _al("b3kT", [128, ND, T], BF16)
            v_aug = _al("b3vaug", [128, NT, 1040], BF16)
            wbuf = _al("b3wbuf", [128, 2432], BF16)
            scr9 = _al("b3scr9", [128, 4096], BF16)
            gvp = _al("b3gvp", [128, 1040], BF16)
            f32scr = _al("b3f32scr", [128, 64], F32)
            ws = dict(lnT=lnT, qT=qT, kT=kT, v_aug=v_aug, wbuf=wbuf, scr9=scr9,
                      bq_sb=f32scr[:, 32:40], bk_sb=f32scr[:, 40:48],
                      bv128=f32scr[:, 48:56], bv128_d=a1bv,
                      gvp=gvp, smalls=f32scr[:, 0:24],
                      sums=f32scr[:, 0:16], sumsq=f32scr[:, 16:32],
                      scale=f32scr[:, 32:48], bias=f32scr[:, 48:64],
                      sq_scr=scr9[:, 0:1024],
                      ln_scr=scr9[:, 1536:3584].rearrange(
                          "pp (a b) -> pp a b", a=2))
            emit_l1attn(nc, "a1", xs, ident, None, a1wq, a1wk, a1wv, a1wo,
                        a1bq, a1bk, a1bo, ws, lnsh=(LN, 2))

        with contextlib.ExitStack() as es4:
            def _al(name, shape, dt):
                return es4.enter_context(nc.sbuf_tensor(name, shape, dt))
            q1T = _al("b4q1T", [128, ND, T], BF16)
            tern = _al("b4tern", [128, 32768], BF16)
            big16 = _al("b4big16", [128, 8192], BF16)
            scr16k = _al("b4scr16k", [128, 4096], F32)
            hsp = _al("b4hsp", [128, 2048], BF16)
            M = _al("b4Mt", [128, 2048], BF16)
            Mn = _al("b4Mn", [128, 2048], BF16)
            sums = _al("b4sums", [128, NT], F32)
            sumsq = _al("b4sumsq", [128, NT], F32)
            scale = _al("b4scale", [128, NT], F32)
            bias = _al("b4bias", [128, NT], F32)
            s2 = _al("b4s2", [128, NT], F32)
            inv_s2 = _al("b4invs2", [128, NT], F32)
            comb = _al("b4comb", [128, NT], F32)
            acc = _al("b4acc", [128, 32], F32)
            smalls = _al("b4smalls", [128, 8], F32)
            cw_sb = _al("b4cw", [128, 32, 3], F32)
            cb_sb = _al("b4cb", [128, 32], F32)
            s0c = _al("b4s0c", [128, 32], F32)
            smc = _al("b4smc", [128, 32], F32)
            spc = _al("b4spc", [128, 32], F32)
            sc1 = _al("b4sc1", [128, 1], F32)
            wm1 = _al("b4wm1", [128, 1], F32)
            sc2 = _al("b4sc2", [128, 1], F32)
            wm2 = _al("b4wm2", [128, 1], F32)
            r1x = _al("b4r1x", [1, 256], F32)
            identf = _al("b4identf", [128, 128], F32)
            onesf = _al("b4onesf", [128, 128], F32)
            with nc.Block() as blk:
                @blk.gpsimd
                def _(g):
                    g.memset(onesf[:, :], 1.0)
            emit_consts(nc, identf)
            lq_f32 = scr16k[:, 2048:3072]
            lq_bf = scr16k[:, 3072:3584].bitcast(BF16)
            ws = dict(sums=sums, sumsq=sumsq, scale=scale, bias=bias,
                      sq_scr=lq_bf,
                      lnf=scr16k[:, 0:2048].rearrange("pp (a b) -> pp a b", a=2),
                      lq_i32=lq_f32.bitcast(I32), lq_f32=lq_f32, lq_bf=lq_bf,
                      q1c=scr16k[:, 3584:4096].bitcast(BF16),
                      smalls=smalls,
                      q1T=q1T, tern=tern, big16=big16, scr16k=scr16k, hsp=hsp,
                      M=M, Mn=Mn, s2=s2, inv_s2=inv_s2, comb=comb, acc=acc,
                      cw_sb=cw_sb, cb_sb=cb_sb, s0c=s0c, smc=smc, spc=spc,
                      sc1=sc1, wm1=wm1, sc2=sc2, wm2=wm2,
                      S2bc=scr16k[:, 3584:3840], I2bc=None, r1x=r1x)
            emit_l1ffn(nc, "g1", xs, ident, identf, onesf, g1w1, g1w2,
                       g1cw, g1cb, hs_d, ws, lnsh=(LN, 3))

        with (nc.semaphore("xout") as xo_s, nc.Block() as blk):
            @blk.sync
            def _(s):
                s.dma_start(x_out.rearrange("(n pp) m -> pp n m", pp=128),
                            xs[:, :, :]).then_inc(xo_s, 16)
                s.wait_ge(xo_s, 16)
    return nc


def _pack_all(inputs):
    wq0, wk0, wv0, wo0, ones3 = host_l0attn_weights(
        inputs["L0_attn_Wq"], inputs["L0_attn_Wk"], inputs["L0_attn_Wv"],
        inputs["L0_attn_Wo"])
    wg0, wu0, wd0 = host_l0ffn_weights(
        inputs["L0_ffn_w_gate"], inputs["L0_ffn_w_up"], inputs["L0_ffn_w_down"])
    wq1, wk1, wv1, wo1, bq1, bk1, bv1, bo1 = host_l1attn_weights(
        inputs["L1_attn_Wq"], inputs["L1_attn_bq"], inputs["L1_attn_Wk"],
        inputs["L1_attn_bk"], inputs["L1_attn_Wv"], inputs["L1_attn_bv"],
        inputs["L1_attn_Wo"], inputs["L1_attn_bo"])
    w11, w21, cw1, cb1 = host_l1ffn_weights(
        inputs["L1_ffn_w1"], inputs["L1_ffn_conv_w"], inputs["L1_ffn_conv_b"],
        inputs["L1_ffn_w2"])
    return {"a0wq": wq0, "a0wk": wk0, "a0wv": wv0, "a0wo": wo0,
            "ones3": ones3, "f0wg": wg0, "f0wu": wu0, "f0wd": wd0,
            "a1wq": wq1, "a1wk": wk1, "a1wv": wv1, "a1wo": wo1,
            "a1bq": bq1, "a1bk": bk1, "a1bv": bv1, "a1bo": bo1,
            "g1w1": w11, "g1w2": w21, "g1cw": cw1, "g1cb": cb1}


def kernel(**inputs):
    from concourse.bass_utils import run_bass_kernel_spmd as _run
    gi = np.asarray(inputs["L1_attn_global_indices"]).reshape(-1).tolist()
    assert gi == [0, 63], f"kernel specialized to global_indices=[0,63], got {gi}"
    x = np.ascontiguousarray(inputs["L0_attn_hidden_states"], dtype=np.float32)
    W = _pack_all(inputs)
    ncore = 8
    nc = build_fused()
    maps = [dict(W, x=np.ascontiguousarray(x[i])) for i in range(ncore)]
    r = _run(nc, maps, core_ids=list(range(ncore)))
    return np.stack([np.asarray(r.results[i]["xo"]) for i in range(ncore)], axis=0)


# version 14
# speedup vs baseline: 1.0092x; 1.0092x over previous
import numpy as np
import concourse.bass as bass
import concourse.mybir as mybir

F32 = mybir.dt.float32
BF16 = mybir.dt.bfloat16
I32 = mybir.dt.int32
AF = mybir.ActivationFunctionType
ALU = mybir.AluOpType
AX = mybir.AxisListType

NT = 16
L1_SKIP_GLOBAL = False
L1_SKIP_LOCAL = False
L1_SKIP_NORM = False
L1_SKIP_GAV = False
L1_SKIP_LAV = False
L1_SKIP_ROW = False
L1_SKIP_TRANS = False
ND = 8
T = 2048
D = 1024
LN_EPS = 1e-5


def emit_consts(nc, ident, ident_f32=None):
    with nc.Block() as blk:
        @blk.gpsimd
        def _(g):
            g.memset(ident[:, :], 1.0)
            g.drain()
            g.affine_select(ident[:, :], ident[:, :], [[1, 128]], ALU.is_equal, 0.0,
                            channel_multiplier=-1)
            if ident_f32 is not None:
                g.memset(ident_f32[:, :], 1.0)
                g.drain()
                g.affine_select(ident_f32[:, :], ident_f32[:, :], [[1, 128]],
                                ALU.is_equal, 0.0, channel_multiplier=-1)


def emit_ln_stats(nc, p, xs, ws, shared=None):
    sums, sumsq, scale, bias = ws["sums"], ws["sumsq"], ws["scale"], ws["bias"]
    sq_scr = ws["sq_scr"]

    with nc.Block() as blk:
        @blk.scalar
        def _(s):
            for ti in range(NT):
                if ti:
                    s.drain()
                s.activation(sq_scr[:, :], xs[:, ti, :], AF.Square,
                             accum_out=sumsq[:, ti:ti + 1])

        @blk.vector
        def _(v):
            for ti in range(NT):
                v.reduce_sum(sums[:, ti:ti + 1], xs[:, ti, :], AX.X)

    import contextlib as _ctl
    with _ctl.ExitStack() as _e:
        if shared is None:
            c1 = _e.enter_context(nc.semaphore(p + "c1"))
            c2 = _e.enter_context(nc.semaphore(p + "c2"))
            Bc = 0
        else:
            (c1, c2) = shared[0]
            Bc = shared[1]
        with nc.Block() as blk:
            @blk.vector
            def _(v):
                iv = 1.0 / D
                v.tensor_scalar(sums[:, :], sums[:, :], iv, None, ALU.mult)
                v.tensor_scalar(sumsq[:, :], sumsq[:, :], iv, None, ALU.mult)
                v.drain()
                v.tensor_tensor(bias[:, :], sums[:, :], sums[:, :], ALU.mult)
                v.drain()
                v.scalar_tensor_tensor(scale[:, :], sumsq[:, :], LN_EPS, bias[:, :],
                                       ALU.add, ALU.subtract).then_inc(c1, 1)
                v.wait_ge(c2, Bc + 1)
                v.scalar_tensor_tensor(bias[:, :], sums[:, :], -1.0, scale[:, :],
                                       ALU.mult, ALU.mult)

            @blk.scalar
            def _(s):
                s.wait_ge(c1, Bc + 1)
                s.activation(sumsq[:, :], scale[:, :], AF.Ln)
                s.drain()
                s.activation(scale[:, :], sumsq[:, :], AF.Exp,
                             scale=-0.5).then_inc(c2, 1)


def emit_ln_to_lnT(nc, p, xs, ident, lnT, ws, shared=None):
    if shared is None:
        emit_ln_stats(nc, p + "s", xs, ws)
    else:
        emit_ln_stats(nc, p + "s", xs, ws, shared=(shared[0][0:2], shared[1]))
    scale, bias, ln_scr = ws["scale"], ws["bias"], ws["ln_scr"]
    import contextlib as _ctl
    with _ctl.ExitStack() as _e:
        if shared is None:
            ap_s = _e.enter_context(nc.semaphore(p + "ap"))
            tp_s = _e.enter_context(nc.semaphore(p + "tp"))
            ev_s = _e.enter_context(nc.semaphore(p + "ev"))
            Ba = Bt = Be = 0
        else:
            ap_s, tp_s, ev_s = shared[0][2:5]
            k = shared[1]
            Ba, Bt, Be = NT * k, 2 * NT * k, 2 * NT * k
        with (nc.psum_tensor(p + "tps", [128, 2, 4, 128], F32) as tps,
              nc.Block() as blk):
            @blk.vector
            def _(v):
                for ti in range(NT):
                    if ti >= 2:
                        v.wait_ge(tp_s, Bt + 2 * (ti - 1))
                    v.tensor_scalar(ln_scr[:, ti % 2, :], xs[:, ti, :],
                                    scale[:, ti:ti + 1], bias[:, ti:ti + 1],
                                    ALU.mult, ALU.add).then_inc(ap_s, 1)
                    if ti >= 1:
                        tj = ti - 1
                        v.wait_ge(tp_s, Bt + 2 * ti)
                        for h in range(2):
                            v.tensor_copy(
                                lnT[:, 4 * h:4 * h + 4, tj * 128:(tj + 1) * 128],
                                tps[:, h, :, :]).then_inc(ev_s, 1)
                tj = NT - 1
                v.wait_ge(tp_s, Bt + 2 * NT)
                for h in range(2):
                    v.tensor_copy(lnT[:, 4 * h:4 * h + 4, tj * 128:(tj + 1) * 128],
                                  tps[:, h, :, :]).then_inc(ev_s, 1)

            @blk.tensor
            def _(t):
                for ti in range(NT):
                    t.wait_ge(ap_s, Ba + ti + 1)
                    if ti >= 1:
                        t.wait_ge(ev_s, Be + 2 * ti)
                    for h in range(2):
                        for j in range(4):
                            dj = 4 * h + j
                            mm = t.matmul(tps[:, h, j, :],
                                          ln_scr[:, ti % 2, dj * 128:(dj + 1) * 128],
                                          ident[:, :], start=True, stop=True)
                        mm.then_inc(tp_s, 1)


def emit_l0ffn(nc, p, xs, ident, wg_d, wu_d, wd_d, ws, lnsh=None):
    lnT = ws["lnT"]
    emit_ln_to_lnT(nc, p + "ln", xs, ident, lnT, ws, shared=lnsh)

    wd_sb = ws["wd_sb"]
    wg_sb = ws["wg_sb"]
    wu_sb = ws["wu_sb"]
    hT = ws["hT"]
    silu_scr = ws["silu_scr"]

    NC_ = 24
    inv127 = float(1.0 / 127.0)
    inv127sq = float(1.0 / (127.0 * 127.0))

    with (nc.semaphore(p + "wd") as wd_s, nc.Block() as blk):
        @blk.gpsimd
        def _(g):
            for c in range(NC_):
                g.dma_start(wd_sb[:, c, :],
                            wd_d[c, :, :]).then_inc(wd_s, 16)
            g.wait_ge(wd_s, 16 * NC_)

    import contextlib as _ctl
    _es0 = _ctl.ExitStack()
    lda_s = _es0.enter_context(nc.semaphore(p + "lda"))
    ldb_s = _es0.enter_context(nc.semaphore(p + "ldb"))
    mm_s = _es0.enter_context(nc.semaphore(p + "mm"))
    si_s = _es0.enter_context(nc.semaphore(p + "si"))
    mu_s = _es0.enter_context(nc.semaphore(p + "mu"))
    dn_s = _es0.enter_context(nc.semaphore(p + "dn"))
    ev_s = _es0.enter_context(nc.semaphore(p + "ev"))
    for half in range(2):
        t0 = half * 1024
        B_ld, B_mm, B_si, B_mu = 384 * half, 48 * half, 24 * half, 24 * half
        B_dn, B_ev = 8 * half, 8 * half
        ld_par = [lda_s, ldb_s]
        with (nc.psum_tensor(p + f"gps{half}", [128, 2, 1024], F32) as gps,
              nc.psum_tensor(p + f"ups{half}", [128, 2, 1024], F32) as ups,
              nc.Block() as blk):
            @blk.gpsimd
            def _(g):
                for c in range(NC_):
                    if c >= 2:
                        g.wait_ge(mm_s, B_mm + 2 * (c - 1))
                    g.dma_start(wg_sb[:, c % 2, :, :],
                                wg_d[c, :, :].rearrange("(dc pp) ci -> pp dc ci", pp=128)
                                ).then_inc(ld_par[c % 2], 16)
                    g.dma_start(wu_sb[:, c % 2, :, :],
                                wu_d[c, :, :].rearrange("(dc pp) ci -> pp dc ci", pp=128)
                                ).then_inc(ld_par[c % 2], 16)

            @blk.tensor
            def _(t):
                for c in range(NC_):
                    t.wait_ge(ld_par[c % 2], B_ld + 32 * (c // 2 + 1))
                    if c >= 2:
                        t.wait_ge(si_s, B_si + c - 1)
                        t.wait_ge(mu_s, B_mu + c - 1)
                    for j in range(ND):
                        for n2 in range(2):
                            mm = t.matmul(gps[:, c % 2, n2 * 512:(n2 + 1) * 512],
                                          wg_sb[:, c % 2, j, :],
                                          lnT[:, j, t0 + n2 * 512:t0 + (n2 + 1) * 512],
                                          start=(j == 0), stop=(j == ND - 1))
                    mm.then_inc(mm_s, 1)
                    for j in range(ND):
                        for n2 in range(2):
                            mm = t.matmul(ups[:, c % 2, n2 * 512:(n2 + 1) * 512],
                                          wu_sb[:, c % 2, j, :],
                                          lnT[:, j, t0 + n2 * 512:t0 + (n2 + 1) * 512],
                                          start=(j == 0), stop=(j == ND - 1))
                    mm.then_inc(mm_s, 1)

            @blk.scalar
            def _(s):
                for c in range(NC_):
                    s.wait_ge(mm_s, B_mm + 2 * c + 1)
                    if c >= 1:
                        s.wait_ge(mu_s, B_mu + c)
                    s.activation(silu_scr[:, :], gps[:, c % 2, :], AF.Silu,
                                 scale=inv127).then_inc(si_s, 1)

            @blk.vector
            def _(v):
                for c in range(NC_):
                    v.wait_ge(mm_s, B_mm + 2 * c + 2)
                    v.wait_ge(si_s, B_si + c + 1)
                    v.tensor_tensor(hT[:, c, :], silu_scr[:, :],
                                    ups[:, c % 2, :], ALU.mult).then_inc(mu_s, 1)

        with (nc.psum_tensor(p + f"dps{half}", [128, 2, 1024], F32) as dps,
              nc.Block() as blk):
            @blk.tensor
            def _(t):
                for tl in range(8):
                    ti = 8 * half + tl
                    if tl >= 2:
                        t.wait_ge(ev_s, B_ev + tl - 1)
                    for c in range(NC_):
                        for n2 in range(2):
                            mm = t.matmul(dps[:, tl % 2, n2 * 512:(n2 + 1) * 512],
                                          hT[:, c, tl * 128:(tl + 1) * 128],
                                          wd_sb[:, c, n2 * 512:(n2 + 1) * 512],
                                          start=(c == 0), stop=(c == NC_ - 1))
                    mm.then_inc(dn_s, 1)

            @blk.vector
            def _(v):
                for tl in range(8):
                    ti = 8 * half + tl
                    v.wait_ge(dn_s, B_dn + tl + 1)
                    v.scalar_tensor_tensor(xs[:, ti, :], dps[:, tl % 2, :],
                                           inv127sq, xs[:, ti, :],
                                           ALU.mult, ALU.add).then_inc(ev_s, 1)
    _es0.close()


def build_l0ffn_standalone():
    nc = bass.Bass("TRN2", target_bir_lowering=False)
    x_in = nc.dram_tensor("x", [T, D], F32, kind="ExternalInput").ap()
    wg_d = nc.dram_tensor("wg", [24, 1024, 128], I32, kind="ExternalInput").ap()
    wu_d = nc.dram_tensor("wu", [24, 1024, 128], I32, kind="ExternalInput").ap()
    wd_d = nc.dram_tensor("wd", [24, 128, 1024], I32, kind="ExternalInput").ap()
    x_out = nc.dram_tensor("xo", [T, D], F32, kind="ExternalOutput").ap()

    with (
        nc.sbuf_tensor("xs", [128, NT, D], F32) as xs,
        nc.sbuf_tensor("ident", [128, 128], BF16) as ident,
        nc.sbuf_tensor("lnT", [128, ND, T], BF16) as lnT,
        nc.sbuf_tensor("wd_sb", [128, 24, 1024], BF16) as wd_sb,
        nc.sbuf_tensor("wg_sb", [128, 2, 8, 128], BF16) as wg_sb,
        nc.sbuf_tensor("wu_sb", [128, 2, 8, 128], BF16) as wu_sb,
        nc.sbuf_tensor("hT", [128, 24, 1024], BF16) as hT,
        nc.sbuf_tensor("silu_scr", [128, 1024], BF16) as silu_scr,
        nc.sbuf_tensor("sums", [128, NT], F32) as sums,
        nc.sbuf_tensor("sumsq", [128, NT], F32) as sumsq,
        nc.sbuf_tensor("scale", [128, NT], F32) as scale,
        nc.sbuf_tensor("bias", [128, NT], F32) as bias,
    ):
        sq_scr = silu_scr
        ln_scr = hT[:, 0:2, :].rearrange("p a b -> p a b") if False else None
        ws = dict(lnT=lnT, wd_sb=wd_sb, wg_sb=wg_sb, wu_sb=wu_sb, hT=hT,
                  silu_scr=silu_scr, sums=sums, sumsq=sumsq,
                  scale=scale, bias=bias, sq_scr=sq_scr, ln_scr=hT)
        with (nc.semaphore("xin") as xin_s, nc.Block() as blk):
            @blk.sync
            def _(s):
                s.dma_start(xs[:, :, :],
                            x_in.rearrange("(n pp) m -> pp n m", pp=128)
                            ).then_inc(xin_s, 16)
                s.wait_ge(xin_s, 16)
        emit_consts(nc, ident)
        emit_l0ffn(nc, "f0", xs, ident, wg_d, wu_d, wd_d, ws)
        with (nc.semaphore("xout") as xo_s, nc.Block() as blk):
            @blk.sync
            def _(s):
                s.dma_start(x_out.rearrange("(n pp) m -> pp n m", pp=128),
                            xs[:, :, :]).then_inc(xo_s, 16)
                s.wait_ge(xo_s, 16)
    return nc


def host_l0ffn_weights(w_gate, w_up, w_down):
    wg = np.ascontiguousarray(
        w_gate.T.reshape(1024, 24, 128).transpose(1, 0, 2))
    wu = np.ascontiguousarray(
        w_up.T.reshape(1024, 24, 128).transpose(1, 0, 2))
    wd = np.ascontiguousarray(
        w_down.T.reshape(24, 128, 1024))
    return wg, wu, wd


def emit_tern(nc, p, w_d, ntiles, W, tern, wtile, scr_f32, acc, smalls,
              onesf, scale_bc, wmean_bc, shared=None):
    count = float(ntiles * 128 * W)
    import contextlib as _ctl
    _et = _ctl.ExitStack()
    if shared is None:
        l1a = _et.enter_context(nc.semaphore(p + "l1a"))
        l1b = _et.enter_context(nc.semaphore(p + "l1b"))
        r1 = _et.enter_context(nc.semaphore(p + "r1"))
        t1 = _et.enter_context(nc.semaphore(p + "t1"))
        t2 = _et.enter_context(nc.semaphore(p + "t2"))
        l3a = _et.enter_context(nc.semaphore(p + "l3a"))
        l3b = _et.enter_context(nc.semaphore(p + "l3b"))
        r3 = _et.enter_context(nc.semaphore(p + "r3"))
        Bl1a = Bl1b = Br1 = Bt1 = Bt2 = Bl3a = Bl3b = Br3 = 0
    else:
        (l1a, l1b, r1, t1, t2, l3a, l3b, r3) = shared[0]
        (Bl1a, Bl1b, Br1, Bt1, Bt2, Bl3a, Bl3b, Br3) = shared[1]
    with nc.Block() as blk:
        ls1 = [(l1a, Bl1a), (l1b, Bl1b)]
        @blk.gpsimd
        def _(g):
            for i in range(ntiles):
                if i >= 2:
                    g.wait_ge(r1, Br1 + i - 1)
                g.dma_start(wtile[:, i % 2, 0:W],
                            w_d[i, :, :]).then_inc(ls1[i % 2][0], 16)

        @blk.vector
        def _(v):
            for i in range(ntiles):
                v.wait_ge(ls1[i % 2][0], ls1[i % 2][1] + 16 * (i // 2 + 1))
                v.tensor_reduce(acc[:, i:i + 1], wtile[:, i % 2, 0:W], AX.X,
                                ALU.add, apply_absolute_value=True).then_inc(r1, 1)
    with (nc.psum_tensor(p + "tp1", [128, 8], F32) as tp1,
          nc.Block() as blk):
        @blk.vector
        def _(v):
            v.tensor_reduce(smalls[:, 0:1], acc[:, 0:ntiles], AX.X, ALU.add)
            v.drain()
            v.nop().then_inc(t1, 1)
            v.wait_ge(t2, Bt2 + 1)
            v.tensor_copy(smalls[0:1, 1:2], tp1[0:1, 1:2])
            v.drain()
            v.tensor_scalar(smalls[0:1, 2:3], smalls[0:1, 1:2], 1.0 / count, None,
                            ALU.mult)
            v.drain()
            v.tensor_scalar(smalls[0:1, 3:4], smalls[0:1, 2:3], 1e-5, None, ALU.max)
            v.drain()
            v.reciprocal(smalls[0:1, 4:5], smalls[0:1, 3:4])
            v.drain()
            v.nop().then_inc(t1, 2)
            v.wait_ge(t2, Bt2 + 2)
            v.tensor_copy(wmean_bc[:, 0:1], tp1[:, 2:3])
            v.tensor_copy(scale_bc[:, 0:1], tp1[:, 3:4])

        @blk.tensor
        def _(t):
            t.wait_ge(t1, Bt1 + 1)
            t.matmul(tp1[0:1, 1:2], smalls[:, 0:1], onesf[:, 0:1],
                     start=True, stop=True).then_inc(t2, 1)
            t.wait_ge(t1, Bt1 + 3)
            t.matmul(tp1[:, 2:3], onesf[0:1, :], smalls[0:1, 3:4],
                     start=True, stop=True)
            mm = t.matmul(tp1[:, 3:4], onesf[0:1, :], smalls[0:1, 4:5],
                          start=True, stop=True)
            mm.then_inc(t2, 1)
    with nc.Block() as blk:
        ls3 = [(l3a, Bl3a), (l3b, Bl3b)]
        @blk.gpsimd
        def _(g):
            for i in range(ntiles):
                if i >= 2:
                    g.wait_ge(r3, Br3 + i - 1)
                g.dma_start(wtile[:, i % 2, 0:W],
                            w_d[i, :, :]).then_inc(ls3[i % 2][0], 16)

        @blk.vector
        def _(v):
            scr_i32 = scr_f32.bitcast(I32)
            for i in range(ntiles):
                v.wait_ge(ls3[i % 2][0], ls3[i % 2][1] + 16 * (i // 2 + 1))
                v.tensor_scalar(scr_f32[:, 0:W], wtile[:, i % 2, 0:W],
                                scale_bc[:, 0:1], None, ALU.mult)
                v.drain()
                v.tensor_copy(scr_i32[:, 0:W], scr_f32[:, 0:W])
                v.drain()
                v.tensor_copy(wtile[:, i % 2, 0:W], scr_i32[:, 0:W])
                v.drain()
                v.tensor_scalar(tern[:, i, 0:W], wtile[:, i % 2, 0:W],
                                1.0, -1.0, ALU.min, ALU.max).then_inc(r3, 1)


def emit_q1_to_q1T(nc, p, xs, ident, q1T, ws, shared=None):
    if shared is None:
        emit_ln_stats(nc, p + "s", xs, ws)
    else:
        emit_ln_stats(nc, p + "s", xs, ws, shared=(shared[0][0:2], shared[1]))
    scale, bias = ws["scale"], ws["bias"]
    lnf, lq_i32, lq_bf, q1c, smalls = (ws["lnf"], ws["lq_i32"], ws["lq_bf"],
                                       ws["q1c"], ws["smalls"])
    lq_f32 = ws["lq_f32"]
    import contextlib as _ctl
    _eq = _ctl.ExitStack()
    if shared is None:
        ap_s = _eq.enter_context(nc.semaphore(p + "ap"))
        tp_s = _eq.enter_context(nc.semaphore(p + "tp"))
        ev_s = _eq.enter_context(nc.semaphore(p + "ev"))
        Bqa = Bqt = Bqe = 0
    else:
        ap_s, tp_s, ev_s = shared[0][2:5]
        _k = shared[1]
        Bqa, Bqt, Bqe = NT * _k, 2 * NT * _k, 2 * NT * _k
    qa_s = _eq.enter_context(nc.semaphore(p + "qa"))
    qb_s = _eq.enter_context(nc.semaphore(p + "qb"))
    with (_eq, nc.psum_tensor(p + "tps", [128, 2, 4, 128], F32) as tps,
          nc.Block() as blk):
        @blk.vector
        def _(v):
            for ti in range(NT):
                if ti >= 2:
                    v.wait_ge(qb_s, ti - 1)
                v.tensor_scalar(lnf[:, ti % 2, :], xs[:, ti, :],
                                scale[:, ti:ti + 1], bias[:, ti:ti + 1],
                                ALU.mult, ALU.add)
                v.drain()
                v.tensor_reduce(smalls[:, 0:1], lnf[:, ti % 2, :], AX.X,
                                ALU.max, apply_absolute_value=True)
                v.drain()
                v.tensor_scalar(smalls[:, 1:2], smalls[:, 0:1], 1e-5, None, ALU.max)
                v.drain()
                v.tensor_scalar(smalls[:, 2:3], smalls[:, 1:2], 1.0 / 127.0,
                                None, ALU.mult)
                v.reciprocal(smalls[:, 3:4], smalls[:, 1:2])
                v.drain()
                v.tensor_scalar(smalls[:, 4:5], smalls[:, 3:4], 127.0, None,
                                ALU.mult).then_inc(qa_s, 1)
                v.wait_ge(qb_s, ti + 1)
                v.tensor_copy(lq_bf[:, :], lq_i32[:, :])
                if ti >= 1:
                    tj = ti - 1
                    v.wait_ge(tp_s, Bqt + 2 * ti)
                    for h in range(2):
                        v.tensor_copy(q1T[:, 4 * h:4 * h + 4, tj * 128:(tj + 1) * 128],
                                      tps[:, h, :, :]).then_inc(ev_s, 1)
                v.drain()
                v.tensor_scalar(q1c[:, :], lq_bf[:, :], smalls[:, 2:3],
                                None, ALU.mult).then_inc(ap_s, 1)
            tj = NT - 1
            v.wait_ge(tp_s, Bqt + 2 * NT)
            for h in range(2):
                v.tensor_copy(q1T[:, 4 * h:4 * h + 4, tj * 128:(tj + 1) * 128],
                              tps[:, h, :, :]).then_inc(ev_s, 1)

        @blk.scalar
        def _(s):
            for ti in range(NT):
                s.wait_ge(qa_s, ti + 1)
                s.activation(lq_i32[:, :], lnf[:, ti % 2, :], AF.Identity,
                             scale=smalls[:, 4:5]).then_inc(qb_s, 1)

        @blk.tensor
        def _(t):
            for ti in range(NT):
                t.wait_ge(ap_s, Bqa + ti + 1)
                if ti >= 1:
                    t.wait_ge(ev_s, Bqe + 2 * ti)
                for h in range(2):
                    for j in range(4):
                        dj = 4 * h + j
                        mm = t.matmul(tps[:, h, j, :],
                                      q1c[:, dj * 128:(dj + 1) * 128],
                                      ident[:, :], start=True, stop=True)
                    mm.then_inc(tp_s, 1)


def emit_l1ffn(nc, p, xs, ident, identf, onesf, w1_d, w2_d, convw_d, convb_d,
               hs_d, ws, lnsh=None):
    q1T = ws["q1T"]
    tern = ws["tern"]
    t1v = tern.rearrange("pp (a b) -> pp a b", a=8)
    t2v = tern.rearrange("pp (a b) -> pp a b", a=32)
    big16 = ws["big16"]
    wtile = big16.rearrange("pp (a b) -> pp a b", a=2)
    hs_sb = big16[:, 0:8192].rearrange("pp (a b) -> pp a b", a=32)
    scr = ws["scr16k"]
    acc, smalls = ws["acc"], ws["smalls"]
    M = ws["M"]
    s2, inv_s2, comb = ws["s2"], ws["inv_s2"], ws["comb"]
    cw_sb, cb_sb = ws["cw_sb"], ws["cb_sb"]
    s0c, smc, spc = ws["s0c"], ws["smc"], ws["spc"]
    sc1, wm1, sc2, wm2 = ws["sc1"], ws["wm1"], ws["sc2"], ws["wm2"]
    S2bc, I2bc = ws["S2bc"], ws["I2bc"]
    r1x = ws["r1x"]

    NC_ = 32

    import contextlib as _ctl
    _etn = _ctl.ExitStack()
    tsems = tuple(_etn.enter_context(nc.semaphore(p + "tn" + s))
                  for s in ("la", "lb", "r1", "t1", "t2", "lc", "ld", "r3"))
    tbase1 = (0, 0, 0, 0, 0, 0, 0, 0)
    tbase2 = (64, 64, 8, 3, 2, 64, 64, 8)

    emit_q1_to_q1T(nc, p + "q", xs, ident, q1T, ws, shared=lnsh)

    with (nc.semaphore(p + "cw") as cw_s, nc.Block() as blk):
        @blk.sync
        def _(s):
            s.dma_start(cw_sb[:, :, :], convw_d).then_inc(cw_s, 16)
            s.dma_start(cb_sb[:, :], convb_d).then_inc(cw_s, 16)
            s.wait_ge(cw_s, 32)
    emit_tern(nc, p + "t1", w1_d, 8, 4096, t1v, wtile,
              scr, acc, smalls, onesf, sc1, wm1, shared=(tsems, tbase1))
    with nc.Block() as blk:
        @blk.vector
        def _(v):
            for k, dst in ((0, smc), (1, s0c), (2, spc)):
                v.tensor_scalar(dst[:, :], cw_sb[:, :, k], wm1[:, 0:1],
                                None, ALU.mult)
        @blk.gpsimd
        def _(g):
            g.memset(M[:, :], -1e30)
            g.memset(ws["Mn"][:, :], 1e30)

    a2 = scr.rearrange("pp (a b) -> pp a b", a=2)
    hsp = ws["hsp"]
    Mn = ws["Mn"]
    with (nc.semaphore(p + "cm") as cm_s, nc.semaphore(p + "c1") as c1_s,
          nc.semaphore(p + "c3") as c3_s, nc.semaphore(p + "cs") as cs_s,
          nc.semaphore(p + "cg") as cg_s, nc.semaphore(p + "ch") as ch_s,
          nc.semaphore(p + "c5") as cm2_s,
          nc.psum_tensor(p + "hp", [128, 2, 2048], F32) as hp,
          nc.Block() as blk):
        @blk.tensor
        def _(t):
            for c in range(NC_):
                if c >= 2:
                    t.wait_ge(c3_s, c - 1)
                    t.wait_ge(cs_s, c - 1)
                for j in range(ND):
                    for n4 in range(4):
                        mm = t.matmul(hp[:, c % 2, n4 * 512:(n4 + 1) * 512],
                                      t1v[:, j, c * 128:(c + 1) * 128],
                                      q1T[:, j, n4 * 512:(n4 + 1) * 512],
                                      start=(j == 0), stop=(j == ND - 1))
                mm.then_inc(cm_s, 1)

        @blk.scalar
        def _(s):
            for c in range(NC_):
                s.wait_ge(cm_s, c + 1)
                if c >= 2:
                    s.wait_ge(c3_s, c - 1)
                s.activation(a2[:, c % 2, :], hp[:, c % 2, :], AF.Identity,
                             scale=s0c[:, c:c + 1],
                             bias=cb_sb[:, c:c + 1]).then_inc(c1_s, 1)
                if c >= 1:
                    s.wait_ge(c3_s, c)
                    s.wait_ge(cg_s, max(0, c - 1))
                    s.activation(hsp[:, :], a2[:, (c - 1) % 2, :],
                                 AF.Silu).then_inc(cs_s, 1)
            s.wait_ge(c3_s, NC_)
            s.wait_ge(cg_s, NC_ - 1)
            s.activation(hsp[:, :], a2[:, (NC_ - 1) % 2, :],
                         AF.Silu).then_inc(cs_s, 1)

        @blk.vector
        def _(v):
            for c in range(NC_):
                v.wait_ge(c1_s, c + 1)
                v.scalar_tensor_tensor(a2[:, c % 2, 1:2048], hp[:, c % 2, 0:2047],
                                       smc[:, c:c + 1], a2[:, c % 2, 1:2048],
                                       ALU.mult, ALU.add)
                v.drain()
                v.scalar_tensor_tensor(a2[:, c % 2, 0:2047], hp[:, c % 2, 1:2048],
                                       spc[:, c:c + 1], a2[:, c % 2, 0:2047],
                                       ALU.mult, ALU.add).then_inc(c3_s, 1)
                v.wait_ge(cs_s, c + 1)
                v.tensor_tensor(M[:, :], M[:, :], hsp[:, :], ALU.max)
                v.drain()
                v.tensor_tensor(Mn[:, :], Mn[:, :], hsp[:, :], ALU.min
                                ).then_inc(cm2_s, 1)
                v.drain()

        @blk.gpsimd
        def _(g):
            for c in range(NC_):
                g.wait_ge(cs_s, c + 1)
                g.dma_start(hs_d[c, :, :], hsp[:, :]).then_inc(ch_s, 16)
                g.wait_ge(ch_s, 16 * (c + 1))
                g.wait_ge(cm2_s, c + 1)
                g.nop().then_inc(cg_s, 1)

    with nc.Block() as blk:
        @blk.vector
        def _(v):
            v.tensor_scalar(Mn[:, :], Mn[:, :], -1.0, None, ALU.mult)
            v.drain()
            v.tensor_tensor(M[:, :], M[:, :], Mn[:, :], ALU.max)
    with (nc.semaphore(p + "dm") as dm_s, nc.semaphore(p + "dv") as dv_s,
          nc.psum_tensor(p + "mt", [128, 2, 512], F32) as mt,
          nc.Block() as blk):
        @blk.tensor
        def _(t):
            for sb in range(4):
                if sb >= 2:
                    t.wait_ge(dv_s, sb - 1)
                for t4 in range(4):
                    tc = 4 * sb + t4
                    mm = t.matmul(mt[:, sb % 2, 128 * t4:128 * (t4 + 1)],
                                  M[:, tc * 128:(tc + 1) * 128],
                                  ident[:, :], start=True, stop=True)
                mm.then_inc(dm_s, 1)

        @blk.vector
        def _(v):
            for sb in range(4):
                cb = 4 * (sb % 2)
                v.wait_ge(dm_s, sb + 1)
                for t4 in range(4):
                    v.tensor_reduce(smalls[:, cb + t4:cb + t4 + 1],
                                    mt[:, sb % 2, 128 * t4:128 * (t4 + 1)],
                                    AX.X, ALU.max)
                v.drain()
                for t4 in range(4):
                    tc = 4 * sb + t4
                    v.tensor_scalar(s2[:, tc:tc + 1],
                                    smalls[:, cb + t4:cb + t4 + 1],
                                    1e-5, None, ALU.max)
                v.nop().then_inc(dv_s, 1)
            v.drain()
            v.tensor_scalar(inv_s2[:, :], s2[:, :], 1.0 / 127.0, None, ALU.mult)
            v.drain()
            v.reciprocal(s2[:, :], s2[:, :])
            v.drain()
            v.tensor_scalar(s2[:, :], s2[:, :], 127.0, None, ALU.mult)

    emit_tern(nc, p + "t2", w2_d, 32, 1024, t2v, wtile,
              scr, acc, smalls, onesf, sc2, wm2, shared=(tsems, tbase2))
    with nc.Block() as blk:
        @blk.vector
        def _(v):
            v.tensor_scalar(comb[:, :], inv_s2[:, :], wm2[:, 0:1], None, ALU.mult)

    i32p = scr.bitcast(I32).rearrange("pp (a b) -> pp a b", a=2)
    import contextlib as _ctl
    _esf = _ctl.ExitStack()
    fl_s = _esf.enter_context(nc.semaphore(p + "fl"))
    fa_s = _esf.enter_context(nc.semaphore(p + "fa"))
    fb_s = _esf.enter_context(nc.semaphore(p + "fb"))
    fc_s = _esf.enter_context(nc.semaphore(p + "fc"))
    fm_s = _esf.enter_context(nc.semaphore(p + "fm"))
    fr_s = _esf.enter_context(nc.semaphore(p + "fr"))
    for e in range(8):
        Bl, Ba, Bb, Bc, Bm, Br = 128 * e, 2 * e, e, 32 * e, 2 * e, 2 * e
        with (nc.psum_tensor(p + f"f{e}p", [128, 2, 1024], F32) as dps,
              nc.psum_tensor(p + f"f{e}s", [128, 4, 256], F32) as sps,
              nc.Block() as blk):
            @blk.sync
            def _(s):
                for cg in range(8):
                    s.dma_start(
                        hs_sb[:, cg * 4:(cg + 1) * 4, :],
                        hs_d[cg * 4:(cg + 1) * 4, :, e * 256:(e + 1) * 256]
                        .rearrange("c pp t -> pp c t")).then_inc(fl_s, 16)

            @blk.tensor
            def _(t):
                for k in range(2):
                    t.matmul(sps[0:1, 0, k * 128:(k + 1) * 128],
                             s2[:, 2 * e + k:2 * e + k + 1], identf[:, :],
                             start=True, stop=True)
                for k in range(2):
                    mm = t.matmul(sps[0:1, 1, k * 128:(k + 1) * 128],
                                  inv_s2[:, 2 * e + k:2 * e + k + 1], identf[:, :],
                                  start=True, stop=True)
                mm.then_inc(fa_s, 1)
                t.wait_ge(fb_s, Bb + 1)
                t.matmul(sps[:, 2, :], onesf[0:1, :], r1x[0:1, 0:256],
                         start=True, stop=True).then_inc(fa_s, 1)
                for tc in range(2):
                    t.wait_ge(fc_s, Bc + NC_)
                    if tc >= 1:
                        t.wait_ge(fr_s, Br + tc)
                    for c in range(NC_):
                        for n2 in range(2):
                            mm = t.matmul(dps[:, tc % 2, n2 * 512:(n2 + 1) * 512],
                                          hs_sb[:, c, tc * 128:(tc + 1) * 128],
                                          t2v[:, c, n2 * 512:(n2 + 1) * 512],
                                          start=(c == 0), stop=(c == NC_ - 1))
                    mm.then_inc(fm_s, 1)

            @blk.vector
            def _(v):
                v.wait_ge(fa_s, Ba + 1)
                v.tensor_copy(r1x[0:1, 0:256], sps[0:1, 0, 0:256])
                v.drain()
                v.nop().then_inc(fb_s, 1)
                v.wait_ge(fa_s, Ba + 2)
                v.tensor_copy(S2bc[:, :], sps[:, 2, :])
                v.drain()
                v.wait_ge(fl_s, Bl + 16 * 8)
                for c in range(NC_):
                    v.tensor_tensor(i32p[:, c % 2, 0:256], hs_sb[:, c, :],
                                    S2bc[:, :], ALU.mult)
                    v.drain()
                    v.tensor_copy(hs_sb[:, c, :], i32p[:, c % 2, 0:256]
                                  ).then_inc(fc_s, 1)
                for tc in range(2):
                    ti = 2 * e + tc
                    v.wait_ge(fm_s, Bm + tc + 1)
                    v.scalar_tensor_tensor(xs[:, ti, :], dps[:, tc % 2, :],
                                           comb[:, ti:ti + 1], xs[:, ti, :],
                                           ALU.mult, ALU.add).then_inc(fr_s, 1)


def build_l1ffn_standalone():
    nc = bass.Bass("TRN2", target_bir_lowering=False)
    x_in = nc.dram_tensor("x", [T, D], F32, kind="ExternalInput").ap()
    w1_d = nc.dram_tensor("w1", [8, 128, 4096], F32, kind="ExternalInput").ap()
    w2_d = nc.dram_tensor("w2", [32, 128, 1024], F32, kind="ExternalInput").ap()
    convw_d = nc.dram_tensor("cw", [128, 32, 3], F32, kind="ExternalInput").ap()
    convb_d = nc.dram_tensor("cb", [128, 32], F32, kind="ExternalInput").ap()
    hs_d = nc.dram_tensor("hs_scr", [32, 128, 2048], BF16).ap()
    x_out = nc.dram_tensor("xo", [T, D], F32, kind="ExternalOutput").ap()

    import contextlib
    with contextlib.ExitStack() as _es:
        def _al(name, shape, dt):
            return _es.enter_context(nc.sbuf_tensor(name, shape, dt))
        xs = _al("xs", [128, NT, D], F32)
        ident = _al("ident", [128, 128], BF16)
        identf = _al("identf", [128, 128], F32)
        onesf = _al("onesf", [128, 128], F32)
        q1T = _al("q1T", [128, ND, T], BF16)
        tern = _al("tern", [128, 32768], BF16)
        big16 = _al("big16", [128, 8192], BF16)
        scr16k = _al("scr16k", [128, 4096], F32)
        hsp = _al("hsp", [128, 2048], BF16)
        M = _al("Mt", [128, 2048], BF16)
        Mn = _al("Mn", [128, 2048], BF16)
        sums = _al("sums", [128, NT], F32)
        sumsq = _al("sumsq", [128, NT], F32)
        scale = _al("scale", [128, NT], F32)
        bias = _al("bias", [128, NT], F32)
        s2 = _al("s2", [128, NT], F32)
        inv_s2 = _al("inv_s2", [128, NT], F32)
        comb = _al("comb", [128, NT], F32)
        acc = _al("acc", [128, 32], F32)
        smalls = _al("smalls", [128, 8], F32)
        cw_sb = _al("cw_sb", [128, 32, 3], F32)
        cb_sb = _al("cb_sb", [128, 32], F32)
        s0c = _al("s0c", [128, 32], F32)
        smc = _al("smc", [128, 32], F32)
        spc = _al("spc", [128, 32], F32)
        sc1 = _al("sc1", [128, 1], F32)
        wm1 = _al("wm1", [128, 1], F32)
        sc2 = _al("sc2", [128, 1], F32)
        wm2 = _al("wm2", [128, 1], F32)
        S2bc = scr16k[:, 3584:3840]
        r1x = _al("r1x", [1, 256], F32)
        lnf = scr16k[:, 0:2048].rearrange("pp (a b) -> pp a b", a=2)
        lq_f32 = scr16k[:, 2048:3072]
        lq_i32 = lq_f32.bitcast(I32)
        lq_bf = scr16k[:, 3072:3584].bitcast(BF16)
        q1c = scr16k[:, 3584:4096].bitcast(BF16)
        ws = dict(sums=sums, sumsq=sumsq, scale=scale, bias=bias, sq_scr=lq_bf,
                  lnf=lnf, lq_i32=lq_i32, lq_f32=lq_f32, lq_bf=lq_bf, q1c=q1c,
                  smalls=smalls,
                  q1T=q1T, tern=tern, big16=big16, scr16k=scr16k, hsp=hsp,
                  M=M, Mn=Mn, s2=s2, inv_s2=inv_s2, comb=comb, acc=acc,
                  cw_sb=cw_sb, cb_sb=cb_sb, s0c=s0c, smc=smc, spc=spc,
                  sc1=sc1, wm1=wm1, sc2=sc2, wm2=wm2, S2bc=S2bc, I2bc=None,
                  r1x=r1x)
        with (nc.semaphore("xin") as xin_s, nc.Block() as blk):
            @blk.sync
            def _(s):
                s.dma_start(xs[:, :, :],
                            x_in.rearrange("(n pp) m -> pp n m", pp=128)
                            ).then_inc(xin_s, 16)
                s.wait_ge(xin_s, 16)
            @blk.gpsimd
            def _(g):
                g.memset(onesf[:, :], 1.0)
        emit_consts(nc, ident, identf)
        emit_l1ffn(nc, "g1", xs, ident, identf, onesf, w1_d, w2_d,
                   convw_d, convb_d, hs_d, ws)
        with (nc.semaphore("xout") as xo_s, nc.Block() as blk):
            @blk.sync
            def _(s):
                s.dma_start(x_out.rearrange("(n pp) m -> pp n m", pp=128),
                            xs[:, :, :]).then_inc(xo_s, 16)
                s.wait_ge(xo_s, 16)
    return nc


def host_l1ffn_weights(w1, conv_w, conv_b, w2):
    w1p = np.ascontiguousarray(w1.T.reshape(8, 128, 4096))
    w2p = np.ascontiguousarray(w2.T.reshape(32, 128, 1024))
    cwp = np.ascontiguousarray(
        conv_w.reshape(4096, 3).reshape(32, 128, 3).transpose(1, 0, 2))
    cbp = np.ascontiguousarray(conv_b.reshape(32, 128).T)
    return w1p, w2p, cwp, cbp



def _l1_window_pieces(j):
    blocks = sorted({(j - 1) % 32, j, (j + 1) % 32})
    out = []
    for b in blocks:
        s, l = 64 * b, 64
        if out and out[-1][0] + out[-1][1] == s:
            out[-1] = (out[-1][0], out[-1][1] + l)
        else:
            out.append((s, l))
    return out


def emit_l1attn(nc, p, xs, ident, onesb, wq_d, wk_d, wv_d, wo_d,
                bq_d, bk_d, brow_d, ws, stop=None, lnsh=None):
    lnT = ws["lnT"]
    qT, kT = ws["qT"], ws["kT"]
    v_aug = ws["v_aug"]
    wbuf = ws["wbuf"]
    scr9 = ws["scr9"]
    vT_scr = scr9[:, 0:4096].rearrange("pp (a b) -> pp a b", a=2)
    pl = scr9[:, 0:1536].rearrange("pp (a b) -> pp a b", a=4)
    pg = scr9[:, 0:1024].rearrange("pp (a b) -> pp a b", a=2)
    pr = scr9[:, 0:128].rearrange("pp (a b) -> pp a b", a=2)
    y_tok = scr9[:, 1536:3584].rearrange("pp (a b) -> pp a b", a=16)
    ytmp = scr9[:, 3584:4096].rearrange("pp (a b c) -> pp a b c", a=4, b=2)
    wb = ws["wbuf"]
    wbuf = wb[:, 0:2048].rearrange("pp (a b) -> pp a b", a=2)
    ysb = wb[:, 0:1040].rearrange("pp (a b c d) -> pp a b c d", a=4, b=2, c=2)
    gvp = ws["gvp"]
    bq_sb, bk_sb = ws["bq_sb"], ws["bk_sb"]
    smalls = ws["smalls"]
    zsc = smalls[:, 8:24]
    yT = lnT

    emit_ln_to_lnT(nc, p + "ln", xs, ident, lnT, ws, shared=lnsh)

    with (nc.semaphore(p + "b") as b_s, nc.Block() as blk):
        @blk.sync
        def _(s):
            s.dma_start(bq_sb[:, :], bq_d).then_inc(b_s, 16)
            s.dma_start(bk_sb[:, :], bk_d).then_inc(b_s, 16)
            s.dma_start(ws["bv128"][:, :], ws["bv128_d"]).then_inc(b_s, 16)
            s.wait_ge(b_s, 48)
        @blk.gpsimd
        def _(g):
            vh = v_aug.rearrange("pp tc (h e) -> pp tc h e", e=65)
            g.memset(vh[:, :, :, 64:65], 1.0)
            g.drain()
            g.memset(wb[0:1, 2048:2176], 1.0)

    if stop == "ln":
        return
    for which, w_d, dst, b_sb in (("q", wq_d, qT, bq_sb), ("k", wk_d, kT, bk_sb)):
        with (nc.semaphore(p + which + "w") as w_s,
              nc.semaphore(p + which + "m") as m_s,
              nc.semaphore(p + which + "e") as e_s,
              nc.psum_tensor(p + which + "ps", [128, 2, 2048], F32) as pps,
              nc.Block() as blk):
            @blk.gpsimd
            def _(g):
                for ocp in range(4):
                    for wave in range(4):
                        gw = 4 * ocp + wave
                        if gw >= 1:
                            g.wait_ge(m_s, gw)
                        for dj in range(2):
                            d = 2 * wave + dj
                            g.dma_start(wbuf[:, dj, 0:256],
                                        w_d[d * 128:(d + 1) * 128,
                                            256 * ocp:256 * (ocp + 1)]
                                        ).then_inc(w_s, 16)

            @blk.tensor
            def _(t):
                for ocp in range(4):
                    if ocp >= 1:
                        t.wait_ge(e_s, 2 * ocp)
                    for wave in range(4):
                        gw = 4 * ocp + wave
                        t.wait_ge(w_s, 32 * (gw + 1))
                        for dj in range(2):
                            d = 2 * wave + dj
                            for oc2 in range(2):
                                for n4 in range(4):
                                    mm = t.matmul(
                                        pps[:, oc2, n4 * 512:(n4 + 1) * 512],
                                        wbuf[:, dj, oc2 * 128:(oc2 + 1) * 128],
                                        lnT[:, d, n4 * 512:(n4 + 1) * 512],
                                        start=(d == 0), stop=(d == 7))
                        mm.then_inc(m_s, 1)

            @blk.vector
            def _(v):
                for ocp in range(4):
                    v.wait_ge(m_s, 4 * ocp + 4)
                    for oc2 in range(2):
                        oc = 2 * ocp + oc2
                        v.tensor_scalar(dst[:, oc, :], pps[:, oc2, :],
                                        b_sb[:, oc:oc + 1], None,
                                        ALU.add).then_inc(e_s, 1)

    if stop == "qk":
        return
    import contextlib as _ctl
    _esv = _ctl.ExitStack()
    vw_s = _esv.enter_context(nc.semaphore(p + "vw"))
    vm_s = _esv.enter_context(nc.semaphore(p + "vm"))
    vt_s = _esv.enter_context(nc.semaphore(p + "vt"))
    vv_s = _esv.enter_context(nc.semaphore(p + "vv"))
    for ocp in range(4):
        Bw, Bm, Bt, Bv = 128 * ocp, 2 * ocp, 8 * ocp, 8 * ocp
        w_s, m_s = vw_s, vm_s
        with (nc.psum_tensor(p + f"vps{ocp}", [128, 2, 2048], F32) as pps,
              nc.Block() as blk):
            @blk.gpsimd
            def _(g):
                for wave in range(4):
                    if wave >= 1:
                        g.wait_ge(m_s, 4 * ocp + wave)
                    for dj in range(2):
                        d = 2 * wave + dj
                        g.dma_start(wbuf[:, dj, 0:256],
                                    wv_d[d * 128:(d + 1) * 128,
                                         256 * ocp:256 * (ocp + 1)]
                                    ).then_inc(w_s, 16)

            @blk.tensor
            def _(t):
                for wave in range(4):
                    t.wait_ge(w_s, 128 * ocp + 32 * (wave + 1))
                    for dj in range(2):
                        d = 2 * wave + dj
                        for oc2 in range(2):
                            for n4 in range(4):
                                mm = t.matmul(
                                    pps[:, oc2, n4 * 512:(n4 + 1) * 512],
                                    wbuf[:, dj, oc2 * 128:(oc2 + 1) * 128],
                                    lnT[:, d, n4 * 512:(n4 + 1) * 512],
                                    start=(d == 0), stop=(d == 7))
                    mm.then_inc(m_s, 1)

            @blk.vector
            def _(v):
                v.wait_ge(m_s, 4 * ocp + 4)
                for oc2 in range(2):
                    oc = 2 * ocp + oc2
                    v.tensor_scalar(vT_scr[:, oc2, :], pps[:, oc2, :],
                                    ws["bv128"][:, oc:oc + 1], None, ALU.add)
        with (nc.psum_tensor(p + f"vtp{ocp}", [128, 2, 512], F32) as tps,
              nc.Block() as blk):
            @blk.tensor
            def _(t):
                for i in range(8):
                    oc2, s4 = i // 4, i % 4
                    if i >= 2:
                        t.wait_ge(vv_s, Bv + i - 1)
                    for t4 in range(4):
                        tc = 4 * s4 + t4
                        mm = t.matmul(tps[:, i % 2, 128 * t4:128 * (t4 + 1)],
                                      vT_scr[:, oc2, tc * 128:(tc + 1) * 128],
                                      ident[:, :], start=True, stop=True)
                    mm.then_inc(vt_s, 1)

            @blk.vector
            def _(v):
                for i in range(8):
                    oc2, s4 = i // 4, i % 4
                    oc = 2 * ocp + oc2
                    v.wait_ge(vt_s, Bt + i + 1)
                    dst = v_aug.rearrange("pp tc (h e) -> pp tc h e", e=65)
                    v.tensor_copy(
                        dst[:, 4 * s4:4 * s4 + 4, 2 * oc:2 * oc + 2, 0:64],
                        tps[:, i % 2, :].rearrange(
                            "pp (tc h e) -> pp tc h e", tc=4, h=2)
                        ).then_inc(vv_s, 1)

    _esv.close()
    if stop == "proj":
        return
    with nc.Block() as blk:
        @blk.vector
        def _(v):
            v.tensor_copy(gvp[0:64, :], v_aug[0:64, 0, :])
            v.tensor_copy(gvp[64:128, :], v_aug[64:128, 15, :])
    import contextlib as _ctl
    _esc = _ctl.ExitStack()
    sl_s = _esc.enter_context(nc.semaphore(p + "csl"))
    pe_s = _esc.enter_context(nc.semaphore(p + "cpe"))
    av_s = _esc.enter_context(nc.semaphore(p + "cav"))
    sg_s = _esc.enter_context(nc.semaphore(p + "csg"))
    pg_s = _esc.enter_context(nc.semaphore(p + "cpg"))
    ev2_s = _esc.enter_context(nc.semaphore(p + "cev"))
    rs_s = _esc.enter_context(nc.semaphore(p + "crs"))
    rp_s = _esc.enter_context(nc.semaphore(p + "crp"))
    ra_s = _esc.enter_context(nc.semaphore(p + "cra"))
    tt_s = _esc.enter_context(nc.semaphore(p + "ctt"))
    tv_s = _esc.enter_context(nc.semaphore(p + "ctv"))
    _gidx = [0]
    _np = 8
    if (stop or "").startswith("core1"):
        _np = 1
    elif (stop or "").startswith("core2"):
        _np = 2
    elif (stop or "").startswith("core4"):
        _np = 4
    for g_ in range(_np):
        hA, hB = 2 * g_, 2 * g_ + 1
        for G in range(4):
            if stop == "core1g" and G >= 1:
                continue
            q0, q1_ = 512 * G, 512 * (G + 1)
            jlist = [(8 * G - 1) % 32] + [8 * G + r for r in range(8)] \
                + [(8 * G + 8) % 32]
            jinfo = []
            for j in jlist:
                pieces = []
                off = 0
                for (s0, l0) in _l1_window_pieces(j):
                    lo, hi = max(s0, q0), min(s0 + l0, q1_)
                    if lo < hi:
                        pieces.append((lo, hi - lo, off))
                        off += hi - lo
                cons = {}
                for bb in ((j - 1) % 32, j, (j + 1) % 32):
                    ts = 64 * bb
                    if q0 <= ts < q1_:
                        for (lo, ln_, o2) in pieces:
                            if lo <= ts < lo + ln_:
                                cons[bb] = o2 + (ts - lo)
                jinfo.append((j, pieces, cons, off))

            BSL, BPE, BAV = 20 * _gidx[0], 20 * _gidx[0], 8 * _gidx[0]
            BSG, BPG, BEV = 4 * _gidx[0], 2 * _gidx[0], 16 * _gidx[0]
            _gidx[0] += 1
            with (nc.psum_tensor(p + f"a{g_}g{G}sl", [128, 2, 512], F32) as slp,
                  nc.psum_tensor(p + f"a{g_}g{G}yl", [128, 4, 512], F32) as ylp,
                  nc.psum_tensor(p + f"a{g_}g{G}yg", [128, 4, 2, 128], F32) as ygp,
                  nc.Block() as blk):
                def _nb(jj):
                    return jj - 2

                @blk.tensor
                def _(t):
                    for h2 in range(2 if not L1_SKIP_GLOBAL else 0):
                        base = 64 * h2
                        mm = None
                        for (gi, (kts, kl)) in enumerate(((0, 64), (1984, 64))):
                            mm = t.matmul(
                                slp[gi * 64:(gi + 1) * 64, h2, 0:512],
                                kT[base:base + 64, g_, kts:kts + kl],
                                qT[base:base + 64, g_, q0:q1_],
                                start=True, stop=True)
                        mm.then_inc(sg_s, 1)
                    if not L1_SKIP_GLOBAL:
                        t.wait_ge(pg_s, BPG + 2)
                    for h2 in range(2 if not (L1_SKIP_GLOBAL or L1_SKIP_GAV) else 0):
                        h = 2 * g_ + h2
                        mm = None
                        for qc in range(4):
                            mm = t.matmul(
                                ygp[:, qc, h2, 0:65],
                                pg[:, h2, qc * 128:(qc + 1) * 128],
                                gvp[:, 65 * h:65 * h + 65],
                                start=True, stop=True)
                        mm.then_inc(sg_s, 1)
                    for jj, (j, pieces, cons, w) in enumerate(
                            jinfo if not L1_SKIP_LOCAL else []):
                        kr = (j % 2) * 64
                        if jj >= 2:
                            t.wait_ge(pe_s, BPE + 2 * (jj - 1))
                        for h2 in range(2):
                            base = 64 * h2
                            mm = None
                            for (lo, ln_, o2) in pieces:
                                mm = t.matmul(
                                    slp[kr:kr + 64, h2, o2:o2 + ln_],
                                    kT[base:base + 64, g_, 64 * j:64 * j + 64],
                                    qT[base:base + 64, g_, lo:lo + ln_],
                                    start=True, stop=True)
                            mm.then_inc(sl_s, 1)
                        if jj < 2:
                            continue
                        bblk = (jlist[jj] - 1) % 32
                        assert bblk == jlist[jj - 1], (jj, jlist)
                        t.wait_ge(pe_s, BPE + 2 * (jj + 1))
                        nb = _nb(jj)
                        if nb >= 1:
                            t.wait_ge(ev2_s, BEV + 2 * nb)
                        qrow = (bblk % 2) * 64
                        for h2 in range(2):
                            h = 2 * g_ + h2
                            for cp in range(2):
                                bank = 2 * h2 + cp
                                mms = []
                                for jj2 in (jj - 2, jj - 1, jj):
                                    j2 = jinfo[jj2][0]
                                    if j2 % 2 != cp:
                                        continue
                                    po = jinfo[jj2][2][bblk]
                                    kr2 = (j2 % 2) * 64
                                    ktc2 = j2 // 2
                                    mms.append((jj2, po, kr2, ktc2))
                                mm = None
                                for ci, (jj2, po, kr2, ktc2) in enumerate(mms):
                                    mm = t.matmul(
                                        ylp[qrow:qrow + 64, bank, 0:65],
                                        pl[kr2:kr2 + 64, jj2 % 4, 192 * h2 + po:
                                           192 * h2 + po + 64],
                                        v_aug[kr2:kr2 + 64, ktc2,
                                              65 * h:65 * h + 65],
                                        start=(ci == 0), stop=(ci == len(mms) - 1))
                        mm.then_inc(av_s, 1)

                @blk.scalar
                def _(s):
                    for h2 in range(2 if not L1_SKIP_GLOBAL else 0):
                        s.wait_ge(sg_s, BSG + h2 + 1)
                        s.activation(pg[:, h2, :], slp[:, h2, 0:512], AF.Exp,
                                     scale=0.125).then_inc(pg_s, 1)
                    first = True
                    for jj, (j, pieces, cons, w) in enumerate(
                            jinfo if not L1_SKIP_LOCAL else []):
                        kr = (j % 2) * 64
                        if first and not L1_SKIP_GLOBAL:
                            s.wait_ge(sg_s, BSG + 4)
                            first = False
                        if jj >= 4:
                            s.wait_ge(av_s, BAV + jj - 3)
                        s.wait_ge(sl_s, BSL + 2 * jj + 2)
                        s.activation(pl[kr:kr + 64, jj % 4, :]
                                     .rearrange("p (h2 c) -> p h2 c", h2=2)
                                     [:, :, 0:w],
                                     slp[kr:kr + 64, :, 0:w],
                                     AF.Exp, scale=0.125).then_inc(pe_s, 2)

                @blk.vector
                def _(v):
                    if L1_SKIP_GLOBAL:
                        v.nop().then_inc(sg_s, 4)
                    if L1_SKIP_LOCAL:
                        v.nop().then_inc(av_s, 8)
                        v.nop().then_inc(ev2_s, 16)
                    if L1_SKIP_GAV and not L1_SKIP_GLOBAL:
                        v.nop().then_inc(sg_s, 2)
                    for jj, (j, pieces, cons, w) in enumerate(
                            jinfo if not L1_SKIP_LOCAL else []):
                        if jj < 2:
                            continue
                        nb = _nb(jj)
                        bblk = jlist[jj - 1]
                        qrow = (bblk % 2) * 64
                        v.wait_ge(av_s, BAV + nb + 1)
                        for h2 in range(2):
                            v.tensor_copy(
                                ysb[qrow:qrow + 64, nb // 2, h2, :, 0:65],
                                ylp[qrow:qrow + 64, 2 * h2:2 * h2 + 2, 0:65]
                                ).then_inc(ev2_s, 1)
                    v.wait_ge(sg_s, BSG + 4)
                    v.wait_ge(ev2_s, BEV + 16)
                    if not L1_SKIP_NORM:
                        ysv = ysb
                        ytk = y_tok[:, 4 * G:4 * G + 4, :].rearrange(
                            "p a (h e) -> p a h e", h=2)
                        zl = zsc[:, 0:8].rearrange("p (a h) -> p a h", a=4
                                                   ).unsqueeze(3)
                        zg = zsc[:, 8:16].rearrange("p (a h) -> p a h", a=4
                                                    ).unsqueeze(3)
                        v.tensor_tensor(ysv[:, :, :, 0, :], ysv[:, :, :, 0, :],
                                        ysv[:, :, :, 1, :], ALU.add)
                        v.drain()
                        v.reciprocal(zl, ysv[:, :, :, 0, 64:65])
                        v.reciprocal(zg, ygp[:, 0:4, :, 64:65])
                        v.drain()
                        v.tensor_tensor(ytk, ysv[:, :, :, 0, 0:64],
                                        zl.broadcast_to([128, 4, 2, 64]), ALU.mult)
                        v.tensor_tensor(ytmp[:, :, :, :], ygp[:, 0:4, :, 0:64],
                                        zg.broadcast_to([128, 4, 2, 64]), ALU.mult)
                        v.drain()
                        v.tensor_tensor(ytk, ytk, ytmp[:, :, :, :], ALU.add)

        if stop == "core1g":
            return
        if L1_SKIP_ROW:
            continue
        BRS, BRP, BRA = 8 * g_, 8 * g_, 8 * g_
        pr4 = scr9[:, 0:1024].rearrange("pp (a b) -> pp a b", a=2)
        with (nc.psum_tensor(p + f"r{g_}sp", [128, 2, 512], F32) as rsp,
              nc.psum_tensor(p + f"r{g_}yp", [64, 2, 512], F32) as ryp,
              nc.Block() as blk):
            @blk.tensor
            def _(t):
                for idx in range(8):
                    h2, sup = idx // 4, idx % 4
                    base = 64 * h2
                    h = 2 * g_ + h2
                    if idx >= 2:
                        t.wait_ge(rp_s, BRP + idx - 1)
                    for t4 in range(4):
                        tc = 4 * sup + t4
                        mm = t.matmul(rsp[:, idx % 2, 64 * t4:64 * t4 + 64],
                                      kT[base:base + 64, g_,
                                         tc * 128:(tc + 1) * 128],
                                      qT[base:base + 64, g_, 0:64],
                                      start=True, stop=True)
                    mm.then_inc(rs_s, 1)
                    t.wait_ge(rp_s, BRP + idx + 1)
                    for t4 in range(4):
                        tc = 4 * sup + t4
                        mm = t.matmul(ryp[0:64, h2, 0:65],
                                      pr4[:, idx % 2, 64 * t4:64 * t4 + 64],
                                      v_aug[:, tc, 65 * h:65 * h + 65],
                                      start=(sup == 0 and t4 == 0),
                                      stop=(sup == 3 and t4 == 3))
                    mm.then_inc(ra_s, 1)

            @blk.scalar
            def _(s):
                for idx in range(8):
                    s.wait_ge(rs_s, BRS + idx + 1)
                    if idx >= 2:
                        s.wait_ge(ra_s, BRA + idx - 1)
                    s.activation(pr4[:, idx % 2, 0:256], rsp[:, idx % 2, 0:256],
                                 AF.Exp, scale=0.125).then_inc(rp_s, 1)

            @blk.vector
            def _(v):
                v.wait_ge(ra_s, BRA + 8)
                for h2 in range(2):
                    v.reciprocal(smalls[0:64, 0:1], ryp[0:64, h2, 64:65])
                    v.drain()
                    v.tensor_scalar(y_tok[0:64, 0, 64 * h2:64 * h2 + 64],
                                    ryp[0:64, h2, 0:64],
                                    smalls[0:64, 0:1], None, ALU.mult)
                    v.drain()

        if L1_SKIP_TRANS:
            continue
        BTT, BTV = 16 * g_, 16 * g_
        with (nc.psum_tensor(p + f"t{g_}p", [128, 2, 512], F32) as tps,
              nc.Block() as blk):
            @blk.tensor
            def _(t):
                for tc in range(16):
                    if tc >= 2:
                        t.wait_ge(tv_s, BTV + tc - 1)
                    t.matmul(tps[:, tc % 2, 0:128], y_tok[:, tc, :], ident[:, :],
                             start=True, stop=True).then_inc(tt_s, 1)

            @blk.vector
            def _(v):
                for tc in range(16):
                    v.wait_ge(tt_s, BTT + tc + 1)
                    v.tensor_copy(yT[:, g_, tc * 128:(tc + 1) * 128],
                                  tps[:, tc % 2, 0:128]).then_inc(tv_s, 1)

    wo_q = wb[:, 0:2048].rearrange("pp (a b) -> pp a b", a=8)
    with (nc.semaphore(p + "ow") as ow_s, nc.semaphore(p + "om") as om_s,
          nc.semaphore(p + "oe") as oe_s,
          nc.psum_tensor(p + "ops", [128, 2, 512], F32) as ops,
          nc.Block() as blk):
        @blk.gpsimd
        def _(g):
            for dq in range(4):
                if dq >= 1:
                    g.wait_ge(om_s, 16 * dq)
                g.dma_start(wb[0:1, 2176:2432],
                            brow_d[0:1, dq * 256:(dq + 1) * 256]).then_inc(ow_s, 16)
                for g2 in range(8):
                    g.dma_start(wo_q[:, g2, :],
                                wo_d[g2 * 128:(g2 + 1) * 128,
                                     dq * 256:(dq + 1) * 256]).then_inc(ow_s, 16)

        @blk.tensor
        def _(t):
            for dq in range(4):
                t.wait_ge(ow_s, 16 * 9 * (dq + 1))
                for ti in range(NT):
                    n = 16 * dq + ti
                    if n >= 2:
                        t.wait_ge(oe_s, n - 1)
                    for g2 in range(8):
                        t.matmul(ops[:, ti % 2, 0:256],
                                 yT[:, g2, ti * 128:(ti + 1) * 128],
                                 wo_q[:, g2, :],
                                 start=(g2 == 0), stop=False)
                    mm = t.matmul(ops[:, ti % 2, 0:256], wb[0:1, 2048:2176],
                                  wb[0:1, 2176:2432],
                                  start=False, stop=True)
                    mm.then_inc(om_s, 1)

        @blk.vector
        def _(v):
            for dq in range(4):
                for ti in range(NT):
                    n = 16 * dq + ti
                    v.wait_ge(om_s, n + 1)
                    v.scalar_tensor_tensor(
                        xs[:, ti, dq * 256:(dq + 1) * 256], ops[:, ti % 2, 0:256],
                        1.0, xs[:, ti, dq * 256:(dq + 1) * 256],
                        ALU.mult, ALU.add).then_inc(oe_s, 1)


def build_l1attn_standalone():
    nc = bass.Bass("TRN2", target_bir_lowering=False)
    x_in = nc.dram_tensor("x", [T, D], F32, kind="ExternalInput").ap()
    wq_d = nc.dram_tensor("wq", [1024, 1024], F32, kind="ExternalInput").ap()
    wk_d = nc.dram_tensor("wk", [1024, 1024], F32, kind="ExternalInput").ap()
    wv_d = nc.dram_tensor("wv", [1024, 1024], F32, kind="ExternalInput").ap()
    wo_d = nc.dram_tensor("wo", [1024, 1024], F32, kind="ExternalInput").ap()
    bq_d = nc.dram_tensor("bq", [128, 8], F32, kind="ExternalInput").ap()
    bk_d = nc.dram_tensor("bk", [128, 8], F32, kind="ExternalInput").ap()
    bv_d = nc.dram_tensor("bv", [128, 8], F32, kind="ExternalInput").ap()
    bo_d = nc.dram_tensor("bo", [1, 1024], F32, kind="ExternalInput").ap()
    x_out = nc.dram_tensor("xo", [T, D], F32, kind="ExternalOutput").ap()

    import contextlib
    with contextlib.ExitStack() as _es:
        def _al(name, shape, dt):
            return _es.enter_context(nc.sbuf_tensor(name, shape, dt))
        xs = _al("xs", [128, NT, D], F32)
        ident = _al("ident", [128, 128], BF16)
        lnT = _al("lnT", [128, ND, T], BF16)
        qT = _al("qT", [128, ND, T], BF16)
        kT = _al("kT", [128, ND, T], BF16)
        v_aug = _al("v_aug", [128, NT, 1040], BF16)
        wbuf = _al("wbuf", [128, 2432], BF16)
        scr9 = _al("scr9", [128, 4096], BF16)
        gvp = _al("gvp", [128, 1040], BF16)
        f32scr = _al("f32scr", [128, 64], F32)
        sums = f32scr[:, 0:16]
        sumsq = f32scr[:, 16:32]
        scale = f32scr[:, 32:48]
        bias = f32scr[:, 48:64]
        bq_sb = f32scr[:, 32:40]
        bk_sb = f32scr[:, 40:48]
        bv128 = f32scr[:, 48:56]
        smalls = f32scr[:, 0:24]
        ln_scr = scr9[:, 1536:3584].rearrange("pp (a b) -> pp a b", a=2)
        sq_scr = scr9[:, 0:1024]
        ws = dict(lnT=lnT, qT=qT, kT=kT, v_aug=v_aug, wbuf=wbuf, scr9=scr9,
                  bq_sb=bq_sb, bk_sb=bk_sb, bv128=bv128, bv128_d=bv_d,
                  gvp=gvp, smalls=smalls,
                  sums=sums, sumsq=sumsq, scale=scale, bias=bias,
                  sq_scr=sq_scr, ln_scr=ln_scr)
        with (nc.semaphore("xin") as xin_s, nc.Block() as blk):
            @blk.sync
            def _(s):
                s.dma_start(xs[:, :, :],
                            x_in.rearrange("(n pp) m -> pp n m", pp=128)
                            ).then_inc(xin_s, 16)
                s.wait_ge(xin_s, 16)
        emit_consts(nc, ident)
        emit_l1attn(nc, "a1", xs, ident, None, wq_d, wk_d, wv_d, wo_d,
                    bq_d, bk_d, bo_d, ws)
        with (nc.semaphore("xout") as xo_s, nc.Block() as blk):
            @blk.sync
            def _(s):
                s.dma_start(x_out.rearrange("(n pp) m -> pp n m", pp=128),
                            xs[:, :, :]).then_inc(xo_s, 16)
                s.wait_ge(xo_s, 16)
    return nc


def host_l1attn_weights(Wq, bq, Wk, bk, Wv, bv, Wo, bo):
    wq = np.ascontiguousarray(Wq.T)
    wk = np.ascontiguousarray(Wk.T)
    wv = np.ascontiguousarray(Wv.T)
    wo = np.ascontiguousarray(Wo.T)
    bq128 = np.ascontiguousarray(bq.reshape(8, 128).T)
    bk128 = np.ascontiguousarray(bk.reshape(8, 128).T)
    bv128 = np.ascontiguousarray(bv.reshape(8, 128).T)
    bo_row = np.ascontiguousarray(bo.reshape(1, 1024))
    return wq, wk, wv, wo, bq128, bk128, bv128, bo_row


emit_l1attn_phased = emit_l1attn


def emit_l0attn(nc, p, xs, ident, wq_d, wk_d, wv_d, wo_d, ones3_d, ws, lnsh=None):
    lnT = ws["lnT"]
    qT = ws["qTa"]
    kT = ws["kTa"]
    v_aug = ws["v_aug"]
    wpb = ws["wpb"]
    y_tok = ws["y_tok0"]
    Bp = ws["Bp"]
    tril = ws["tril"]
    onescol = ws["onescol"]
    C_sb = ws["C_sb"]
    smalls = ws["smalls"]
    zc = ws["zc0"]
    yT = lnT

    emit_ln_to_lnT(nc, p + "ln", xs, ident, lnT, ws, shared=lnsh)

    with nc.Block() as blk:
        @blk.gpsimd
        def _(g):
            g.memset(tril[:, :], 1.0)
            g.drain()
            g.affine_select(tril[:, :], tril[:, :], [[1, 128]], ALU.is_ge, 0.0,
                            channel_multiplier=-1)
            g.drain()
            g.memset(onescol[:, :], 1.0)
            g.drain()
            g.memset(C_sb[:, :], 0.0)
            vh = v_aug.rearrange("pp tc (h e) -> pp tc h e", e=65)
            g.memset(vh[:, :, :, 64:65], 1.0)

    for which, w_d, dst in (("q", wq_d, qT), ("k", wk_d, kT)):
        with (nc.semaphore(p + which + "w") as w_s,
              nc.semaphore(p + which + "m") as m_s,
              nc.semaphore(p + which + "e") as e_s,
              nc.psum_tensor(p + which + "ps", [128, 2, 2048], F32) as pps,
              nc.Block() as blk):
            @blk.gpsimd
            def _(g):
                for dj in range(8):
                    g.dma_start(wpb[:, dj, 0:768],
                                w_d[dj * 128:(dj + 1) * 128, :]).then_inc(w_s, 16)

            @blk.tensor
            def _(t):
                t.wait_ge(w_s, 128)
                for oc in range(6):
                    if oc >= 2:
                        t.wait_ge(e_s, oc - 1)
                    for dj in range(8):
                        for n4 in range(4):
                            mm = t.matmul(
                                pps[:, oc % 2, n4 * 512:(n4 + 1) * 512],
                                wpb[:, dj, oc * 128:(oc + 1) * 128],
                                lnT[:, dj, n4 * 512:(n4 + 1) * 512],
                                start=(dj == 0), stop=(dj == 7))
                    mm.then_inc(m_s, 1)

            @blk.vector
            def _(v):
                for oc in range(6):
                    v.wait_ge(m_s, oc + 1)
                    v.tensor_scalar(dst[:, oc, :], pps[:, oc % 2, :],
                                    0.5, None, ALU.mult).then_inc(e_s, 1)

        with (nc.semaphore(p + which + "o") as o_s, nc.Block() as blk):
            @blk.gpsimd
            def _(g):
                nd = 0
                for oc in range(6):
                    for a_ in range(3):
                        r = 16 + 32 * a_
                        g.dma_start(dst[r:r + 1, oc, :],
                                    ones3_d[a_:a_ + 1, :]).then_inc(o_s, 16)
                        nd += 1
                g.wait_ge(o_s, 16 * nd)

    with (nc.semaphore(p + "vw") as w_s, nc.semaphore(p + "vm") as m_s,
          nc.semaphore(p + "ve") as e_s,
          nc.psum_tensor(p + "vps", [128, 2, 1024], F32) as vps,
          nc.Block() as blk):
        @blk.gpsimd
        def _(g):
            for dj in range(8):
                g.dma_start(wpb[:, dj, :],
                            wv_d[dj * 128:(dj + 1) * 128, :]).then_inc(w_s, 16)

        @blk.tensor
        def _(t):
            t.wait_ge(w_s, 128)
            for ti in range(NT):
                if ti >= 2:
                    t.wait_ge(e_s, ti - 1)
                for dj in range(8):
                    for n2 in range(2):
                        mm = t.matmul(vps[:, ti % 2, n2 * 512:(n2 + 1) * 512],
                                      lnT[:, dj, ti * 128:(ti + 1) * 128],
                                      wpb[:, dj, n2 * 512:(n2 + 1) * 512],
                                      start=(dj == 0), stop=(dj == 7))
                mm.then_inc(m_s, 1)

        @blk.vector
        def _(v):
            vh = v_aug.rearrange("pp tc (h e) -> pp tc h e", e=65)
            for ti in range(NT):
                v.wait_ge(m_s, ti + 1)
                v.tensor_copy(vh[:, ti, :, 0:64],
                              vps[:, ti % 2, :].rearrange("pp (h e) -> pp h e",
                                                          h=16)).then_inc(e_s, 1)

    import contextlib as _ctl
    _esc = _ctl.ExitStack()
    ss_s = _esc.enter_context(nc.semaphore(p + "css"))
    ph_s = _esc.enter_context(nc.semaphore(p + "cph"))
    p2_s = _esc.enter_context(nc.semaphore(p + "cp2"))
    av_s = _esc.enter_context(nc.semaphore(p + "cav"))
    cs_s = _esc.enter_context(nc.semaphore(p + "ccs"))
    nm_s = _esc.enter_context(nc.semaphore(p + "cnm"))
    ca_s = _esc.enter_context(nc.semaphore(p + "cca"))
    tt_s = _esc.enter_context(nc.semaphore(p + "ctt"))
    tv_s = _esc.enter_context(nc.semaphore(p + "ctv"))
    ab_s = _esc.enter_context(nc.semaphore(p + "cab"))

    def _batches(i):
        out = []
        j0 = 0
        while j0 <= i:
            out.append((j0, min(4, i + 1 - j0)))
            j0 += 4
        return out

    _binfo = []
    _gb = 0
    _ss = 0
    for i in range(NT):
        bl = _batches(i)
        for h in range(16):
            for k, (j0, nj) in enumerate(bl):
                diag = (j0 + nj - 1 == i)
                _binfo.append((i, h, k, j0, nj, _gb, _ss, diag))
                _gb += 1
                _ss += nj
    _bykey = {(i, h, k): e for e in _binfo for (i, h, k) in [(e[0], e[1], e[2])]}

    with (nc.psum_tensor(p + "sb", [128, 2, 512], F32) as sbat,
          nc.psum_tensor(p + "yp", [128, 16, 128], F32) as yps,
          nc.psum_tensor(p + "tp", [128, 2, 512], F32) as tps,
          nc.Block() as blk):
        @blk.tensor
        def _(t):
            for i in range(NT):
                bl = _batches(i)
                nb_ = len(bl)

                def emit_s(h, k):
                    (j0, nj) = bl[k]
                    gb = _bykey[(i, h, k)][5]
                    hb = 32 * (h % 3)
                    oc = h // 3
                    if gb >= 2:
                        t.wait_ge(ph_s, gb - 1)
                    for jj in range(nj):
                        j = j0 + jj
                        t.matmul(sbat[:, gb % 2, jj * 128:(jj + 1) * 128],
                                 kT[hb:hb + 17, oc, j * 128:(j + 1) * 128],
                                 qT[hb:hb + 17, oc, i * 128:(i + 1) * 128],
                                 start=True, stop=True).then_inc(ss_s, 1)

                def emit_av(h, k):
                    (j0, nj) = bl[k]
                    gb = _bykey[(i, h, k)][5]
                    t.wait_ge(ph_s, gb + 1)
                    if i >= 1 and h == 0 and k == 0:
                        t.wait_ge(nm_s, i)
                        t.wait_ge(ca_s, i)
                    for jj in range(nj):
                        j = j0 + jj
                        mm = t.matmul(
                            yps[:, h, 0:65],
                            Bp[:, gb % 2, jj * 128:(jj + 1) * 128],
                            v_aug[:, j, 65 * h:65 * h + 65],
                            start=(k == 0 and jj == 0), stop=False)
                    mm.then_inc(ab_s, 1)
                    if k == nb_ - 1:
                        mm = t.matmul(yps[:, h, 0:65], tril[:, :],
                                      v_aug[:, i, 65 * h:65 * h + 65],
                                      start=False, stop=(i == 0))
                        if i >= 1:
                            mm = t.matmul(yps[:, h, 0:65], onescol[:, :],
                                          ws["C_bf"][0:1, 65 * h:65 * h + 65],
                                          start=False, stop=True)
                        mm.then_inc(av_s, 1)

                stream = [(h, k) for h in range(16) for k in range(nb_)]
                for n, (h, k) in enumerate(stream):
                    emit_s(h, k)
                    if n >= 1:
                        emit_av(*stream[n - 1])
                emit_av(*stream[-1])
                if i >= 1:
                    t.wait_ge(nm_s, i)
                    for occ in range(8):
                        n = 8 * (i - 1) + occ
                        if n >= 2:
                            t.wait_ge(tv_s, n - 1)
                        t.matmul(tps[:, n % 2, 0:128],
                                 y_tok[:, (i - 1) % 2, occ * 128:(occ + 1) * 128],
                                 ident[:, :], start=True, stop=True
                                 ).then_inc(tt_s, 1)
                t.wait_ge(nm_s, i + 1)
                for h in range(16):
                    mm = t.matmul(yps[0:1, h, 64:128],
                                  ws["onesc2"][:, 0:1],
                                  v_aug[:, i, 65 * h:65 * h + 64],
                                  start=True, stop=True)
                mm.then_inc(cs_s, 1)
            for occ in range(8):
                n = 8 * (NT - 1) + occ
                t.wait_ge(tv_s, n - 1)
                t.matmul(tps[:, n % 2, 0:128],
                         y_tok[:, (NT - 1) % 2, occ * 128:(occ + 1) * 128],
                         ident[:, :], start=True, stop=True).then_inc(tt_s, 1)

        @blk.scalar
        def _(s):
            for (i, h, k, j0, nj, gb, ssb, diag) in _binfo:
                s.wait_ge(ss_s, ssb + nj)
                if gb >= 2:
                    s.wait_ge(ab_s, gb - 1)
                if diag:
                    s.activation(Bp[:, gb % 2, 0:128 * nj],
                                 sbat[:, gb % 2, 0:128 * nj],
                                 AF.Square).then_inc(p2_s, 1)
                else:
                    s.wait_ge(ph_s, gb)
                    s.activation(Bp[:, gb % 2, 0:128 * nj],
                                 sbat[:, gb % 2, 0:128 * nj],
                                 AF.Square).then_inc(ph_s, 1)

        @blk.vector
        def _(v):
            ndiag = 0
            for i in range(NT):
                bl = _batches(i)
                for h in range(16):
                    k = len(bl) - 1
                    (_, _, _, j0, nj, gb, ssb, diag) = _bykey[(i, h, k)]
                    assert diag
                    ndiag += 1
                    v.wait_ge(p2_s, ndiag)
                    v.tensor_tensor(Bp[:, gb % 2, 128 * (nj - 1):128 * nj],
                                    Bp[:, gb % 2, 128 * (nj - 1):128 * nj],
                                    tril[:, :], ALU.mult).then_inc(ph_s, 1)
                if i >= 1:
                    for occ in range(8):
                        n = 8 * (i - 1) + occ
                        v.wait_ge(tt_s, n + 1)
                        v.tensor_copy(yT[:, occ, (i - 1) * 128:i * 128],
                                      tps[:, n % 2, 0:128]).then_inc(tv_s, 1)
                v.wait_ge(av_s, 16 * (i + 1))
                if i >= 1:
                    v.wait_ge(tv_s, 8 * i)
                v.reciprocal(zc[:, 0:16].unsqueeze(2), yps[:, :, 64:65])
                v.drain()
                v.tensor_tensor(
                    y_tok[:, i % 2, :].rearrange("pp (h e) -> pp h e", h=16),
                    yps[:, :, 0:64],
                    zc[:, 0:16].unsqueeze(2).broadcast_to([128, 16, 64]),
                    ALU.mult).then_inc(nm_s, 1)
                v.wait_ge(cs_s, i + 1)
                cview = C_sb[0:1, :].rearrange("a (h e) -> a h e", e=65)
                v.tensor_tensor(cview[:, :, 0:64], cview[:, :, 0:64],
                                yps[0:1, :, 64:128][:, :, 0:64], ALU.add)
                v.drain()
                for h in range(16):
                    v.memset(C_sb[0:1, 65 * h + 64:65 * h + 65],
                             float(128 * (i + 1)))
                v.drain()
                v.tensor_copy(ws["C_bf"][0:1, :], C_sb[0:1, :])
                v.drain()
                v.nop().then_inc(ca_s, 1)
            for occ in range(8):
                n = 8 * (NT - 1) + occ
                v.wait_ge(tt_s, n + 1)
                v.tensor_copy(yT[:, occ, (NT - 1) * 128:NT * 128],
                              tps[:, n % 2, 0:128]).then_inc(tv_s, 1)
    _esc.close()

    with (nc.semaphore(p + "ow") as ow_s, nc.semaphore(p + "om") as om_s,
          nc.semaphore(p + "oe") as oe_s,
          nc.psum_tensor(p + "ops", [128, 2, 512], F32) as ops,
          nc.Block() as blk):
        @blk.gpsimd
        def _(g):
            for dq in range(4):
                if dq >= 1:
                    g.wait_ge(om_s, 16 * dq)
                for g2 in range(8):
                    g.dma_start(wpb[:, g2, 0:256],
                                wo_d[g2 * 128:(g2 + 1) * 128,
                                     dq * 256:(dq + 1) * 256]).then_inc(ow_s, 16)

        @blk.tensor
        def _(t):
            for dq in range(4):
                t.wait_ge(ow_s, 16 * 8 * (dq + 1))
                for ti in range(NT):
                    n = 16 * dq + ti
                    if n >= 2:
                        t.wait_ge(oe_s, n - 1)
                    for g2 in range(8):
                        mm = t.matmul(ops[:, ti % 2, 0:256],
                                      yT[:, g2, ti * 128:(ti + 1) * 128],
                                      wpb[:, g2, 0:256],
                                      start=(g2 == 0), stop=(g2 == 7))
                    mm.then_inc(om_s, 1)

        @blk.vector
        def _(v):
            for dq in range(4):
                for ti in range(NT):
                    n = 16 * dq + ti
                    v.wait_ge(om_s, n + 1)
                    v.scalar_tensor_tensor(
                        xs[:, ti, dq * 256:(dq + 1) * 256], ops[:, ti % 2, 0:256],
                        1.0, xs[:, ti, dq * 256:(dq + 1) * 256],
                        ALU.mult, ALU.add).then_inc(oe_s, 1)


def build_l0attn_standalone():
    nc = bass.Bass("TRN2", target_bir_lowering=False)
    x_in = nc.dram_tensor("x", [T, D], F32, kind="ExternalInput").ap()
    wq_d = nc.dram_tensor("wq", [1024, 768], F32, kind="ExternalInput").ap()
    wk_d = nc.dram_tensor("wk", [1024, 768], F32, kind="ExternalInput").ap()
    wv_d = nc.dram_tensor("wv", [1024, 1024], F32, kind="ExternalInput").ap()
    wo_d = nc.dram_tensor("wo", [1024, 1024], F32, kind="ExternalInput").ap()
    ones3_d = nc.dram_tensor("ones3", [3, 2048], F32, kind="ExternalInput").ap()
    x_out = nc.dram_tensor("xo", [T, D], F32, kind="ExternalOutput").ap()

    import contextlib
    with contextlib.ExitStack() as _es:
        def _al(name, shape, dt):
            return _es.enter_context(nc.sbuf_tensor(name, shape, dt))
        xs = _al("xs", [128, NT, D], F32)
        ident = _al("ident", [128, 128], BF16)
        lnT = _al("lnT", [128, ND, T], BF16)
        qTa = _al("qTa", [128, 6, T], BF16)
        kTa = _al("kTa", [128, 6, T], BF16)
        v_aug = _al("v_aug", [128, NT, 1040], BF16)
        wpb = _al("wpb", [128, 8, 1024], BF16)
        y_tok0 = _al("y_tok0", [128, 2, 1024], BF16)
        Bp = _al("Bp", [128, 2, 512], BF16)
        tril = _al("tril", [128, 128], BF16)
        onescol = _al("onescol", [1, 128], BF16)
        onesc2 = _al("onesc2", [128, 1], BF16)
        C_sb = _al("C_sb", [1, 1040], F32)
        C_bf = _al("C_bf", [1, 1040], BF16)
        f32scr = _al("f32scr", [128, 80], F32)
        sums = f32scr[:, 0:16]
        sumsq = f32scr[:, 16:32]
        scale = f32scr[:, 32:48]
        bias = f32scr[:, 48:64]
        zc0 = f32scr[:, 64:80]
        smalls = f32scr[:, 0:24]
        ln_scr = y_tok0
        sq_scr = y_tok0[:, 0, :]
        ws = dict(lnT=lnT, qTa=qTa, kTa=kTa, v_aug=v_aug, wpb=wpb,
                  y_tok0=y_tok0, Bp=Bp, tril=tril, onescol=onescol,
                  onesc2=onesc2, C_sb=C_sb, C_bf=C_bf, zc0=zc0, smalls=smalls,
                  sums=sums, sumsq=sumsq, scale=scale, bias=bias,
                  sq_scr=sq_scr, ln_scr=ln_scr)
        with (nc.semaphore("xin") as xin_s, nc.Block() as blk):
            @blk.sync
            def _(s):
                s.dma_start(xs[:, :, :],
                            x_in.rearrange("(n pp) m -> pp n m", pp=128)
                            ).then_inc(xin_s, 16)
                s.wait_ge(xin_s, 16)
            @blk.gpsimd
            def _(g):
                g.memset(ws["onesc2"][:, :], 1.0)
        emit_consts(nc, ident)
        emit_l0attn(nc, "a0", xs, ident, wq_d, wk_d, wv_d, wo_d, ones3_d, ws)
        with (nc.semaphore("xout") as xo_s, nc.Block() as blk):
            @blk.sync
            def _(s):
                s.dma_start(x_out.rearrange("(n pp) m -> pp n m", pp=128),
                            xs[:, :, :]).then_inc(xo_s, 16)
                s.wait_ge(xo_s, 16)
    return nc


def host_l0attn_weights(Wq, Wk, Wv, Wo):
    def pad6(W):
        out = np.zeros((1024, 768), dtype=W.dtype)
        for oc in range(6):
            for a_ in range(3):
                h = 3 * oc + a_
                if h >= 16:
                    break
                out[:, oc * 128 + 32 * a_:oc * 128 + 32 * a_ + 16] = \
                    W[:, 16 * h:16 * h + 16]
        return np.ascontiguousarray(out)
    ones3 = np.ones((3, 2048), dtype=np.float32)
    return pad6(Wq), pad6(Wk), np.ascontiguousarray(Wv), \
        np.ascontiguousarray(Wo), ones3


def build_fused():
    nc = bass.Bass("TRN2", target_bir_lowering=False)
    x_in = nc.dram_tensor("x", [T, D], F32, kind="ExternalInput").ap()
    a0wq = nc.dram_tensor("a0wq", [1024, 768], F32, kind="ExternalInput").ap()
    a0wk = nc.dram_tensor("a0wk", [1024, 768], F32, kind="ExternalInput").ap()
    a0wv = nc.dram_tensor("a0wv", [1024, 1024], F32, kind="ExternalInput").ap()
    a0wo = nc.dram_tensor("a0wo", [1024, 1024], F32, kind="ExternalInput").ap()
    ones3_d = nc.dram_tensor("ones3", [3, 2048], F32, kind="ExternalInput").ap()
    f0wg = nc.dram_tensor("f0wg", [24, 1024, 128], I32, kind="ExternalInput").ap()
    f0wu = nc.dram_tensor("f0wu", [24, 1024, 128], I32, kind="ExternalInput").ap()
    f0wd = nc.dram_tensor("f0wd", [24, 128, 1024], I32, kind="ExternalInput").ap()
    a1wq = nc.dram_tensor("a1wq", [1024, 1024], F32, kind="ExternalInput").ap()
    a1wk = nc.dram_tensor("a1wk", [1024, 1024], F32, kind="ExternalInput").ap()
    a1wv = nc.dram_tensor("a1wv", [1024, 1024], F32, kind="ExternalInput").ap()
    a1wo = nc.dram_tensor("a1wo", [1024, 1024], F32, kind="ExternalInput").ap()
    a1bq = nc.dram_tensor("a1bq", [128, 8], F32, kind="ExternalInput").ap()
    a1bk = nc.dram_tensor("a1bk", [128, 8], F32, kind="ExternalInput").ap()
    a1bv = nc.dram_tensor("a1bv", [128, 8], F32, kind="ExternalInput").ap()
    a1bo = nc.dram_tensor("a1bo", [1, 1024], F32, kind="ExternalInput").ap()
    g1w1 = nc.dram_tensor("g1w1", [8, 128, 4096], F32, kind="ExternalInput").ap()
    g1w2 = nc.dram_tensor("g1w2", [32, 128, 1024], F32, kind="ExternalInput").ap()
    g1cw = nc.dram_tensor("g1cw", [128, 32, 3], F32, kind="ExternalInput").ap()
    g1cb = nc.dram_tensor("g1cb", [128, 32], F32, kind="ExternalInput").ap()
    hs_d = nc.dram_tensor("hs_scr", [32, 128, 2048], BF16).ap()
    x_out = nc.dram_tensor("xo", [T, D], F32, kind="ExternalOutput").ap()

    import contextlib
    with contextlib.ExitStack() as es0:
        def _al0(name, shape, dt):
            return es0.enter_context(nc.sbuf_tensor(name, shape, dt))
        xs = _al0("xs", [128, NT, D], F32)
        ident = _al0("ident", [128, 128], BF16)
        LN = tuple(es0.enter_context(nc.semaphore("LN" + s))
                   for s in ("c1", "c2", "ap", "tp", "ev"))

        with (nc.semaphore("xin") as xin_s, nc.Block() as blk):
            @blk.sync
            def _(s):
                s.dma_start(xs[:, :, :],
                            x_in.rearrange("(n pp) m -> pp n m", pp=128)
                            ).then_inc(xin_s, 16)
                s.wait_ge(xin_s, 16)
        emit_consts(nc, ident)

        with contextlib.ExitStack() as es1:
            def _al(name, shape, dt):
                return es1.enter_context(nc.sbuf_tensor(name, shape, dt))
            lnT = _al("b1lnT", [128, ND, T], BF16)
            qTa = _al("b1qTa", [128, 6, T], BF16)
            kTa = _al("b1kTa", [128, 6, T], BF16)
            v_aug = _al("b1vaug", [128, NT, 1040], BF16)
            wpb = _al("b1wpb", [128, 8, 1024], BF16)
            y_tok0 = _al("b1ytok0", [128, 2, 1024], BF16)
            Bp = _al("b1Bp", [128, 2, 512], BF16)
            tril = _al("b1tril", [128, 128], BF16)
            onescol = _al("b1onescol", [1, 128], BF16)
            C_sb = _al("b1Csb", [1, 1040], F32)
            C_bf = _al("b1Cbf", [1, 1040], BF16)
            f32scr = _al("b1f32scr", [128, 80], F32)
            onesc2 = _al("b1onesc2", [128, 1], BF16)
            with nc.Block() as blk:
                @blk.gpsimd
                def _(g):
                    g.memset(onesc2[:, :], 1.0)
            ws = dict(lnT=lnT, qTa=qTa, kTa=kTa, v_aug=v_aug, wpb=wpb,
                      y_tok0=y_tok0, Bp=Bp, tril=tril, onescol=onescol,
                      onesc2=onesc2, C_sb=C_sb, C_bf=C_bf,
                      zc0=f32scr[:, 64:80], smalls=f32scr[:, 0:24],
                      sums=f32scr[:, 0:16], sumsq=f32scr[:, 16:32],
                      scale=f32scr[:, 32:48], bias=f32scr[:, 48:64],
                      sq_scr=y_tok0[:, 0, :], ln_scr=y_tok0)
            emit_l0attn(nc, "a0", xs, ident, a0wq, a0wk, a0wv, a0wo, ones3_d,
                        ws, lnsh=(LN, 0))

        with contextlib.ExitStack() as es2:
            def _al(name, shape, dt):
                return es2.enter_context(nc.sbuf_tensor(name, shape, dt))
            lnT = _al("b2lnT", [128, ND, T], BF16)
            wd_sb = _al("b2wd", [128, 24, 1024], BF16)
            wg_sb = _al("b2wg", [128, 2, 8, 128], BF16)
            wu_sb = _al("b2wu", [128, 2, 8, 128], BF16)
            hT = _al("b2hT", [128, 24, 1024], BF16)
            silu_scr = _al("b2silu", [128, 1024], BF16)
            sums = _al("b2sums", [128, NT], F32)
            sumsq = _al("b2sumsq", [128, NT], F32)
            scale = _al("b2scale", [128, NT], F32)
            bias = _al("b2bias", [128, NT], F32)
            ws = dict(lnT=lnT, wd_sb=wd_sb, wg_sb=wg_sb, wu_sb=wu_sb, hT=hT,
                      silu_scr=silu_scr, sums=sums, sumsq=sumsq,
                      scale=scale, bias=bias, sq_scr=silu_scr, ln_scr=hT)
            emit_l0ffn(nc, "f0", xs, ident, f0wg, f0wu, f0wd, ws, lnsh=(LN, 1))

        with contextlib.ExitStack() as es3:
            def _al(name, shape, dt):
                return es3.enter_context(nc.sbuf_tensor(name, shape, dt))
            lnT = _al("b3lnT", [128, ND, T], BF16)
            qT = _al("b3qT", [128, ND, T], BF16)
            kT = _al("b3kT", [128, ND, T], BF16)
            v_aug = _al("b3vaug", [128, NT, 1040], BF16)
            wbuf = _al("b3wbuf", [128, 2432], BF16)
            scr9 = _al("b3scr9", [128, 4096], BF16)
            gvp = _al("b3gvp", [128, 1040], BF16)
            f32scr = _al("b3f32scr", [128, 64], F32)
            ws = dict(lnT=lnT, qT=qT, kT=kT, v_aug=v_aug, wbuf=wbuf, scr9=scr9,
                      bq_sb=f32scr[:, 32:40], bk_sb=f32scr[:, 40:48],
                      bv128=f32scr[:, 48:56], bv128_d=a1bv,
                      gvp=gvp, smalls=f32scr[:, 0:24],
                      sums=f32scr[:, 0:16], sumsq=f32scr[:, 16:32],
                      scale=f32scr[:, 32:48], bias=f32scr[:, 48:64],
                      sq_scr=scr9[:, 0:1024],
                      ln_scr=scr9[:, 1536:3584].rearrange(
                          "pp (a b) -> pp a b", a=2))
            emit_l1attn(nc, "a1", xs, ident, None, a1wq, a1wk, a1wv, a1wo,
                        a1bq, a1bk, a1bo, ws, lnsh=(LN, 2))

        with contextlib.ExitStack() as es4:
            def _al(name, shape, dt):
                return es4.enter_context(nc.sbuf_tensor(name, shape, dt))
            q1T = _al("b4q1T", [128, ND, T], BF16)
            tern = _al("b4tern", [128, 32768], BF16)
            big16 = _al("b4big16", [128, 8192], BF16)
            scr16k = _al("b4scr16k", [128, 4096], F32)
            hsp = _al("b4hsp", [128, 2048], BF16)
            M = _al("b4Mt", [128, 2048], BF16)
            Mn = _al("b4Mn", [128, 2048], BF16)
            sums = _al("b4sums", [128, NT], F32)
            sumsq = _al("b4sumsq", [128, NT], F32)
            scale = _al("b4scale", [128, NT], F32)
            bias = _al("b4bias", [128, NT], F32)
            s2 = _al("b4s2", [128, NT], F32)
            inv_s2 = _al("b4invs2", [128, NT], F32)
            comb = _al("b4comb", [128, NT], F32)
            acc = _al("b4acc", [128, 32], F32)
            smalls = _al("b4smalls", [128, 8], F32)
            cw_sb = _al("b4cw", [128, 32, 3], F32)
            cb_sb = _al("b4cb", [128, 32], F32)
            s0c = _al("b4s0c", [128, 32], F32)
            smc = _al("b4smc", [128, 32], F32)
            spc = _al("b4spc", [128, 32], F32)
            sc1 = _al("b4sc1", [128, 1], F32)
            wm1 = _al("b4wm1", [128, 1], F32)
            sc2 = _al("b4sc2", [128, 1], F32)
            wm2 = _al("b4wm2", [128, 1], F32)
            r1x = _al("b4r1x", [1, 256], F32)
            identf = _al("b4identf", [128, 128], F32)
            onesf = _al("b4onesf", [128, 128], F32)
            with nc.Block() as blk:
                @blk.gpsimd
                def _(g):
                    g.memset(onesf[:, :], 1.0)
            emit_consts(nc, identf)
            lq_f32 = scr16k[:, 2048:3072]
            lq_bf = scr16k[:, 3072:3584].bitcast(BF16)
            ws = dict(sums=sums, sumsq=sumsq, scale=scale, bias=bias,
                      sq_scr=lq_bf,
                      lnf=scr16k[:, 0:2048].rearrange("pp (a b) -> pp a b", a=2),
                      lq_i32=lq_f32.bitcast(I32), lq_f32=lq_f32, lq_bf=lq_bf,
                      q1c=scr16k[:, 3584:4096].bitcast(BF16),
                      smalls=smalls,
                      q1T=q1T, tern=tern, big16=big16, scr16k=scr16k, hsp=hsp,
                      M=M, Mn=Mn, s2=s2, inv_s2=inv_s2, comb=comb, acc=acc,
                      cw_sb=cw_sb, cb_sb=cb_sb, s0c=s0c, smc=smc, spc=spc,
                      sc1=sc1, wm1=wm1, sc2=sc2, wm2=wm2,
                      S2bc=scr16k[:, 3584:3840], I2bc=None, r1x=r1x)
            emit_l1ffn(nc, "g1", xs, ident, identf, onesf, g1w1, g1w2,
                       g1cw, g1cb, hs_d, ws, lnsh=(LN, 3))

        with (nc.semaphore("xout") as xo_s, nc.Block() as blk):
            @blk.sync
            def _(s):
                s.dma_start(x_out.rearrange("(n pp) m -> pp n m", pp=128),
                            xs[:, :, :]).then_inc(xo_s, 16)
                s.wait_ge(xo_s, 16)
    return nc


def _pack_all(inputs):
    wq0, wk0, wv0, wo0, ones3 = host_l0attn_weights(
        inputs["L0_attn_Wq"], inputs["L0_attn_Wk"], inputs["L0_attn_Wv"],
        inputs["L0_attn_Wo"])
    wg0, wu0, wd0 = host_l0ffn_weights(
        inputs["L0_ffn_w_gate"], inputs["L0_ffn_w_up"], inputs["L0_ffn_w_down"])
    wq1, wk1, wv1, wo1, bq1, bk1, bv1, bo1 = host_l1attn_weights(
        inputs["L1_attn_Wq"], inputs["L1_attn_bq"], inputs["L1_attn_Wk"],
        inputs["L1_attn_bk"], inputs["L1_attn_Wv"], inputs["L1_attn_bv"],
        inputs["L1_attn_Wo"], inputs["L1_attn_bo"])
    w11, w21, cw1, cb1 = host_l1ffn_weights(
        inputs["L1_ffn_w1"], inputs["L1_ffn_conv_w"], inputs["L1_ffn_conv_b"],
        inputs["L1_ffn_w2"])
    return {"a0wq": wq0, "a0wk": wk0, "a0wv": wv0, "a0wo": wo0,
            "ones3": ones3, "f0wg": wg0, "f0wu": wu0, "f0wd": wd0,
            "a1wq": wq1, "a1wk": wk1, "a1wv": wv1, "a1wo": wo1,
            "a1bq": bq1, "a1bk": bk1, "a1bv": bv1, "a1bo": bo1,
            "g1w1": w11, "g1w2": w21, "g1cw": cw1, "g1cb": cb1}


def kernel(**inputs):
    from concourse.bass_utils import run_bass_kernel_spmd as _run
    gi = np.asarray(inputs["L1_attn_global_indices"]).reshape(-1).tolist()
    assert gi == [0, 63], f"kernel specialized to global_indices=[0,63], got {gi}"
    x = np.ascontiguousarray(inputs["L0_attn_hidden_states"], dtype=np.float32)
    W = _pack_all(inputs)
    ncore = 8
    nc = build_fused()
    maps = [dict(W, x=np.ascontiguousarray(x[i])) for i in range(ncore)]
    r = _run(nc, maps, core_ids=list(range(ncore)))
    return np.stack([np.asarray(r.results[i]["xo"]) for i in range(ncore)], axis=0)


# version 15 (final)
# speedup vs baseline: 1.0347x; 1.0253x over previous
import numpy as np
import concourse.bass as bass
import concourse.mybir as mybir

F32 = mybir.dt.float32
BF16 = mybir.dt.bfloat16
I32 = mybir.dt.int32
AF = mybir.ActivationFunctionType
ALU = mybir.AluOpType
AX = mybir.AxisListType

NT = 16
L1_SKIP_GLOBAL = False
L1_SKIP_LOCAL = False
L1_SKIP_NORM = False
L1_SKIP_GAV = False
L1_SKIP_LAV = False
L1_SKIP_ROW = False
L1_SKIP_TRANS = False
ND = 8
T = 2048
D = 1024
LN_EPS = 1e-5


def emit_consts(nc, ident, ident_f32=None):
    with nc.Block() as blk:
        @blk.gpsimd
        def _(g):
            g.memset(ident[:, :], 1.0)
            g.drain()
            g.affine_select(ident[:, :], ident[:, :], [[1, 128]], ALU.is_equal, 0.0,
                            channel_multiplier=-1)
            if ident_f32 is not None:
                g.memset(ident_f32[:, :], 1.0)
                g.drain()
                g.affine_select(ident_f32[:, :], ident_f32[:, :], [[1, 128]],
                                ALU.is_equal, 0.0, channel_multiplier=-1)


def emit_ln_stats(nc, p, xs, ws, shared=None):
    sums, sumsq, scale, bias = ws["sums"], ws["sumsq"], ws["scale"], ws["bias"]
    sq_scr = ws["sq_scr"]

    with nc.Block() as blk:
        @blk.scalar
        def _(s):
            for ti in range(NT):
                if ti:
                    s.drain()
                s.activation(sq_scr[:, :], xs[:, ti, :], AF.Square,
                             accum_out=sumsq[:, ti:ti + 1])

        @blk.vector
        def _(v):
            for ti in range(NT):
                v.reduce_sum(sums[:, ti:ti + 1], xs[:, ti, :], AX.X)

    import contextlib as _ctl
    with _ctl.ExitStack() as _e:
        if shared is None:
            c1 = _e.enter_context(nc.semaphore(p + "c1"))
            c2 = _e.enter_context(nc.semaphore(p + "c2"))
            Bc = 0
        else:
            (c1, c2) = shared[0]
            Bc = shared[1]
        with nc.Block() as blk:
            @blk.vector
            def _(v):
                iv = 1.0 / D
                v.tensor_scalar(sums[:, :], sums[:, :], iv, None, ALU.mult)
                v.tensor_scalar(sumsq[:, :], sumsq[:, :], iv, None, ALU.mult)
                v.drain()
                v.tensor_tensor(bias[:, :], sums[:, :], sums[:, :], ALU.mult)
                v.drain()
                v.scalar_tensor_tensor(scale[:, :], sumsq[:, :], LN_EPS, bias[:, :],
                                       ALU.add, ALU.subtract).then_inc(c1, 1)
                v.wait_ge(c2, Bc + 1)
                v.scalar_tensor_tensor(bias[:, :], sums[:, :], -1.0, scale[:, :],
                                       ALU.mult, ALU.mult)

            @blk.scalar
            def _(s):
                s.wait_ge(c1, Bc + 1)
                s.activation(sumsq[:, :], scale[:, :], AF.Ln)
                s.drain()
                s.activation(scale[:, :], sumsq[:, :], AF.Exp,
                             scale=-0.5).then_inc(c2, 1)


def emit_ln_to_lnT(nc, p, xs, ident, lnT, ws, shared=None):
    if shared is None:
        emit_ln_stats(nc, p + "s", xs, ws)
    else:
        emit_ln_stats(nc, p + "s", xs, ws, shared=(shared[0][0:2], shared[1]))
    scale, bias, ln_scr = ws["scale"], ws["bias"], ws["ln_scr"]
    import contextlib as _ctl
    with _ctl.ExitStack() as _e:
        if shared is None:
            ap_s = _e.enter_context(nc.semaphore(p + "ap"))
            tp_s = _e.enter_context(nc.semaphore(p + "tp"))
            ev_s = _e.enter_context(nc.semaphore(p + "ev"))
            Ba = Bt = Be = 0
        else:
            ap_s, tp_s, ev_s = shared[0][2:5]
            k = shared[1]
            Ba, Bt, Be = NT * k, 2 * NT * k, 2 * NT * k
        with (nc.psum_tensor(p + "tps", [128, 2, 4, 128], F32) as tps,
              nc.Block() as blk):
            @blk.vector
            def _(v):
                for ti in range(NT):
                    if ti >= 2:
                        v.wait_ge(tp_s, Bt + 2 * (ti - 1))
                    v.tensor_scalar(ln_scr[:, ti % 2, :], xs[:, ti, :],
                                    scale[:, ti:ti + 1], bias[:, ti:ti + 1],
                                    ALU.mult, ALU.add).then_inc(ap_s, 1)
                    if ti >= 1:
                        tj = ti - 1
                        v.wait_ge(tp_s, Bt + 2 * ti)
                        for h in range(2):
                            v.tensor_copy(
                                lnT[:, 4 * h:4 * h + 4, tj * 128:(tj + 1) * 128],
                                tps[:, h, :, :]).then_inc(ev_s, 1)
                tj = NT - 1
                v.wait_ge(tp_s, Bt + 2 * NT)
                for h in range(2):
                    v.tensor_copy(lnT[:, 4 * h:4 * h + 4, tj * 128:(tj + 1) * 128],
                                  tps[:, h, :, :]).then_inc(ev_s, 1)

            @blk.tensor
            def _(t):
                for ti in range(NT):
                    t.wait_ge(ap_s, Ba + ti + 1)
                    if ti >= 1:
                        t.wait_ge(ev_s, Be + 2 * ti)
                    for h in range(2):
                        for j in range(4):
                            dj = 4 * h + j
                            mm = t.matmul(tps[:, h, j, :],
                                          ln_scr[:, ti % 2, dj * 128:(dj + 1) * 128],
                                          ident[:, :], start=True, stop=True)
                        mm.then_inc(tp_s, 1)


def emit_l0ffn(nc, p, xs, ident, wg_d, wu_d, wd_d, ws, lnsh=None):
    lnT = ws["lnT"]
    emit_ln_to_lnT(nc, p + "ln", xs, ident, lnT, ws, shared=lnsh)

    wd_sb = ws["wd_sb"]
    wg_sb = ws["wg_sb"]
    wu_sb = ws["wu_sb"]
    hT = ws["hT"]
    silu_scr = ws["silu_scr"]

    NC_ = 24
    inv127 = float(1.0 / 127.0)
    inv127sq = float(1.0 / (127.0 * 127.0))

    with (nc.semaphore(p + "wd") as wd_s, nc.Block() as blk):
        @blk.gpsimd
        def _(g):
            for c in range(NC_):
                g.dma_start(wd_sb[:, c, :],
                            wd_d[c, :, :]).then_inc(wd_s, 16)
            g.wait_ge(wd_s, 16 * NC_)

    import contextlib as _ctl
    _es0 = _ctl.ExitStack()
    lda_s = _es0.enter_context(nc.semaphore(p + "lda"))
    ldb_s = _es0.enter_context(nc.semaphore(p + "ldb"))
    mm_s = _es0.enter_context(nc.semaphore(p + "mm"))
    si_s = _es0.enter_context(nc.semaphore(p + "si"))
    mu_s = _es0.enter_context(nc.semaphore(p + "mu"))
    dn_s = _es0.enter_context(nc.semaphore(p + "dn"))
    ev_s = _es0.enter_context(nc.semaphore(p + "ev"))
    for half in range(2):
        t0 = half * 1024
        B_ld, B_mm, B_si, B_mu = 384 * half, 48 * half, 24 * half, 24 * half
        B_dn, B_ev = 8 * half, 8 * half
        ld_par = [lda_s, ldb_s]
        with (nc.psum_tensor(p + f"gps{half}", [128, 2, 1024], F32) as gps,
              nc.psum_tensor(p + f"ups{half}", [128, 2, 1024], F32) as ups,
              nc.Block() as blk):
            @blk.gpsimd
            def _(g):
                for c in range(NC_):
                    if c >= 2:
                        g.wait_ge(mm_s, B_mm + 2 * (c - 1))
                    g.dma_start(wg_sb[:, c % 2, :, :],
                                wg_d[c, :, :].rearrange("(dc pp) ci -> pp dc ci", pp=128)
                                ).then_inc(ld_par[c % 2], 16)
                    g.dma_start(wu_sb[:, c % 2, :, :],
                                wu_d[c, :, :].rearrange("(dc pp) ci -> pp dc ci", pp=128)
                                ).then_inc(ld_par[c % 2], 16)

            @blk.tensor
            def _(t):
                for c in range(NC_):
                    t.wait_ge(ld_par[c % 2], B_ld + 32 * (c // 2 + 1))
                    if c >= 2:
                        t.wait_ge(si_s, B_si + c - 1)
                        t.wait_ge(mu_s, B_mu + c - 1)
                    for j in range(ND):
                        for n2 in range(2):
                            mm = t.matmul(gps[:, c % 2, n2 * 512:(n2 + 1) * 512],
                                          wg_sb[:, c % 2, j, :],
                                          lnT[:, j, t0 + n2 * 512:t0 + (n2 + 1) * 512],
                                          start=(j == 0), stop=(j == ND - 1))
                    mm.then_inc(mm_s, 1)
                    for j in range(ND):
                        for n2 in range(2):
                            mm = t.matmul(ups[:, c % 2, n2 * 512:(n2 + 1) * 512],
                                          wu_sb[:, c % 2, j, :],
                                          lnT[:, j, t0 + n2 * 512:t0 + (n2 + 1) * 512],
                                          start=(j == 0), stop=(j == ND - 1))
                    mm.then_inc(mm_s, 1)

            @blk.scalar
            def _(s):
                for c in range(NC_):
                    s.wait_ge(mm_s, B_mm + 2 * c + 1)
                    if c >= 1:
                        s.wait_ge(mu_s, B_mu + c)
                    s.activation(silu_scr[:, :], gps[:, c % 2, :], AF.Silu,
                                 scale=inv127).then_inc(si_s, 1)

            @blk.vector
            def _(v):
                for c in range(NC_):
                    v.wait_ge(mm_s, B_mm + 2 * c + 2)
                    v.wait_ge(si_s, B_si + c + 1)
                    v.tensor_tensor(hT[:, c, :], silu_scr[:, :],
                                    ups[:, c % 2, :], ALU.mult).then_inc(mu_s, 1)

        with (nc.psum_tensor(p + f"dps{half}", [128, 2, 1024], F32) as dps,
              nc.Block() as blk):
            @blk.tensor
            def _(t):
                for tl in range(8):
                    ti = 8 * half + tl
                    if tl >= 2:
                        t.wait_ge(ev_s, B_ev + tl - 1)
                    for c in range(NC_):
                        for n2 in range(2):
                            mm = t.matmul(dps[:, tl % 2, n2 * 512:(n2 + 1) * 512],
                                          hT[:, c, tl * 128:(tl + 1) * 128],
                                          wd_sb[:, c, n2 * 512:(n2 + 1) * 512],
                                          start=(c == 0), stop=(c == NC_ - 1))
                    mm.then_inc(dn_s, 1)

            @blk.vector
            def _(v):
                for tl in range(8):
                    ti = 8 * half + tl
                    v.wait_ge(dn_s, B_dn + tl + 1)
                    v.scalar_tensor_tensor(xs[:, ti, :], dps[:, tl % 2, :],
                                           inv127sq, xs[:, ti, :],
                                           ALU.mult, ALU.add).then_inc(ev_s, 1)
    _es0.close()


def build_l0ffn_standalone():
    nc = bass.Bass("TRN2", target_bir_lowering=False)
    x_in = nc.dram_tensor("x", [T, D], F32, kind="ExternalInput").ap()
    wg_d = nc.dram_tensor("wg", [24, 1024, 128], I32, kind="ExternalInput").ap()
    wu_d = nc.dram_tensor("wu", [24, 1024, 128], I32, kind="ExternalInput").ap()
    wd_d = nc.dram_tensor("wd", [24, 128, 1024], I32, kind="ExternalInput").ap()
    x_out = nc.dram_tensor("xo", [T, D], F32, kind="ExternalOutput").ap()

    with (
        nc.sbuf_tensor("xs", [128, NT, D], F32) as xs,
        nc.sbuf_tensor("ident", [128, 128], BF16) as ident,
        nc.sbuf_tensor("lnT", [128, ND, T], BF16) as lnT,
        nc.sbuf_tensor("wd_sb", [128, 24, 1024], BF16) as wd_sb,
        nc.sbuf_tensor("wg_sb", [128, 2, 8, 128], BF16) as wg_sb,
        nc.sbuf_tensor("wu_sb", [128, 2, 8, 128], BF16) as wu_sb,
        nc.sbuf_tensor("hT", [128, 24, 1024], BF16) as hT,
        nc.sbuf_tensor("silu_scr", [128, 1024], BF16) as silu_scr,
        nc.sbuf_tensor("sums", [128, NT], F32) as sums,
        nc.sbuf_tensor("sumsq", [128, NT], F32) as sumsq,
        nc.sbuf_tensor("scale", [128, NT], F32) as scale,
        nc.sbuf_tensor("bias", [128, NT], F32) as bias,
    ):
        sq_scr = silu_scr
        ln_scr = hT[:, 0:2, :].rearrange("p a b -> p a b") if False else None
        ws = dict(lnT=lnT, wd_sb=wd_sb, wg_sb=wg_sb, wu_sb=wu_sb, hT=hT,
                  silu_scr=silu_scr, sums=sums, sumsq=sumsq,
                  scale=scale, bias=bias, sq_scr=sq_scr, ln_scr=hT)
        with (nc.semaphore("xin") as xin_s, nc.Block() as blk):
            @blk.sync
            def _(s):
                s.dma_start(xs[:, :, :],
                            x_in.rearrange("(n pp) m -> pp n m", pp=128)
                            ).then_inc(xin_s, 16)
                s.wait_ge(xin_s, 16)
        emit_consts(nc, ident)
        emit_l0ffn(nc, "f0", xs, ident, wg_d, wu_d, wd_d, ws)
        with (nc.semaphore("xout") as xo_s, nc.Block() as blk):
            @blk.sync
            def _(s):
                s.dma_start(x_out.rearrange("(n pp) m -> pp n m", pp=128),
                            xs[:, :, :]).then_inc(xo_s, 16)
                s.wait_ge(xo_s, 16)
    return nc


def host_l0ffn_weights(w_gate, w_up, w_down):
    wg = np.ascontiguousarray(
        w_gate.T.reshape(1024, 24, 128).transpose(1, 0, 2))
    wu = np.ascontiguousarray(
        w_up.T.reshape(1024, 24, 128).transpose(1, 0, 2))
    wd = np.ascontiguousarray(
        w_down.T.reshape(24, 128, 1024))
    return wg, wu, wd


def emit_tern(nc, p, w_d, ntiles, W, tern, wtile, scr_f32, acc, smalls,
              onesf, scale_bc, wmean_bc, shared=None):
    count = float(ntiles * 128 * W)
    import contextlib as _ctl
    _et = _ctl.ExitStack()
    if shared is None:
        l1a = _et.enter_context(nc.semaphore(p + "l1a"))
        l1b = _et.enter_context(nc.semaphore(p + "l1b"))
        r1 = _et.enter_context(nc.semaphore(p + "r1"))
        t1 = _et.enter_context(nc.semaphore(p + "t1"))
        t2 = _et.enter_context(nc.semaphore(p + "t2"))
        l3a = _et.enter_context(nc.semaphore(p + "l3a"))
        l3b = _et.enter_context(nc.semaphore(p + "l3b"))
        r3 = _et.enter_context(nc.semaphore(p + "r3"))
        Bl1a = Bl1b = Br1 = Bt1 = Bt2 = Bl3a = Bl3b = Br3 = 0
    else:
        (l1a, l1b, r1, t1, t2, l3a, l3b, r3) = shared[0]
        (Bl1a, Bl1b, Br1, Bt1, Bt2, Bl3a, Bl3b, Br3) = shared[1]
    with nc.Block() as blk:
        ls1 = [(l1a, Bl1a), (l1b, Bl1b)]
        @blk.gpsimd
        def _(g):
            for i in range(ntiles):
                if i >= 2:
                    g.wait_ge(r1, Br1 + i - 1)
                g.dma_start(wtile[:, i % 2, 0:W],
                            w_d[i, :, :]).then_inc(ls1[i % 2][0], 16)

        @blk.vector
        def _(v):
            for i in range(ntiles):
                v.wait_ge(ls1[i % 2][0], ls1[i % 2][1] + 16 * (i // 2 + 1))
                v.tensor_reduce(acc[:, i:i + 1], wtile[:, i % 2, 0:W], AX.X,
                                ALU.add, apply_absolute_value=True).then_inc(r1, 1)
    with (nc.psum_tensor(p + "tp1", [128, 8], F32) as tp1,
          nc.Block() as blk):
        @blk.vector
        def _(v):
            v.tensor_reduce(smalls[:, 0:1], acc[:, 0:ntiles], AX.X, ALU.add)
            v.drain()
            v.nop().then_inc(t1, 1)
            v.wait_ge(t2, Bt2 + 1)
            v.tensor_copy(smalls[0:1, 1:2], tp1[0:1, 1:2])
            v.drain()
            v.tensor_scalar(smalls[0:1, 2:3], smalls[0:1, 1:2], 1.0 / count, None,
                            ALU.mult)
            v.drain()
            v.tensor_scalar(smalls[0:1, 3:4], smalls[0:1, 2:3], 1e-5, None, ALU.max)
            v.drain()
            v.reciprocal(smalls[0:1, 4:5], smalls[0:1, 3:4])
            v.drain()
            v.nop().then_inc(t1, 2)
            v.wait_ge(t2, Bt2 + 2)
            v.tensor_copy(wmean_bc[:, 0:1], tp1[:, 2:3])
            v.tensor_copy(scale_bc[:, 0:1], tp1[:, 3:4])

        @blk.tensor
        def _(t):
            t.wait_ge(t1, Bt1 + 1)
            t.matmul(tp1[0:1, 1:2], smalls[:, 0:1], onesf[:, 0:1],
                     start=True, stop=True).then_inc(t2, 1)
            t.wait_ge(t1, Bt1 + 3)
            t.matmul(tp1[:, 2:3], onesf[0:1, :], smalls[0:1, 3:4],
                     start=True, stop=True)
            mm = t.matmul(tp1[:, 3:4], onesf[0:1, :], smalls[0:1, 4:5],
                          start=True, stop=True)
            mm.then_inc(t2, 1)
    with nc.Block() as blk:
        ls3 = [(l3a, Bl3a), (l3b, Bl3b)]
        @blk.gpsimd
        def _(g):
            for i in range(ntiles):
                if i >= 2:
                    g.wait_ge(r3, Br3 + i - 1)
                g.dma_start(wtile[:, i % 2, 0:W],
                            w_d[i, :, :]).then_inc(ls3[i % 2][0], 16)

        @blk.vector
        def _(v):
            scr_i32 = scr_f32.bitcast(I32)
            for i in range(ntiles):
                v.wait_ge(ls3[i % 2][0], ls3[i % 2][1] + 16 * (i // 2 + 1))
                v.tensor_scalar(scr_f32[:, 0:W], wtile[:, i % 2, 0:W],
                                scale_bc[:, 0:1], None, ALU.mult)
                v.drain()
                v.tensor_copy(scr_i32[:, 0:W], scr_f32[:, 0:W])
                v.drain()
                v.tensor_copy(wtile[:, i % 2, 0:W], scr_i32[:, 0:W])
                v.drain()
                v.tensor_scalar(tern[:, i, 0:W], wtile[:, i % 2, 0:W],
                                1.0, -1.0, ALU.min, ALU.max).then_inc(r3, 1)


def emit_q1_to_q1T(nc, p, xs, ident, q1T, ws, shared=None):
    if shared is None:
        emit_ln_stats(nc, p + "s", xs, ws)
    else:
        emit_ln_stats(nc, p + "s", xs, ws, shared=(shared[0][0:2], shared[1]))
    scale, bias = ws["scale"], ws["bias"]
    lnf, lq_i32, lq_bf, q1c, smalls = (ws["lnf"], ws["lq_i32"], ws["lq_bf"],
                                       ws["q1c"], ws["smalls"])
    lq_f32 = ws["lq_f32"]
    import contextlib as _ctl
    _eq = _ctl.ExitStack()
    if shared is None:
        ap_s = _eq.enter_context(nc.semaphore(p + "ap"))
        tp_s = _eq.enter_context(nc.semaphore(p + "tp"))
        ev_s = _eq.enter_context(nc.semaphore(p + "ev"))
        Bqa = Bqt = Bqe = 0
    else:
        ap_s, tp_s, ev_s = shared[0][2:5]
        _k = shared[1]
        Bqa, Bqt, Bqe = NT * _k, 2 * NT * _k, 2 * NT * _k
    qa_s = _eq.enter_context(nc.semaphore(p + "qa"))
    qb_s = _eq.enter_context(nc.semaphore(p + "qb"))
    with (_eq, nc.psum_tensor(p + "tps", [128, 2, 4, 128], F32) as tps,
          nc.Block() as blk):
        @blk.vector
        def _(v):
            for ti in range(NT):
                if ti >= 2:
                    v.wait_ge(qb_s, ti - 1)
                v.tensor_scalar(lnf[:, ti % 2, :], xs[:, ti, :],
                                scale[:, ti:ti + 1], bias[:, ti:ti + 1],
                                ALU.mult, ALU.add)
                v.drain()
                v.tensor_reduce(smalls[:, 0:1], lnf[:, ti % 2, :], AX.X,
                                ALU.max, apply_absolute_value=True)
                v.drain()
                v.tensor_scalar(smalls[:, 1:2], smalls[:, 0:1], 1e-5, None, ALU.max)
                v.drain()
                v.tensor_scalar(smalls[:, 2:3], smalls[:, 1:2], 1.0 / 127.0,
                                None, ALU.mult)
                v.reciprocal(smalls[:, 3:4], smalls[:, 1:2])
                v.drain()
                v.tensor_scalar(smalls[:, 4:5], smalls[:, 3:4], 127.0, None,
                                ALU.mult).then_inc(qa_s, 1)
                v.wait_ge(qb_s, ti + 1)
                v.tensor_copy(lq_bf[:, :], lq_i32[:, :])
                if ti >= 1:
                    tj = ti - 1
                    v.wait_ge(tp_s, Bqt + 2 * ti)
                    for h in range(2):
                        v.tensor_copy(q1T[:, 4 * h:4 * h + 4, tj * 128:(tj + 1) * 128],
                                      tps[:, h, :, :]).then_inc(ev_s, 1)
                v.drain()
                v.tensor_scalar(q1c[:, :], lq_bf[:, :], smalls[:, 2:3],
                                None, ALU.mult).then_inc(ap_s, 1)
            tj = NT - 1
            v.wait_ge(tp_s, Bqt + 2 * NT)
            for h in range(2):
                v.tensor_copy(q1T[:, 4 * h:4 * h + 4, tj * 128:(tj + 1) * 128],
                              tps[:, h, :, :]).then_inc(ev_s, 1)

        @blk.scalar
        def _(s):
            for ti in range(NT):
                s.wait_ge(qa_s, ti + 1)
                s.activation(lq_i32[:, :], lnf[:, ti % 2, :], AF.Identity,
                             scale=smalls[:, 4:5]).then_inc(qb_s, 1)

        @blk.tensor
        def _(t):
            for ti in range(NT):
                t.wait_ge(ap_s, Bqa + ti + 1)
                if ti >= 1:
                    t.wait_ge(ev_s, Bqe + 2 * ti)
                for h in range(2):
                    for j in range(4):
                        dj = 4 * h + j
                        mm = t.matmul(tps[:, h, j, :],
                                      q1c[:, dj * 128:(dj + 1) * 128],
                                      ident[:, :], start=True, stop=True)
                    mm.then_inc(tp_s, 1)


def emit_l1ffn(nc, p, xs, ident, identf, onesf, w1_d, w2_d, convw_d, convb_d,
               hs_d, ws, lnsh=None):
    q1T = ws["q1T"]
    tern = ws["tern"]
    t1v = tern.rearrange("pp (a b) -> pp a b", a=8)
    t2v = tern.rearrange("pp (a b) -> pp a b", a=32)
    big16 = ws["big16"]
    wtile = big16.rearrange("pp (a b) -> pp a b", a=2)
    hs_sb = big16[:, 0:8192].rearrange("pp (a b) -> pp a b", a=32)
    scr = ws["scr16k"]
    acc, smalls = ws["acc"], ws["smalls"]
    M = ws["M"]
    s2, inv_s2, comb = ws["s2"], ws["inv_s2"], ws["comb"]
    cw_sb, cb_sb = ws["cw_sb"], ws["cb_sb"]
    s0c, smc, spc = ws["s0c"], ws["smc"], ws["spc"]
    sc1, wm1, sc2, wm2 = ws["sc1"], ws["wm1"], ws["sc2"], ws["wm2"]
    S2bc, I2bc = ws["S2bc"], ws["I2bc"]
    r1x = ws["r1x"]

    NC_ = 32

    import contextlib as _ctl
    _etn = _ctl.ExitStack()
    tsems = tuple(_etn.enter_context(nc.semaphore(p + "tn" + s))
                  for s in ("la", "lb", "r1", "t1", "t2", "lc", "ld", "r3"))
    tbase1 = (0, 0, 0, 0, 0, 0, 0, 0)
    tbase2 = (64, 64, 8, 3, 2, 64, 64, 8)

    emit_q1_to_q1T(nc, p + "q", xs, ident, q1T, ws, shared=lnsh)

    with (nc.semaphore(p + "cw") as cw_s, nc.Block() as blk):
        @blk.sync
        def _(s):
            s.dma_start(cw_sb[:, :, :], convw_d).then_inc(cw_s, 16)
            s.dma_start(cb_sb[:, :], convb_d).then_inc(cw_s, 16)
            s.wait_ge(cw_s, 32)
    emit_tern(nc, p + "t1", w1_d, 8, 4096, t1v, wtile,
              scr, acc, smalls, onesf, sc1, wm1, shared=(tsems, tbase1))
    with nc.Block() as blk:
        @blk.vector
        def _(v):
            for k, dst in ((0, smc), (1, s0c), (2, spc)):
                v.tensor_scalar(dst[:, :], cw_sb[:, :, k], wm1[:, 0:1],
                                None, ALU.mult)
        @blk.gpsimd
        def _(g):
            g.memset(M[:, :], -1e30)
            g.memset(ws["Mn"][:, :], 1e30)

    a2 = scr.rearrange("pp (a b) -> pp a b", a=2)
    hsp = ws["hsp"]
    Mn = ws["Mn"]
    with (nc.semaphore(p + "cm") as cm_s, nc.semaphore(p + "c1") as c1_s,
          nc.semaphore(p + "c3") as c3_s, nc.semaphore(p + "cs") as cs_s,
          nc.semaphore(p + "cg") as cg_s, nc.semaphore(p + "ch") as ch_s,
          nc.semaphore(p + "c5") as cm2_s,
          nc.psum_tensor(p + "hp", [128, 2, 2048], F32) as hp,
          nc.Block() as blk):
        @blk.tensor
        def _(t):
            for c in range(NC_):
                if c >= 2:
                    t.wait_ge(c3_s, c - 1)
                    t.wait_ge(cs_s, c - 1)
                for j in range(ND):
                    for n4 in range(4):
                        mm = t.matmul(hp[:, c % 2, n4 * 512:(n4 + 1) * 512],
                                      t1v[:, j, c * 128:(c + 1) * 128],
                                      q1T[:, j, n4 * 512:(n4 + 1) * 512],
                                      start=(j == 0), stop=(j == ND - 1))
                mm.then_inc(cm_s, 1)

        @blk.scalar
        def _(s):
            for c in range(NC_):
                s.wait_ge(cm_s, c + 1)
                if c >= 2:
                    s.wait_ge(c3_s, c - 1)
                s.activation(a2[:, c % 2, :], hp[:, c % 2, :], AF.Identity,
                             scale=s0c[:, c:c + 1],
                             bias=cb_sb[:, c:c + 1]).then_inc(c1_s, 1)
                if c >= 1:
                    s.wait_ge(c3_s, c)
                    s.wait_ge(cg_s, max(0, c - 1))
                    s.activation(hsp[:, :], a2[:, (c - 1) % 2, :],
                                 AF.Silu).then_inc(cs_s, 1)
            s.wait_ge(c3_s, NC_)
            s.wait_ge(cg_s, NC_ - 1)
            s.activation(hsp[:, :], a2[:, (NC_ - 1) % 2, :],
                         AF.Silu).then_inc(cs_s, 1)

        @blk.vector
        def _(v):
            for c in range(NC_):
                v.wait_ge(c1_s, c + 1)
                v.scalar_tensor_tensor(a2[:, c % 2, 1:2048], hp[:, c % 2, 0:2047],
                                       smc[:, c:c + 1], a2[:, c % 2, 1:2048],
                                       ALU.mult, ALU.add)
                v.drain()
                v.scalar_tensor_tensor(a2[:, c % 2, 0:2047], hp[:, c % 2, 1:2048],
                                       spc[:, c:c + 1], a2[:, c % 2, 0:2047],
                                       ALU.mult, ALU.add).then_inc(c3_s, 1)
                v.wait_ge(cs_s, c + 1)
                v.tensor_tensor(M[:, :], M[:, :], hsp[:, :], ALU.max)
                v.drain()
                v.tensor_tensor(Mn[:, :], Mn[:, :], hsp[:, :], ALU.min
                                ).then_inc(cm2_s, 1)
                v.drain()

        @blk.gpsimd
        def _(g):
            for c in range(NC_):
                g.wait_ge(cs_s, c + 1)
                g.dma_start(hs_d[c, :, :], hsp[:, :]).then_inc(ch_s, 16)
                g.wait_ge(ch_s, 16 * (c + 1))
                g.wait_ge(cm2_s, c + 1)
                g.nop().then_inc(cg_s, 1)

    with nc.Block() as blk:
        @blk.vector
        def _(v):
            v.tensor_scalar(Mn[:, :], Mn[:, :], -1.0, None, ALU.mult)
            v.drain()
            v.tensor_tensor(M[:, :], M[:, :], Mn[:, :], ALU.max)
    with (nc.semaphore(p + "dm") as dm_s, nc.semaphore(p + "dv") as dv_s,
          nc.psum_tensor(p + "mt", [128, 2, 512], F32) as mt,
          nc.Block() as blk):
        @blk.tensor
        def _(t):
            for sb in range(4):
                if sb >= 2:
                    t.wait_ge(dv_s, sb - 1)
                for t4 in range(4):
                    tc = 4 * sb + t4
                    mm = t.matmul(mt[:, sb % 2, 128 * t4:128 * (t4 + 1)],
                                  M[:, tc * 128:(tc + 1) * 128],
                                  ident[:, :], start=True, stop=True)
                mm.then_inc(dm_s, 1)

        @blk.vector
        def _(v):
            for sb in range(4):
                cb = 4 * (sb % 2)
                v.wait_ge(dm_s, sb + 1)
                for t4 in range(4):
                    v.tensor_reduce(smalls[:, cb + t4:cb + t4 + 1],
                                    mt[:, sb % 2, 128 * t4:128 * (t4 + 1)],
                                    AX.X, ALU.max)
                v.drain()
                for t4 in range(4):
                    tc = 4 * sb + t4
                    v.tensor_scalar(s2[:, tc:tc + 1],
                                    smalls[:, cb + t4:cb + t4 + 1],
                                    1e-5, None, ALU.max)
                v.nop().then_inc(dv_s, 1)
            v.drain()
            v.tensor_scalar(inv_s2[:, :], s2[:, :], 1.0 / 127.0, None, ALU.mult)
            v.drain()
            v.reciprocal(s2[:, :], s2[:, :])
            v.drain()
            v.tensor_scalar(s2[:, :], s2[:, :], 127.0, None, ALU.mult)

    emit_tern(nc, p + "t2", w2_d, 32, 1024, t2v, wtile,
              scr, acc, smalls, onesf, sc2, wm2, shared=(tsems, tbase2))
    with nc.Block() as blk:
        @blk.vector
        def _(v):
            v.tensor_scalar(comb[:, :], inv_s2[:, :], wm2[:, 0:1], None, ALU.mult)

    i32p = scr.bitcast(I32).rearrange("pp (a b) -> pp a b", a=2)
    import contextlib as _ctl
    _esf = _ctl.ExitStack()
    fl_s = _esf.enter_context(nc.semaphore(p + "fl"))
    fa_s = _esf.enter_context(nc.semaphore(p + "fa"))
    fb_s = _esf.enter_context(nc.semaphore(p + "fb"))
    fc_s = _esf.enter_context(nc.semaphore(p + "fc"))
    fm_s = _esf.enter_context(nc.semaphore(p + "fm"))
    fr_s = _esf.enter_context(nc.semaphore(p + "fr"))
    for e in range(8):
        Bl, Ba, Bb, Bc, Bm, Br = 128 * e, 2 * e, e, 32 * e, 2 * e, 2 * e
        with (nc.psum_tensor(p + f"f{e}p", [128, 2, 1024], F32) as dps,
              nc.psum_tensor(p + f"f{e}s", [128, 4, 256], F32) as sps,
              nc.Block() as blk):
            @blk.sync
            def _(s):
                for cg in range(8):
                    s.dma_start(
                        hs_sb[:, cg * 4:(cg + 1) * 4, :],
                        hs_d[cg * 4:(cg + 1) * 4, :, e * 256:(e + 1) * 256]
                        .rearrange("c pp t -> pp c t")).then_inc(fl_s, 16)

            @blk.tensor
            def _(t):
                for k in range(2):
                    t.matmul(sps[0:1, 0, k * 128:(k + 1) * 128],
                             s2[:, 2 * e + k:2 * e + k + 1], identf[:, :],
                             start=True, stop=True)
                for k in range(2):
                    mm = t.matmul(sps[0:1, 1, k * 128:(k + 1) * 128],
                                  inv_s2[:, 2 * e + k:2 * e + k + 1], identf[:, :],
                                  start=True, stop=True)
                mm.then_inc(fa_s, 1)
                t.wait_ge(fb_s, Bb + 1)
                t.matmul(sps[:, 2, :], onesf[0:1, :], r1x[0:1, 0:256],
                         start=True, stop=True).then_inc(fa_s, 1)
                for tc in range(2):
                    if tc >= 1:
                        t.wait_ge(fr_s, Br + tc)
                    for c in range(NC_):
                        if tc == 0:
                            t.wait_ge(fc_s, Bc + c + 1)
                        for n2 in range(2):
                            mm = t.matmul(dps[:, tc % 2, n2 * 512:(n2 + 1) * 512],
                                          hs_sb[:, c, tc * 128:(tc + 1) * 128],
                                          t2v[:, c, n2 * 512:(n2 + 1) * 512],
                                          start=(c == 0), stop=(c == NC_ - 1))
                    mm.then_inc(fm_s, 1)

            @blk.vector
            def _(v):
                v.wait_ge(fa_s, Ba + 1)
                v.tensor_copy(r1x[0:1, 0:256], sps[0:1, 0, 0:256])
                v.drain()
                v.nop().then_inc(fb_s, 1)
                v.wait_ge(fa_s, Ba + 2)
                v.tensor_copy(S2bc[:, :], sps[:, 2, :])
                v.drain()
                v.wait_ge(fl_s, Bl + 16 * 8)
                for c in range(NC_):
                    v.tensor_tensor(i32p[:, c % 2, 0:256], hs_sb[:, c, :],
                                    S2bc[:, :], ALU.mult)
                    v.drain()
                    v.tensor_copy(hs_sb[:, c, :], i32p[:, c % 2, 0:256]
                                  ).then_inc(fc_s, 1)
                for tc in range(2):
                    ti = 2 * e + tc
                    v.wait_ge(fm_s, Bm + tc + 1)
                    v.scalar_tensor_tensor(xs[:, ti, :], dps[:, tc % 2, :],
                                           comb[:, ti:ti + 1], xs[:, ti, :],
                                           ALU.mult, ALU.add).then_inc(fr_s, 1)


def build_l1ffn_standalone():
    nc = bass.Bass("TRN2", target_bir_lowering=False)
    x_in = nc.dram_tensor("x", [T, D], F32, kind="ExternalInput").ap()
    w1_d = nc.dram_tensor("w1", [8, 128, 4096], F32, kind="ExternalInput").ap()
    w2_d = nc.dram_tensor("w2", [32, 128, 1024], F32, kind="ExternalInput").ap()
    convw_d = nc.dram_tensor("cw", [128, 32, 3], F32, kind="ExternalInput").ap()
    convb_d = nc.dram_tensor("cb", [128, 32], F32, kind="ExternalInput").ap()
    hs_d = nc.dram_tensor("hs_scr", [32, 128, 2048], BF16).ap()
    x_out = nc.dram_tensor("xo", [T, D], F32, kind="ExternalOutput").ap()

    import contextlib
    with contextlib.ExitStack() as _es:
        def _al(name, shape, dt):
            return _es.enter_context(nc.sbuf_tensor(name, shape, dt))
        xs = _al("xs", [128, NT, D], F32)
        ident = _al("ident", [128, 128], BF16)
        identf = _al("identf", [128, 128], F32)
        onesf = _al("onesf", [128, 128], F32)
        q1T = _al("q1T", [128, ND, T], BF16)
        tern = _al("tern", [128, 32768], BF16)
        big16 = _al("big16", [128, 8192], BF16)
        scr16k = _al("scr16k", [128, 4096], F32)
        hsp = _al("hsp", [128, 2048], BF16)
        M = _al("Mt", [128, 2048], BF16)
        Mn = _al("Mn", [128, 2048], BF16)
        sums = _al("sums", [128, NT], F32)
        sumsq = _al("sumsq", [128, NT], F32)
        scale = _al("scale", [128, NT], F32)
        bias = _al("bias", [128, NT], F32)
        s2 = _al("s2", [128, NT], F32)
        inv_s2 = _al("inv_s2", [128, NT], F32)
        comb = _al("comb", [128, NT], F32)
        acc = _al("acc", [128, 32], F32)
        smalls = _al("smalls", [128, 8], F32)
        cw_sb = _al("cw_sb", [128, 32, 3], F32)
        cb_sb = _al("cb_sb", [128, 32], F32)
        s0c = _al("s0c", [128, 32], F32)
        smc = _al("smc", [128, 32], F32)
        spc = _al("spc", [128, 32], F32)
        sc1 = _al("sc1", [128, 1], F32)
        wm1 = _al("wm1", [128, 1], F32)
        sc2 = _al("sc2", [128, 1], F32)
        wm2 = _al("wm2", [128, 1], F32)
        S2bc = scr16k[:, 3584:3840]
        r1x = _al("r1x", [1, 256], F32)
        lnf = scr16k[:, 0:2048].rearrange("pp (a b) -> pp a b", a=2)
        lq_f32 = scr16k[:, 2048:3072]
        lq_i32 = lq_f32.bitcast(I32)
        lq_bf = scr16k[:, 3072:3584].bitcast(BF16)
        q1c = scr16k[:, 3584:4096].bitcast(BF16)
        ws = dict(sums=sums, sumsq=sumsq, scale=scale, bias=bias, sq_scr=lq_bf,
                  lnf=lnf, lq_i32=lq_i32, lq_f32=lq_f32, lq_bf=lq_bf, q1c=q1c,
                  smalls=smalls,
                  q1T=q1T, tern=tern, big16=big16, scr16k=scr16k, hsp=hsp,
                  M=M, Mn=Mn, s2=s2, inv_s2=inv_s2, comb=comb, acc=acc,
                  cw_sb=cw_sb, cb_sb=cb_sb, s0c=s0c, smc=smc, spc=spc,
                  sc1=sc1, wm1=wm1, sc2=sc2, wm2=wm2, S2bc=S2bc, I2bc=None,
                  r1x=r1x)
        with (nc.semaphore("xin") as xin_s, nc.Block() as blk):
            @blk.sync
            def _(s):
                s.dma_start(xs[:, :, :],
                            x_in.rearrange("(n pp) m -> pp n m", pp=128)
                            ).then_inc(xin_s, 16)
                s.wait_ge(xin_s, 16)
            @blk.gpsimd
            def _(g):
                g.memset(onesf[:, :], 1.0)
        emit_consts(nc, ident, identf)
        emit_l1ffn(nc, "g1", xs, ident, identf, onesf, w1_d, w2_d,
                   convw_d, convb_d, hs_d, ws)
        with (nc.semaphore("xout") as xo_s, nc.Block() as blk):
            @blk.sync
            def _(s):
                s.dma_start(x_out.rearrange("(n pp) m -> pp n m", pp=128),
                            xs[:, :, :]).then_inc(xo_s, 16)
                s.wait_ge(xo_s, 16)
    return nc


def host_l1ffn_weights(w1, conv_w, conv_b, w2):
    w1p = np.ascontiguousarray(w1.T.reshape(8, 128, 4096))
    w2p = np.ascontiguousarray(w2.T.reshape(32, 128, 1024))
    cwp = np.ascontiguousarray(
        conv_w.reshape(4096, 3).reshape(32, 128, 3).transpose(1, 0, 2))
    cbp = np.ascontiguousarray(conv_b.reshape(32, 128).T)
    return w1p, w2p, cwp, cbp



def _l1_window_pieces(j):
    blocks = sorted({(j - 1) % 32, j, (j + 1) % 32})
    out = []
    for b in blocks:
        s, l = 64 * b, 64
        if out and out[-1][0] + out[-1][1] == s:
            out[-1] = (out[-1][0], out[-1][1] + l)
        else:
            out.append((s, l))
    return out


def emit_l1attn(nc, p, xs, ident, onesb, wq_d, wk_d, wv_d, wo_d,
                bq_d, bk_d, brow_d, ws, stop=None, lnsh=None):
    lnT = ws["lnT"]
    qT, kT = ws["qT"], ws["kT"]
    v_aug = ws["v_aug"]
    wbuf = ws["wbuf"]
    scr9 = ws["scr9"]
    vT_scr = scr9[:, 0:4096].rearrange("pp (a b) -> pp a b", a=2)
    pl = scr9[:, 0:1536].rearrange("pp (a b) -> pp a b", a=4)
    pg = scr9[:, 0:1024].rearrange("pp (a b) -> pp a b", a=2)
    pr = scr9[:, 0:128].rearrange("pp (a b) -> pp a b", a=2)
    y_tok = scr9[:, 1536:3584].rearrange("pp (a b) -> pp a b", a=16)
    ytmp = scr9[:, 3584:4096].rearrange("pp (a b c) -> pp a b c", a=4, b=2)
    wb = ws["wbuf"]
    wbuf = wb[:, 0:2048].rearrange("pp (a b) -> pp a b", a=2)
    ysb = wb[:, 0:1040].rearrange("pp (a b c d) -> pp a b c d", a=4, b=2, c=2)
    gvp = ws["gvp"]
    bq_sb, bk_sb = ws["bq_sb"], ws["bk_sb"]
    smalls = ws["smalls"]
    zsc = smalls[:, 8:24]
    yT = lnT

    emit_ln_to_lnT(nc, p + "ln", xs, ident, lnT, ws, shared=lnsh)

    with (nc.semaphore(p + "b") as b_s, nc.Block() as blk):
        @blk.sync
        def _(s):
            s.dma_start(bq_sb[:, :], bq_d).then_inc(b_s, 16)
            s.dma_start(bk_sb[:, :], bk_d).then_inc(b_s, 16)
            s.dma_start(ws["bv128"][:, :], ws["bv128_d"]).then_inc(b_s, 16)
            s.wait_ge(b_s, 48)
        @blk.gpsimd
        def _(g):
            vh = v_aug.rearrange("pp tc (h e) -> pp tc h e", e=65)
            g.memset(vh[:, :, :, 64:65], 1.0)
            g.drain()
            g.memset(wb[0:1, 2048:2176], 1.0)

    if stop == "ln":
        return
    for which, w_d, dst, b_sb in (("q", wq_d, qT, bq_sb), ("k", wk_d, kT, bk_sb)):
        with (nc.semaphore(p + which + "w") as w_s,
              nc.semaphore(p + which + "m") as m_s,
              nc.semaphore(p + which + "e") as e_s,
              nc.psum_tensor(p + which + "ps", [128, 2, 2048], F32) as pps,
              nc.Block() as blk):
            @blk.gpsimd
            def _(g):
                for ocp in range(4):
                    for wave in range(4):
                        gw = 4 * ocp + wave
                        if gw >= 1:
                            g.wait_ge(m_s, gw)
                        for dj in range(2):
                            d = 2 * wave + dj
                            g.dma_start(wbuf[:, dj, 0:256],
                                        w_d[d * 128:(d + 1) * 128,
                                            256 * ocp:256 * (ocp + 1)]
                                        ).then_inc(w_s, 16)

            @blk.tensor
            def _(t):
                for ocp in range(4):
                    if ocp >= 1:
                        t.wait_ge(e_s, 2 * ocp)
                    for wave in range(4):
                        gw = 4 * ocp + wave
                        t.wait_ge(w_s, 32 * (gw + 1))
                        for dj in range(2):
                            d = 2 * wave + dj
                            for oc2 in range(2):
                                for n4 in range(4):
                                    mm = t.matmul(
                                        pps[:, oc2, n4 * 512:(n4 + 1) * 512],
                                        wbuf[:, dj, oc2 * 128:(oc2 + 1) * 128],
                                        lnT[:, d, n4 * 512:(n4 + 1) * 512],
                                        start=(d == 0), stop=(d == 7))
                        mm.then_inc(m_s, 1)

            @blk.vector
            def _(v):
                for ocp in range(4):
                    v.wait_ge(m_s, 4 * ocp + 4)
                    for oc2 in range(2):
                        oc = 2 * ocp + oc2
                        v.tensor_scalar(dst[:, oc, :], pps[:, oc2, :],
                                        b_sb[:, oc:oc + 1], None,
                                        ALU.add).then_inc(e_s, 1)

    if stop == "qk":
        return
    import contextlib as _ctl
    _esv = _ctl.ExitStack()
    vw_s = _esv.enter_context(nc.semaphore(p + "vw"))
    vm_s = _esv.enter_context(nc.semaphore(p + "vm"))
    vt_s = _esv.enter_context(nc.semaphore(p + "vt"))
    vv_s = _esv.enter_context(nc.semaphore(p + "vv"))
    for ocp in range(4):
        Bw, Bm, Bt, Bv = 128 * ocp, 2 * ocp, 8 * ocp, 8 * ocp
        w_s, m_s = vw_s, vm_s
        with (nc.psum_tensor(p + f"vps{ocp}", [128, 2, 2048], F32) as pps,
              nc.Block() as blk):
            @blk.gpsimd
            def _(g):
                for wave in range(4):
                    if wave >= 1:
                        g.wait_ge(m_s, 4 * ocp + wave)
                    for dj in range(2):
                        d = 2 * wave + dj
                        g.dma_start(wbuf[:, dj, 0:256],
                                    wv_d[d * 128:(d + 1) * 128,
                                         256 * ocp:256 * (ocp + 1)]
                                    ).then_inc(w_s, 16)

            @blk.tensor
            def _(t):
                for wave in range(4):
                    t.wait_ge(w_s, 128 * ocp + 32 * (wave + 1))
                    for dj in range(2):
                        d = 2 * wave + dj
                        for oc2 in range(2):
                            for n4 in range(4):
                                mm = t.matmul(
                                    pps[:, oc2, n4 * 512:(n4 + 1) * 512],
                                    wbuf[:, dj, oc2 * 128:(oc2 + 1) * 128],
                                    lnT[:, d, n4 * 512:(n4 + 1) * 512],
                                    start=(d == 0), stop=(d == 7))
                    mm.then_inc(m_s, 1)

            @blk.vector
            def _(v):
                v.wait_ge(m_s, 4 * ocp + 4)
                for oc2 in range(2):
                    oc = 2 * ocp + oc2
                    v.tensor_scalar(vT_scr[:, oc2, :], pps[:, oc2, :],
                                    ws["bv128"][:, oc:oc + 1], None, ALU.add)
        with (nc.psum_tensor(p + f"vtp{ocp}", [128, 2, 512], F32) as tps,
              nc.Block() as blk):
            @blk.tensor
            def _(t):
                for i in range(8):
                    oc2, s4 = i // 4, i % 4
                    if i >= 2:
                        t.wait_ge(vv_s, Bv + i - 1)
                    for t4 in range(4):
                        tc = 4 * s4 + t4
                        mm = t.matmul(tps[:, i % 2, 128 * t4:128 * (t4 + 1)],
                                      vT_scr[:, oc2, tc * 128:(tc + 1) * 128],
                                      ident[:, :], start=True, stop=True)
                    mm.then_inc(vt_s, 1)

            @blk.vector
            def _(v):
                for i in range(8):
                    oc2, s4 = i // 4, i % 4
                    oc = 2 * ocp + oc2
                    v.wait_ge(vt_s, Bt + i + 1)
                    dst = v_aug.rearrange("pp tc (h e) -> pp tc h e", e=65)
                    v.tensor_copy(
                        dst[:, 4 * s4:4 * s4 + 4, 2 * oc:2 * oc + 2, 0:64],
                        tps[:, i % 2, :].rearrange(
                            "pp (tc h e) -> pp tc h e", tc=4, h=2)
                        ).then_inc(vv_s, 1)

    _esv.close()
    if stop == "proj":
        return
    with nc.Block() as blk:
        @blk.vector
        def _(v):
            v.tensor_copy(gvp[0:64, :], v_aug[0:64, 0, :])
            v.tensor_copy(gvp[64:128, :], v_aug[64:128, 15, :])
    import contextlib as _ctl
    _esc = _ctl.ExitStack()
    sl_s = _esc.enter_context(nc.semaphore(p + "csl"))
    pe_s = _esc.enter_context(nc.semaphore(p + "cpe"))
    av_s = _esc.enter_context(nc.semaphore(p + "cav"))
    sg_s = _esc.enter_context(nc.semaphore(p + "csg"))
    pg_s = _esc.enter_context(nc.semaphore(p + "cpg"))
    ev2_s = _esc.enter_context(nc.semaphore(p + "cev"))
    rs_s = _esc.enter_context(nc.semaphore(p + "crs"))
    rp_s = _esc.enter_context(nc.semaphore(p + "crp"))
    ra_s = _esc.enter_context(nc.semaphore(p + "cra"))
    tt_s = _esc.enter_context(nc.semaphore(p + "ctt"))
    tv_s = _esc.enter_context(nc.semaphore(p + "ctv"))
    _gidx = [0]
    _np = 8
    if (stop or "").startswith("core1"):
        _np = 1
    elif (stop or "").startswith("core2"):
        _np = 2
    elif (stop or "").startswith("core4"):
        _np = 4
    for g_ in range(_np):
        hA, hB = 2 * g_, 2 * g_ + 1
        for G in range(4):
            if stop == "core1g" and G >= 1:
                continue
            q0, q1_ = 512 * G, 512 * (G + 1)
            jlist = [(8 * G - 1) % 32] + [8 * G + r for r in range(8)] \
                + [(8 * G + 8) % 32]
            jinfo = []
            for j in jlist:
                pieces = []
                off = 0
                for (s0, l0) in _l1_window_pieces(j):
                    lo, hi = max(s0, q0), min(s0 + l0, q1_)
                    if lo < hi:
                        pieces.append((lo, hi - lo, off))
                        off += hi - lo
                cons = {}
                for bb in ((j - 1) % 32, j, (j + 1) % 32):
                    ts = 64 * bb
                    if q0 <= ts < q1_:
                        for (lo, ln_, o2) in pieces:
                            if lo <= ts < lo + ln_:
                                cons[bb] = o2 + (ts - lo)
                jinfo.append((j, pieces, cons, off))

            BSL, BPE, BAV = 20 * _gidx[0], 20 * _gidx[0], 8 * _gidx[0]
            BSG, BPG, BEV = 4 * _gidx[0], 2 * _gidx[0], 16 * _gidx[0]
            _gidx[0] += 1
            with (nc.psum_tensor(p + f"a{g_}g{G}sl", [128, 2, 512], F32) as slp,
                  nc.psum_tensor(p + f"a{g_}g{G}yl", [128, 4, 512], F32) as ylp,
                  nc.psum_tensor(p + f"a{g_}g{G}yg", [128, 4, 2, 128], F32) as ygp,
                  nc.Block() as blk):
                def _nb(jj):
                    return jj - 2

                @blk.tensor
                def _(t):
                    for h2 in range(2 if not L1_SKIP_GLOBAL else 0):
                        base = 64 * h2
                        mm = None
                        for (gi, (kts, kl)) in enumerate(((0, 64), (1984, 64))):
                            mm = t.matmul(
                                slp[gi * 64:(gi + 1) * 64, h2, 0:512],
                                kT[base:base + 64, g_, kts:kts + kl],
                                qT[base:base + 64, g_, q0:q1_],
                                start=True, stop=True)
                        mm.then_inc(sg_s, 1)
                    if not L1_SKIP_GLOBAL:
                        t.wait_ge(pg_s, BPG + 2)
                    for h2 in range(2 if not (L1_SKIP_GLOBAL or L1_SKIP_GAV) else 0):
                        h = 2 * g_ + h2
                        mm = None
                        for qc in range(4):
                            mm = t.matmul(
                                ygp[:, qc, h2, 0:65],
                                pg[:, h2, qc * 128:(qc + 1) * 128],
                                gvp[:, 65 * h:65 * h + 65],
                                start=True, stop=True)
                        mm.then_inc(sg_s, 1)
                    for jj, (j, pieces, cons, w) in enumerate(
                            jinfo if not L1_SKIP_LOCAL else []):
                        kr = (j % 2) * 64
                        if jj >= 2:
                            t.wait_ge(pe_s, BPE + 2 * (jj - 1))
                        for h2 in range(2):
                            base = 64 * h2
                            mm = None
                            for (lo, ln_, o2) in pieces:
                                mm = t.matmul(
                                    slp[kr:kr + 64, h2, o2:o2 + ln_],
                                    kT[base:base + 64, g_, 64 * j:64 * j + 64],
                                    qT[base:base + 64, g_, lo:lo + ln_],
                                    start=True, stop=True)
                            mm.then_inc(sl_s, 1)
                        if jj < 2:
                            continue
                        bblk = (jlist[jj] - 1) % 32
                        assert bblk == jlist[jj - 1], (jj, jlist)
                        t.wait_ge(pe_s, BPE + 2 * (jj + 1))
                        nb = _nb(jj)
                        if nb >= 1:
                            t.wait_ge(ev2_s, BEV + 2 * nb)
                        qrow = (bblk % 2) * 64
                        for h2 in range(2):
                            h = 2 * g_ + h2
                            for cp in range(2):
                                bank = 2 * h2 + cp
                                mms = []
                                for jj2 in (jj - 2, jj - 1, jj):
                                    j2 = jinfo[jj2][0]
                                    if j2 % 2 != cp:
                                        continue
                                    po = jinfo[jj2][2][bblk]
                                    kr2 = (j2 % 2) * 64
                                    ktc2 = j2 // 2
                                    mms.append((jj2, po, kr2, ktc2))
                                mm = None
                                for ci, (jj2, po, kr2, ktc2) in enumerate(mms):
                                    mm = t.matmul(
                                        ylp[qrow:qrow + 64, bank, 0:65],
                                        pl[kr2:kr2 + 64, jj2 % 4, 192 * h2 + po:
                                           192 * h2 + po + 64],
                                        v_aug[kr2:kr2 + 64, ktc2,
                                              65 * h:65 * h + 65],
                                        start=(ci == 0), stop=(ci == len(mms) - 1))
                        mm.then_inc(av_s, 1)

                @blk.scalar
                def _(s):
                    for h2 in range(2 if not L1_SKIP_GLOBAL else 0):
                        s.wait_ge(sg_s, BSG + h2 + 1)
                        s.activation(pg[:, h2, :], slp[:, h2, 0:512], AF.Exp,
                                     scale=0.125).then_inc(pg_s, 1)
                    first = True
                    for jj, (j, pieces, cons, w) in enumerate(
                            jinfo if not L1_SKIP_LOCAL else []):
                        kr = (j % 2) * 64
                        if first and not L1_SKIP_GLOBAL:
                            s.wait_ge(sg_s, BSG + 4)
                            first = False
                        if jj >= 4:
                            s.wait_ge(av_s, BAV + jj - 3)
                        s.wait_ge(sl_s, BSL + 2 * jj + 2)
                        s.activation(pl[kr:kr + 64, jj % 4, :]
                                     .rearrange("p (h2 c) -> p h2 c", h2=2)
                                     [:, :, 0:w],
                                     slp[kr:kr + 64, :, 0:w],
                                     AF.Exp, scale=0.125).then_inc(pe_s, 2)

                @blk.vector
                def _(v):
                    if L1_SKIP_GLOBAL:
                        v.nop().then_inc(sg_s, 4)
                    if L1_SKIP_LOCAL:
                        v.nop().then_inc(av_s, 8)
                        v.nop().then_inc(ev2_s, 16)
                    if L1_SKIP_GAV and not L1_SKIP_GLOBAL:
                        v.nop().then_inc(sg_s, 2)
                    for jj, (j, pieces, cons, w) in enumerate(
                            jinfo if not L1_SKIP_LOCAL else []):
                        if jj < 2:
                            continue
                        nb = _nb(jj)
                        bblk = jlist[jj - 1]
                        qrow = (bblk % 2) * 64
                        v.wait_ge(av_s, BAV + nb + 1)
                        for h2 in range(2):
                            v.tensor_copy(
                                ysb[qrow:qrow + 64, nb // 2, h2, :, 0:65],
                                ylp[qrow:qrow + 64, 2 * h2:2 * h2 + 2, 0:65]
                                ).then_inc(ev2_s, 1)
                    v.wait_ge(sg_s, BSG + 4)
                    v.wait_ge(ev2_s, BEV + 16)
                    if not L1_SKIP_NORM:
                        ysv = ysb
                        ytk = y_tok[:, 4 * G:4 * G + 4, :].rearrange(
                            "p a (h e) -> p a h e", h=2)
                        zl = zsc[:, 0:8].rearrange("p (a h) -> p a h", a=4
                                                   ).unsqueeze(3)
                        zg = zsc[:, 8:16].rearrange("p (a h) -> p a h", a=4
                                                    ).unsqueeze(3)
                        v.tensor_tensor(ysv[:, :, :, 0, :], ysv[:, :, :, 0, :],
                                        ysv[:, :, :, 1, :], ALU.add)
                        v.drain()
                        v.reciprocal(zl, ysv[:, :, :, 0, 64:65])
                        v.reciprocal(zg, ygp[:, 0:4, :, 64:65])
                        v.drain()
                        v.tensor_tensor(ytk, ysv[:, :, :, 0, 0:64],
                                        zl.broadcast_to([128, 4, 2, 64]), ALU.mult)
                        v.tensor_tensor(ytmp[:, :, :, :], ygp[:, 0:4, :, 0:64],
                                        zg.broadcast_to([128, 4, 2, 64]), ALU.mult)
                        v.drain()
                        v.tensor_tensor(ytk, ytk, ytmp[:, :, :, :], ALU.add)

        if stop == "core1g":
            return
        if L1_SKIP_ROW:
            continue
        BRS, BRP, BRA = 8 * g_, 8 * g_, 8 * g_
        pr4 = scr9[:, 0:1024].rearrange("pp (a b) -> pp a b", a=2)
        with (nc.psum_tensor(p + f"r{g_}sp", [128, 2, 512], F32) as rsp,
              nc.psum_tensor(p + f"r{g_}yp", [64, 2, 512], F32) as ryp,
              nc.Block() as blk):
            @blk.tensor
            def _(t):
                for idx in range(8):
                    h2, sup = idx // 4, idx % 4
                    base = 64 * h2
                    h = 2 * g_ + h2
                    if idx >= 2:
                        t.wait_ge(rp_s, BRP + idx - 1)
                    for t4 in range(4):
                        tc = 4 * sup + t4
                        mm = t.matmul(rsp[:, idx % 2, 64 * t4:64 * t4 + 64],
                                      kT[base:base + 64, g_,
                                         tc * 128:(tc + 1) * 128],
                                      qT[base:base + 64, g_, 0:64],
                                      start=True, stop=True)
                    mm.then_inc(rs_s, 1)
                    t.wait_ge(rp_s, BRP + idx + 1)
                    for t4 in range(4):
                        tc = 4 * sup + t4
                        mm = t.matmul(ryp[0:64, h2, 0:65],
                                      pr4[:, idx % 2, 64 * t4:64 * t4 + 64],
                                      v_aug[:, tc, 65 * h:65 * h + 65],
                                      start=(sup == 0 and t4 == 0),
                                      stop=(sup == 3 and t4 == 3))
                    mm.then_inc(ra_s, 1)

            @blk.scalar
            def _(s):
                for idx in range(8):
                    s.wait_ge(rs_s, BRS + idx + 1)
                    if idx >= 2:
                        s.wait_ge(ra_s, BRA + idx - 1)
                    s.activation(pr4[:, idx % 2, 0:256], rsp[:, idx % 2, 0:256],
                                 AF.Exp, scale=0.125).then_inc(rp_s, 1)

            @blk.vector
            def _(v):
                v.wait_ge(ra_s, BRA + 8)
                for h2 in range(2):
                    v.reciprocal(smalls[0:64, 0:1], ryp[0:64, h2, 64:65])
                    v.drain()
                    v.tensor_scalar(y_tok[0:64, 0, 64 * h2:64 * h2 + 64],
                                    ryp[0:64, h2, 0:64],
                                    smalls[0:64, 0:1], None, ALU.mult)
                    v.drain()

        if L1_SKIP_TRANS:
            continue
        BTT, BTV = 16 * g_, 16 * g_
        with (nc.psum_tensor(p + f"t{g_}p", [128, 2, 512], F32) as tps,
              nc.Block() as blk):
            @blk.tensor
            def _(t):
                for tc in range(16):
                    if tc >= 2:
                        t.wait_ge(tv_s, BTV + tc - 1)
                    t.matmul(tps[:, tc % 2, 0:128], y_tok[:, tc, :], ident[:, :],
                             start=True, stop=True).then_inc(tt_s, 1)

            @blk.vector
            def _(v):
                for tc in range(16):
                    v.wait_ge(tt_s, BTT + tc + 1)
                    v.tensor_copy(yT[:, g_, tc * 128:(tc + 1) * 128],
                                  tps[:, tc % 2, 0:128]).then_inc(tv_s, 1)

    wo_q = wb[:, 0:2048].rearrange("pp (a b) -> pp a b", a=8)
    with (nc.semaphore(p + "ow") as ow_s, nc.semaphore(p + "om") as om_s,
          nc.semaphore(p + "oe") as oe_s,
          nc.psum_tensor(p + "ops", [128, 2, 512], F32) as ops,
          nc.Block() as blk):
        @blk.gpsimd
        def _(g):
            for dq in range(4):
                if dq >= 1:
                    g.wait_ge(om_s, 16 * dq)
                g.dma_start(wb[0:1, 2176:2432],
                            brow_d[0:1, dq * 256:(dq + 1) * 256]).then_inc(ow_s, 16)
                for g2 in range(8):
                    g.dma_start(wo_q[:, g2, :],
                                wo_d[g2 * 128:(g2 + 1) * 128,
                                     dq * 256:(dq + 1) * 256]).then_inc(ow_s, 16)

        @blk.tensor
        def _(t):
            for dq in range(4):
                t.wait_ge(ow_s, 16 * 9 * (dq + 1))
                for ti in range(NT):
                    n = 16 * dq + ti
                    if n >= 2:
                        t.wait_ge(oe_s, n - 1)
                    for g2 in range(8):
                        t.matmul(ops[:, ti % 2, 0:256],
                                 yT[:, g2, ti * 128:(ti + 1) * 128],
                                 wo_q[:, g2, :],
                                 start=(g2 == 0), stop=False)
                    mm = t.matmul(ops[:, ti % 2, 0:256], wb[0:1, 2048:2176],
                                  wb[0:1, 2176:2432],
                                  start=False, stop=True)
                    mm.then_inc(om_s, 1)

        @blk.vector
        def _(v):
            for dq in range(4):
                for ti in range(NT):
                    n = 16 * dq + ti
                    v.wait_ge(om_s, n + 1)
                    v.scalar_tensor_tensor(
                        xs[:, ti, dq * 256:(dq + 1) * 256], ops[:, ti % 2, 0:256],
                        1.0, xs[:, ti, dq * 256:(dq + 1) * 256],
                        ALU.mult, ALU.add).then_inc(oe_s, 1)


def build_l1attn_standalone():
    nc = bass.Bass("TRN2", target_bir_lowering=False)
    x_in = nc.dram_tensor("x", [T, D], F32, kind="ExternalInput").ap()
    wq_d = nc.dram_tensor("wq", [1024, 1024], F32, kind="ExternalInput").ap()
    wk_d = nc.dram_tensor("wk", [1024, 1024], F32, kind="ExternalInput").ap()
    wv_d = nc.dram_tensor("wv", [1024, 1024], F32, kind="ExternalInput").ap()
    wo_d = nc.dram_tensor("wo", [1024, 1024], F32, kind="ExternalInput").ap()
    bq_d = nc.dram_tensor("bq", [128, 8], F32, kind="ExternalInput").ap()
    bk_d = nc.dram_tensor("bk", [128, 8], F32, kind="ExternalInput").ap()
    bv_d = nc.dram_tensor("bv", [128, 8], F32, kind="ExternalInput").ap()
    bo_d = nc.dram_tensor("bo", [1, 1024], F32, kind="ExternalInput").ap()
    x_out = nc.dram_tensor("xo", [T, D], F32, kind="ExternalOutput").ap()

    import contextlib
    with contextlib.ExitStack() as _es:
        def _al(name, shape, dt):
            return _es.enter_context(nc.sbuf_tensor(name, shape, dt))
        xs = _al("xs", [128, NT, D], F32)
        ident = _al("ident", [128, 128], BF16)
        lnT = _al("lnT", [128, ND, T], BF16)
        qT = _al("qT", [128, ND, T], BF16)
        kT = _al("kT", [128, ND, T], BF16)
        v_aug = _al("v_aug", [128, NT, 1040], BF16)
        wbuf = _al("wbuf", [128, 2432], BF16)
        scr9 = _al("scr9", [128, 4096], BF16)
        gvp = _al("gvp", [128, 1040], BF16)
        f32scr = _al("f32scr", [128, 64], F32)
        sums = f32scr[:, 0:16]
        sumsq = f32scr[:, 16:32]
        scale = f32scr[:, 32:48]
        bias = f32scr[:, 48:64]
        bq_sb = f32scr[:, 32:40]
        bk_sb = f32scr[:, 40:48]
        bv128 = f32scr[:, 48:56]
        smalls = f32scr[:, 0:24]
        ln_scr = scr9[:, 1536:3584].rearrange("pp (a b) -> pp a b", a=2)
        sq_scr = scr9[:, 0:1024]
        ws = dict(lnT=lnT, qT=qT, kT=kT, v_aug=v_aug, wbuf=wbuf, scr9=scr9,
                  bq_sb=bq_sb, bk_sb=bk_sb, bv128=bv128, bv128_d=bv_d,
                  gvp=gvp, smalls=smalls,
                  sums=sums, sumsq=sumsq, scale=scale, bias=bias,
                  sq_scr=sq_scr, ln_scr=ln_scr)
        with (nc.semaphore("xin") as xin_s, nc.Block() as blk):
            @blk.sync
            def _(s):
                s.dma_start(xs[:, :, :],
                            x_in.rearrange("(n pp) m -> pp n m", pp=128)
                            ).then_inc(xin_s, 16)
                s.wait_ge(xin_s, 16)
        emit_consts(nc, ident)
        emit_l1attn(nc, "a1", xs, ident, None, wq_d, wk_d, wv_d, wo_d,
                    bq_d, bk_d, bo_d, ws)
        with (nc.semaphore("xout") as xo_s, nc.Block() as blk):
            @blk.sync
            def _(s):
                s.dma_start(x_out.rearrange("(n pp) m -> pp n m", pp=128),
                            xs[:, :, :]).then_inc(xo_s, 16)
                s.wait_ge(xo_s, 16)
    return nc


def host_l1attn_weights(Wq, bq, Wk, bk, Wv, bv, Wo, bo):
    wq = np.ascontiguousarray(Wq.T)
    wk = np.ascontiguousarray(Wk.T)
    wv = np.ascontiguousarray(Wv.T)
    wo = np.ascontiguousarray(Wo.T)
    bq128 = np.ascontiguousarray(bq.reshape(8, 128).T)
    bk128 = np.ascontiguousarray(bk.reshape(8, 128).T)
    bv128 = np.ascontiguousarray(bv.reshape(8, 128).T)
    bo_row = np.ascontiguousarray(bo.reshape(1, 1024))
    return wq, wk, wv, wo, bq128, bk128, bv128, bo_row


emit_l1attn_phased = emit_l1attn


def emit_l0attn(nc, p, xs, ident, wq_d, wk_d, wv_d, wo_d, ones3_d, ws, lnsh=None):
    lnT = ws["lnT"]
    qT = ws["qTa"]
    kT = ws["kTa"]
    v_aug = ws["v_aug"]
    wpb = ws["wpb"]
    y_tok = ws["y_tok0"]
    Bp = ws["Bp"]
    tril = ws["tril"]
    onescol = ws["onescol"]
    C_sb = ws["C_sb"]
    smalls = ws["smalls"]
    zc = ws["zc0"]
    yT = lnT

    emit_ln_to_lnT(nc, p + "ln", xs, ident, lnT, ws, shared=lnsh)

    with nc.Block() as blk:
        @blk.gpsimd
        def _(g):
            g.memset(tril[:, :], 1.0)
            g.drain()
            g.affine_select(tril[:, :], tril[:, :], [[1, 128]], ALU.is_ge, 0.0,
                            channel_multiplier=-1)
            g.drain()
            g.memset(onescol[:, :], 1.0)
            g.drain()
            g.memset(C_sb[:, :], 0.0)
            vh = v_aug.rearrange("pp tc (h e) -> pp tc h e", e=65)
            g.memset(vh[:, :, :, 64:65], 1.0)

    for which, w_d, dst in (("q", wq_d, qT), ("k", wk_d, kT)):
        with (nc.semaphore(p + which + "w") as w_s,
              nc.semaphore(p + which + "m") as m_s,
              nc.semaphore(p + which + "e") as e_s,
              nc.psum_tensor(p + which + "ps", [128, 2, 2048], F32) as pps,
              nc.Block() as blk):
            @blk.gpsimd
            def _(g):
                for dj in range(8):
                    g.dma_start(wpb[:, dj, 0:768],
                                w_d[dj * 128:(dj + 1) * 128, :]).then_inc(w_s, 16)

            @blk.tensor
            def _(t):
                t.wait_ge(w_s, 128)
                for oc in range(6):
                    if oc >= 2:
                        t.wait_ge(e_s, oc - 1)
                    for dj in range(8):
                        for n4 in range(4):
                            mm = t.matmul(
                                pps[:, oc % 2, n4 * 512:(n4 + 1) * 512],
                                wpb[:, dj, oc * 128:(oc + 1) * 128],
                                lnT[:, dj, n4 * 512:(n4 + 1) * 512],
                                start=(dj == 0), stop=(dj == 7))
                    mm.then_inc(m_s, 1)

            @blk.vector
            def _(v):
                for oc in range(6):
                    v.wait_ge(m_s, oc + 1)
                    v.tensor_scalar(dst[:, oc, :], pps[:, oc % 2, :],
                                    0.5, None, ALU.mult).then_inc(e_s, 1)

        with (nc.semaphore(p + which + "o") as o_s, nc.Block() as blk):
            @blk.gpsimd
            def _(g):
                nd = 0
                for oc in range(6):
                    for a_ in range(3):
                        r = 16 + 32 * a_
                        g.dma_start(dst[r:r + 1, oc, :],
                                    ones3_d[a_:a_ + 1, :]).then_inc(o_s, 16)
                        nd += 1
                g.wait_ge(o_s, 16 * nd)

    with (nc.semaphore(p + "vw") as w_s, nc.semaphore(p + "vm") as m_s,
          nc.semaphore(p + "ve") as e_s,
          nc.psum_tensor(p + "vps", [128, 2, 1024], F32) as vps,
          nc.Block() as blk):
        @blk.gpsimd
        def _(g):
            for dj in range(8):
                g.dma_start(wpb[:, dj, :],
                            wv_d[dj * 128:(dj + 1) * 128, :]).then_inc(w_s, 16)

        @blk.tensor
        def _(t):
            t.wait_ge(w_s, 128)
            for ti in range(NT):
                if ti >= 2:
                    t.wait_ge(e_s, ti - 1)
                for dj in range(8):
                    for n2 in range(2):
                        mm = t.matmul(vps[:, ti % 2, n2 * 512:(n2 + 1) * 512],
                                      lnT[:, dj, ti * 128:(ti + 1) * 128],
                                      wpb[:, dj, n2 * 512:(n2 + 1) * 512],
                                      start=(dj == 0), stop=(dj == 7))
                mm.then_inc(m_s, 1)

        @blk.vector
        def _(v):
            vh = v_aug.rearrange("pp tc (h e) -> pp tc h e", e=65)
            for ti in range(NT):
                v.wait_ge(m_s, ti + 1)
                v.tensor_copy(vh[:, ti, :, 0:64],
                              vps[:, ti % 2, :].rearrange("pp (h e) -> pp h e",
                                                          h=16)).then_inc(e_s, 1)

    import contextlib as _ctl
    _esc = _ctl.ExitStack()
    ss_s = _esc.enter_context(nc.semaphore(p + "css"))
    ph_s = _esc.enter_context(nc.semaphore(p + "cph"))
    p2_s = _esc.enter_context(nc.semaphore(p + "cp2"))
    av_s = _esc.enter_context(nc.semaphore(p + "cav"))
    cs_s = _esc.enter_context(nc.semaphore(p + "ccs"))
    nm_s = _esc.enter_context(nc.semaphore(p + "cnm"))
    ca_s = _esc.enter_context(nc.semaphore(p + "cca"))
    tt_s = _esc.enter_context(nc.semaphore(p + "ctt"))
    tv_s = _esc.enter_context(nc.semaphore(p + "ctv"))
    ab_s = _esc.enter_context(nc.semaphore(p + "cab"))

    def _batches(i):
        out = []
        j0 = 0
        while j0 <= i:
            out.append((j0, min(4, i + 1 - j0)))
            j0 += 4
        return out

    _binfo = []
    _gb = 0
    _ss = 0
    for i in range(NT):
        bl = _batches(i)
        for h in range(16):
            for k, (j0, nj) in enumerate(bl):
                diag = (j0 + nj - 1 == i)
                _binfo.append((i, h, k, j0, nj, _gb, _ss, diag))
                _gb += 1
                _ss += nj
    _bykey = {(i, h, k): e for e in _binfo for (i, h, k) in [(e[0], e[1], e[2])]}

    with (nc.psum_tensor(p + "sb", [128, 2, 512], F32) as sbat,
          nc.psum_tensor(p + "yp", [128, 16, 128], F32) as yps,
          nc.psum_tensor(p + "tp", [128, 2, 512], F32) as tps,
          nc.Block() as blk):
        @blk.tensor
        def _(t):
            for i in range(NT):
                bl = _batches(i)
                nb_ = len(bl)

                def emit_s(h, k):
                    (j0, nj) = bl[k]
                    gb = _bykey[(i, h, k)][5]
                    hb = 32 * (h % 3)
                    oc = h // 3
                    if gb >= 2:
                        t.wait_ge(ph_s, gb - 1)
                    for jj in range(nj):
                        j = j0 + jj
                        t.matmul(sbat[:, gb % 2, jj * 128:(jj + 1) * 128],
                                 kT[hb:hb + 17, oc, j * 128:(j + 1) * 128],
                                 qT[hb:hb + 17, oc, i * 128:(i + 1) * 128],
                                 start=True, stop=True).then_inc(ss_s, 1)

                def emit_av(h, k):
                    (j0, nj) = bl[k]
                    gb = _bykey[(i, h, k)][5]
                    t.wait_ge(ph_s, gb + 1)
                    if i >= 1 and h == 0 and k == 0:
                        t.wait_ge(nm_s, i)
                        t.wait_ge(ca_s, i)
                    for jj in range(nj):
                        j = j0 + jj
                        mm = t.matmul(
                            yps[:, h, 0:65],
                            Bp[:, gb % 2, jj * 128:(jj + 1) * 128],
                            v_aug[:, j, 65 * h:65 * h + 65],
                            start=(k == 0 and jj == 0), stop=False)
                    mm.then_inc(ab_s, 1)
                    if k == nb_ - 1:
                        mm = t.matmul(yps[:, h, 0:65], tril[:, :],
                                      v_aug[:, i, 65 * h:65 * h + 65],
                                      start=False, stop=(i == 0))
                        if i >= 1:
                            mm = t.matmul(yps[:, h, 0:65], onescol[:, :],
                                          ws["C_bf"][0:1, 65 * h:65 * h + 65],
                                          start=False, stop=True)
                        mm.then_inc(av_s, 1)

                stream = [(h, k) for h in range(16) for k in range(nb_)]
                for n, (h, k) in enumerate(stream):
                    emit_s(h, k)
                    if n >= 1:
                        emit_av(*stream[n - 1])
                emit_av(*stream[-1])
                if i >= 1:
                    t.wait_ge(nm_s, i)
                    for occ in range(8):
                        n = 8 * (i - 1) + occ
                        if n >= 2:
                            t.wait_ge(tv_s, n - 1)
                        t.matmul(tps[:, n % 2, 0:128],
                                 y_tok[:, (i - 1) % 2, occ * 128:(occ + 1) * 128],
                                 ident[:, :], start=True, stop=True
                                 ).then_inc(tt_s, 1)
                t.wait_ge(nm_s, i + 1)
                for h in range(16):
                    mm = t.matmul(yps[0:1, h, 64:128],
                                  ws["onesc2"][:, 0:1],
                                  v_aug[:, i, 65 * h:65 * h + 64],
                                  start=True, stop=True)
                mm.then_inc(cs_s, 1)
            for occ in range(8):
                n = 8 * (NT - 1) + occ
                t.wait_ge(tv_s, n - 1)
                t.matmul(tps[:, n % 2, 0:128],
                         y_tok[:, (NT - 1) % 2, occ * 128:(occ + 1) * 128],
                         ident[:, :], start=True, stop=True).then_inc(tt_s, 1)

        @blk.scalar
        def _(s):
            for (i, h, k, j0, nj, gb, ssb, diag) in _binfo:
                s.wait_ge(ss_s, ssb + nj)
                if gb >= 2:
                    s.wait_ge(ab_s, gb - 1)
                if diag:
                    s.activation(Bp[:, gb % 2, 0:128 * nj],
                                 sbat[:, gb % 2, 0:128 * nj],
                                 AF.Square).then_inc(p2_s, 1)
                else:
                    s.wait_ge(ph_s, gb)
                    s.activation(Bp[:, gb % 2, 0:128 * nj],
                                 sbat[:, gb % 2, 0:128 * nj],
                                 AF.Square).then_inc(ph_s, 1)

        @blk.vector
        def _(v):
            ndiag = 0
            for i in range(NT):
                bl = _batches(i)
                for h in range(16):
                    k = len(bl) - 1
                    (_, _, _, j0, nj, gb, ssb, diag) = _bykey[(i, h, k)]
                    assert diag
                    ndiag += 1
                    v.wait_ge(p2_s, ndiag)
                    v.tensor_tensor(Bp[:, gb % 2, 128 * (nj - 1):128 * nj],
                                    Bp[:, gb % 2, 128 * (nj - 1):128 * nj],
                                    tril[:, :], ALU.mult).then_inc(ph_s, 1)
                if i >= 1:
                    for occ in range(8):
                        n = 8 * (i - 1) + occ
                        v.wait_ge(tt_s, n + 1)
                        v.tensor_copy(yT[:, occ, (i - 1) * 128:i * 128],
                                      tps[:, n % 2, 0:128]).then_inc(tv_s, 1)
                v.wait_ge(av_s, 16 * (i + 1))
                if i >= 1:
                    v.wait_ge(tv_s, 8 * i)
                v.reciprocal(zc[:, 0:16].unsqueeze(2), yps[:, :, 64:65])
                v.drain()
                v.tensor_tensor(
                    y_tok[:, i % 2, :].rearrange("pp (h e) -> pp h e", h=16),
                    yps[:, :, 0:64],
                    zc[:, 0:16].unsqueeze(2).broadcast_to([128, 16, 64]),
                    ALU.mult).then_inc(nm_s, 1)
                v.wait_ge(cs_s, i + 1)
                cview = C_sb[0:1, :].rearrange("a (h e) -> a h e", e=65)
                v.tensor_tensor(cview[:, :, 0:64], cview[:, :, 0:64],
                                yps[0:1, :, 64:128][:, :, 0:64], ALU.add)
                v.drain()
                for h in range(16):
                    v.memset(C_sb[0:1, 65 * h + 64:65 * h + 65],
                             float(128 * (i + 1)))
                v.drain()
                v.tensor_copy(ws["C_bf"][0:1, :], C_sb[0:1, :])
                v.drain()
                v.nop().then_inc(ca_s, 1)
            for occ in range(8):
                n = 8 * (NT - 1) + occ
                v.wait_ge(tt_s, n + 1)
                v.tensor_copy(yT[:, occ, (NT - 1) * 128:NT * 128],
                              tps[:, n % 2, 0:128]).then_inc(tv_s, 1)
    _esc.close()

    with (nc.semaphore(p + "ow") as ow_s, nc.semaphore(p + "om") as om_s,
          nc.semaphore(p + "oe") as oe_s,
          nc.psum_tensor(p + "ops", [128, 2, 512], F32) as ops,
          nc.Block() as blk):
        @blk.gpsimd
        def _(g):
            for dq in range(4):
                if dq >= 1:
                    g.wait_ge(om_s, 16 * dq)
                for g2 in range(8):
                    g.dma_start(wpb[:, g2, 0:256],
                                wo_d[g2 * 128:(g2 + 1) * 128,
                                     dq * 256:(dq + 1) * 256]).then_inc(ow_s, 16)

        @blk.tensor
        def _(t):
            for dq in range(4):
                t.wait_ge(ow_s, 16 * 8 * (dq + 1))
                for ti in range(NT):
                    n = 16 * dq + ti
                    if n >= 2:
                        t.wait_ge(oe_s, n - 1)
                    for g2 in range(8):
                        mm = t.matmul(ops[:, ti % 2, 0:256],
                                      yT[:, g2, ti * 128:(ti + 1) * 128],
                                      wpb[:, g2, 0:256],
                                      start=(g2 == 0), stop=(g2 == 7))
                    mm.then_inc(om_s, 1)

        @blk.vector
        def _(v):
            for dq in range(4):
                for ti in range(NT):
                    n = 16 * dq + ti
                    v.wait_ge(om_s, n + 1)
                    v.scalar_tensor_tensor(
                        xs[:, ti, dq * 256:(dq + 1) * 256], ops[:, ti % 2, 0:256],
                        1.0, xs[:, ti, dq * 256:(dq + 1) * 256],
                        ALU.mult, ALU.add).then_inc(oe_s, 1)


def build_l0attn_standalone():
    nc = bass.Bass("TRN2", target_bir_lowering=False)
    x_in = nc.dram_tensor("x", [T, D], F32, kind="ExternalInput").ap()
    wq_d = nc.dram_tensor("wq", [1024, 768], F32, kind="ExternalInput").ap()
    wk_d = nc.dram_tensor("wk", [1024, 768], F32, kind="ExternalInput").ap()
    wv_d = nc.dram_tensor("wv", [1024, 1024], F32, kind="ExternalInput").ap()
    wo_d = nc.dram_tensor("wo", [1024, 1024], F32, kind="ExternalInput").ap()
    ones3_d = nc.dram_tensor("ones3", [3, 2048], F32, kind="ExternalInput").ap()
    x_out = nc.dram_tensor("xo", [T, D], F32, kind="ExternalOutput").ap()

    import contextlib
    with contextlib.ExitStack() as _es:
        def _al(name, shape, dt):
            return _es.enter_context(nc.sbuf_tensor(name, shape, dt))
        xs = _al("xs", [128, NT, D], F32)
        ident = _al("ident", [128, 128], BF16)
        lnT = _al("lnT", [128, ND, T], BF16)
        qTa = _al("qTa", [128, 6, T], BF16)
        kTa = _al("kTa", [128, 6, T], BF16)
        v_aug = _al("v_aug", [128, NT, 1040], BF16)
        wpb = _al("wpb", [128, 8, 1024], BF16)
        y_tok0 = _al("y_tok0", [128, 2, 1024], BF16)
        Bp = _al("Bp", [128, 2, 512], BF16)
        tril = _al("tril", [128, 128], BF16)
        onescol = _al("onescol", [1, 128], BF16)
        onesc2 = _al("onesc2", [128, 1], BF16)
        C_sb = _al("C_sb", [1, 1040], F32)
        C_bf = _al("C_bf", [1, 1040], BF16)
        f32scr = _al("f32scr", [128, 80], F32)
        sums = f32scr[:, 0:16]
        sumsq = f32scr[:, 16:32]
        scale = f32scr[:, 32:48]
        bias = f32scr[:, 48:64]
        zc0 = f32scr[:, 64:80]
        smalls = f32scr[:, 0:24]
        ln_scr = y_tok0
        sq_scr = y_tok0[:, 0, :]
        ws = dict(lnT=lnT, qTa=qTa, kTa=kTa, v_aug=v_aug, wpb=wpb,
                  y_tok0=y_tok0, Bp=Bp, tril=tril, onescol=onescol,
                  onesc2=onesc2, C_sb=C_sb, C_bf=C_bf, zc0=zc0, smalls=smalls,
                  sums=sums, sumsq=sumsq, scale=scale, bias=bias,
                  sq_scr=sq_scr, ln_scr=ln_scr)
        with (nc.semaphore("xin") as xin_s, nc.Block() as blk):
            @blk.sync
            def _(s):
                s.dma_start(xs[:, :, :],
                            x_in.rearrange("(n pp) m -> pp n m", pp=128)
                            ).then_inc(xin_s, 16)
                s.wait_ge(xin_s, 16)
            @blk.gpsimd
            def _(g):
                g.memset(ws["onesc2"][:, :], 1.0)
        emit_consts(nc, ident)
        emit_l0attn(nc, "a0", xs, ident, wq_d, wk_d, wv_d, wo_d, ones3_d, ws)
        with (nc.semaphore("xout") as xo_s, nc.Block() as blk):
            @blk.sync
            def _(s):
                s.dma_start(x_out.rearrange("(n pp) m -> pp n m", pp=128),
                            xs[:, :, :]).then_inc(xo_s, 16)
                s.wait_ge(xo_s, 16)
    return nc


def host_l0attn_weights(Wq, Wk, Wv, Wo):
    def pad6(W):
        out = np.zeros((1024, 768), dtype=W.dtype)
        for oc in range(6):
            for a_ in range(3):
                h = 3 * oc + a_
                if h >= 16:
                    break
                out[:, oc * 128 + 32 * a_:oc * 128 + 32 * a_ + 16] = \
                    W[:, 16 * h:16 * h + 16]
        return np.ascontiguousarray(out)
    ones3 = np.ones((3, 2048), dtype=np.float32)
    return pad6(Wq), pad6(Wk), np.ascontiguousarray(Wv), \
        np.ascontiguousarray(Wo), ones3


def build_fused():
    nc = bass.Bass("TRN2", target_bir_lowering=False)
    x_in = nc.dram_tensor("x", [T, D], F32, kind="ExternalInput").ap()
    a0wq = nc.dram_tensor("a0wq", [1024, 768], F32, kind="ExternalInput").ap()
    a0wk = nc.dram_tensor("a0wk", [1024, 768], F32, kind="ExternalInput").ap()
    a0wv = nc.dram_tensor("a0wv", [1024, 1024], F32, kind="ExternalInput").ap()
    a0wo = nc.dram_tensor("a0wo", [1024, 1024], F32, kind="ExternalInput").ap()
    ones3_d = nc.dram_tensor("ones3", [3, 2048], F32, kind="ExternalInput").ap()
    f0wg = nc.dram_tensor("f0wg", [24, 1024, 128], I32, kind="ExternalInput").ap()
    f0wu = nc.dram_tensor("f0wu", [24, 1024, 128], I32, kind="ExternalInput").ap()
    f0wd = nc.dram_tensor("f0wd", [24, 128, 1024], I32, kind="ExternalInput").ap()
    a1wq = nc.dram_tensor("a1wq", [1024, 1024], F32, kind="ExternalInput").ap()
    a1wk = nc.dram_tensor("a1wk", [1024, 1024], F32, kind="ExternalInput").ap()
    a1wv = nc.dram_tensor("a1wv", [1024, 1024], F32, kind="ExternalInput").ap()
    a1wo = nc.dram_tensor("a1wo", [1024, 1024], F32, kind="ExternalInput").ap()
    a1bq = nc.dram_tensor("a1bq", [128, 8], F32, kind="ExternalInput").ap()
    a1bk = nc.dram_tensor("a1bk", [128, 8], F32, kind="ExternalInput").ap()
    a1bv = nc.dram_tensor("a1bv", [128, 8], F32, kind="ExternalInput").ap()
    a1bo = nc.dram_tensor("a1bo", [1, 1024], F32, kind="ExternalInput").ap()
    g1w1 = nc.dram_tensor("g1w1", [8, 128, 4096], F32, kind="ExternalInput").ap()
    g1w2 = nc.dram_tensor("g1w2", [32, 128, 1024], F32, kind="ExternalInput").ap()
    g1cw = nc.dram_tensor("g1cw", [128, 32, 3], F32, kind="ExternalInput").ap()
    g1cb = nc.dram_tensor("g1cb", [128, 32], F32, kind="ExternalInput").ap()
    hs_d = nc.dram_tensor("hs_scr", [32, 128, 2048], BF16).ap()
    x_out = nc.dram_tensor("xo", [T, D], F32, kind="ExternalOutput").ap()

    import contextlib
    with contextlib.ExitStack() as es0:
        def _al0(name, shape, dt):
            return es0.enter_context(nc.sbuf_tensor(name, shape, dt))
        xs = _al0("xs", [128, NT, D], F32)
        ident = _al0("ident", [128, 128], BF16)
        LN = tuple(es0.enter_context(nc.semaphore("LN" + s))
                   for s in ("c1", "c2", "ap", "tp", "ev"))

        with (nc.semaphore("xin") as xin_s, nc.Block() as blk):
            @blk.sync
            def _(s):
                s.dma_start(xs[:, :, :],
                            x_in.rearrange("(n pp) m -> pp n m", pp=128)
                            ).then_inc(xin_s, 16)
                s.wait_ge(xin_s, 16)
        emit_consts(nc, ident)

        with contextlib.ExitStack() as es1:
            def _al(name, shape, dt):
                return es1.enter_context(nc.sbuf_tensor(name, shape, dt))
            lnT = _al("b1lnT", [128, ND, T], BF16)
            qTa = _al("b1qTa", [128, 6, T], BF16)
            kTa = _al("b1kTa", [128, 6, T], BF16)
            v_aug = _al("b1vaug", [128, NT, 1040], BF16)
            wpb = _al("b1wpb", [128, 8, 1024], BF16)
            y_tok0 = _al("b1ytok0", [128, 2, 1024], BF16)
            Bp = _al("b1Bp", [128, 2, 512], BF16)
            tril = _al("b1tril", [128, 128], BF16)
            onescol = _al("b1onescol", [1, 128], BF16)
            C_sb = _al("b1Csb", [1, 1040], F32)
            C_bf = _al("b1Cbf", [1, 1040], BF16)
            f32scr = _al("b1f32scr", [128, 80], F32)
            onesc2 = _al("b1onesc2", [128, 1], BF16)
            with nc.Block() as blk:
                @blk.gpsimd
                def _(g):
                    g.memset(onesc2[:, :], 1.0)
            ws = dict(lnT=lnT, qTa=qTa, kTa=kTa, v_aug=v_aug, wpb=wpb,
                      y_tok0=y_tok0, Bp=Bp, tril=tril, onescol=onescol,
                      onesc2=onesc2, C_sb=C_sb, C_bf=C_bf,
                      zc0=f32scr[:, 64:80], smalls=f32scr[:, 0:24],
                      sums=f32scr[:, 0:16], sumsq=f32scr[:, 16:32],
                      scale=f32scr[:, 32:48], bias=f32scr[:, 48:64],
                      sq_scr=y_tok0[:, 0, :], ln_scr=y_tok0)
            emit_l0attn(nc, "a0", xs, ident, a0wq, a0wk, a0wv, a0wo, ones3_d,
                        ws, lnsh=(LN, 0))

        with contextlib.ExitStack() as es2:
            def _al(name, shape, dt):
                return es2.enter_context(nc.sbuf_tensor(name, shape, dt))
            lnT = _al("b2lnT", [128, ND, T], BF16)
            wd_sb = _al("b2wd", [128, 24, 1024], BF16)
            wg_sb = _al("b2wg", [128, 2, 8, 128], BF16)
            wu_sb = _al("b2wu", [128, 2, 8, 128], BF16)
            hT = _al("b2hT", [128, 24, 1024], BF16)
            silu_scr = _al("b2silu", [128, 1024], BF16)
            sums = _al("b2sums", [128, NT], F32)
            sumsq = _al("b2sumsq", [128, NT], F32)
            scale = _al("b2scale", [128, NT], F32)
            bias = _al("b2bias", [128, NT], F32)
            ws = dict(lnT=lnT, wd_sb=wd_sb, wg_sb=wg_sb, wu_sb=wu_sb, hT=hT,
                      silu_scr=silu_scr, sums=sums, sumsq=sumsq,
                      scale=scale, bias=bias, sq_scr=silu_scr, ln_scr=hT)
            emit_l0ffn(nc, "f0", xs, ident, f0wg, f0wu, f0wd, ws, lnsh=(LN, 1))

        with contextlib.ExitStack() as es3:
            def _al(name, shape, dt):
                return es3.enter_context(nc.sbuf_tensor(name, shape, dt))
            lnT = _al("b3lnT", [128, ND, T], BF16)
            qT = _al("b3qT", [128, ND, T], BF16)
            kT = _al("b3kT", [128, ND, T], BF16)
            v_aug = _al("b3vaug", [128, NT, 1040], BF16)
            wbuf = _al("b3wbuf", [128, 2432], BF16)
            scr9 = _al("b3scr9", [128, 4096], BF16)
            gvp = _al("b3gvp", [128, 1040], BF16)
            f32scr = _al("b3f32scr", [128, 64], F32)
            ws = dict(lnT=lnT, qT=qT, kT=kT, v_aug=v_aug, wbuf=wbuf, scr9=scr9,
                      bq_sb=f32scr[:, 32:40], bk_sb=f32scr[:, 40:48],
                      bv128=f32scr[:, 48:56], bv128_d=a1bv,
                      gvp=gvp, smalls=f32scr[:, 0:24],
                      sums=f32scr[:, 0:16], sumsq=f32scr[:, 16:32],
                      scale=f32scr[:, 32:48], bias=f32scr[:, 48:64],
                      sq_scr=scr9[:, 0:1024],
                      ln_scr=scr9[:, 1536:3584].rearrange(
                          "pp (a b) -> pp a b", a=2))
            emit_l1attn(nc, "a1", xs, ident, None, a1wq, a1wk, a1wv, a1wo,
                        a1bq, a1bk, a1bo, ws, lnsh=(LN, 2))

        with contextlib.ExitStack() as es4:
            def _al(name, shape, dt):
                return es4.enter_context(nc.sbuf_tensor(name, shape, dt))
            q1T = _al("b4q1T", [128, ND, T], BF16)
            tern = _al("b4tern", [128, 32768], BF16)
            big16 = _al("b4big16", [128, 8192], BF16)
            scr16k = _al("b4scr16k", [128, 4096], F32)
            hsp = _al("b4hsp", [128, 2048], BF16)
            M = _al("b4Mt", [128, 2048], BF16)
            Mn = _al("b4Mn", [128, 2048], BF16)
            sums = _al("b4sums", [128, NT], F32)
            sumsq = _al("b4sumsq", [128, NT], F32)
            scale = _al("b4scale", [128, NT], F32)
            bias = _al("b4bias", [128, NT], F32)
            s2 = _al("b4s2", [128, NT], F32)
            inv_s2 = _al("b4invs2", [128, NT], F32)
            comb = _al("b4comb", [128, NT], F32)
            acc = _al("b4acc", [128, 32], F32)
            smalls = _al("b4smalls", [128, 8], F32)
            cw_sb = _al("b4cw", [128, 32, 3], F32)
            cb_sb = _al("b4cb", [128, 32], F32)
            s0c = _al("b4s0c", [128, 32], F32)
            smc = _al("b4smc", [128, 32], F32)
            spc = _al("b4spc", [128, 32], F32)
            sc1 = _al("b4sc1", [128, 1], F32)
            wm1 = _al("b4wm1", [128, 1], F32)
            sc2 = _al("b4sc2", [128, 1], F32)
            wm2 = _al("b4wm2", [128, 1], F32)
            r1x = _al("b4r1x", [1, 256], F32)
            identf = _al("b4identf", [128, 128], F32)
            onesf = _al("b4onesf", [128, 128], F32)
            with nc.Block() as blk:
                @blk.gpsimd
                def _(g):
                    g.memset(onesf[:, :], 1.0)
            emit_consts(nc, identf)
            lq_f32 = scr16k[:, 2048:3072]
            lq_bf = scr16k[:, 3072:3584].bitcast(BF16)
            ws = dict(sums=sums, sumsq=sumsq, scale=scale, bias=bias,
                      sq_scr=lq_bf,
                      lnf=scr16k[:, 0:2048].rearrange("pp (a b) -> pp a b", a=2),
                      lq_i32=lq_f32.bitcast(I32), lq_f32=lq_f32, lq_bf=lq_bf,
                      q1c=scr16k[:, 3584:4096].bitcast(BF16),
                      smalls=smalls,
                      q1T=q1T, tern=tern, big16=big16, scr16k=scr16k, hsp=hsp,
                      M=M, Mn=Mn, s2=s2, inv_s2=inv_s2, comb=comb, acc=acc,
                      cw_sb=cw_sb, cb_sb=cb_sb, s0c=s0c, smc=smc, spc=spc,
                      sc1=sc1, wm1=wm1, sc2=sc2, wm2=wm2,
                      S2bc=scr16k[:, 3584:3840], I2bc=None, r1x=r1x)
            emit_l1ffn(nc, "g1", xs, ident, identf, onesf, g1w1, g1w2,
                       g1cw, g1cb, hs_d, ws, lnsh=(LN, 3))

        with (nc.semaphore("xout") as xo_s, nc.Block() as blk):
            @blk.sync
            def _(s):
                s.dma_start(x_out.rearrange("(n pp) m -> pp n m", pp=128),
                            xs[:, :, :]).then_inc(xo_s, 16)
                s.wait_ge(xo_s, 16)
    return nc


def _pack_all(inputs):
    wq0, wk0, wv0, wo0, ones3 = host_l0attn_weights(
        inputs["L0_attn_Wq"], inputs["L0_attn_Wk"], inputs["L0_attn_Wv"],
        inputs["L0_attn_Wo"])
    wg0, wu0, wd0 = host_l0ffn_weights(
        inputs["L0_ffn_w_gate"], inputs["L0_ffn_w_up"], inputs["L0_ffn_w_down"])
    wq1, wk1, wv1, wo1, bq1, bk1, bv1, bo1 = host_l1attn_weights(
        inputs["L1_attn_Wq"], inputs["L1_attn_bq"], inputs["L1_attn_Wk"],
        inputs["L1_attn_bk"], inputs["L1_attn_Wv"], inputs["L1_attn_bv"],
        inputs["L1_attn_Wo"], inputs["L1_attn_bo"])
    w11, w21, cw1, cb1 = host_l1ffn_weights(
        inputs["L1_ffn_w1"], inputs["L1_ffn_conv_w"], inputs["L1_ffn_conv_b"],
        inputs["L1_ffn_w2"])
    return {"a0wq": wq0, "a0wk": wk0, "a0wv": wv0, "a0wo": wo0,
            "ones3": ones3, "f0wg": wg0, "f0wu": wu0, "f0wd": wd0,
            "a1wq": wq1, "a1wk": wk1, "a1wv": wv1, "a1wo": wo1,
            "a1bq": bq1, "a1bk": bk1, "a1bv": bv1, "a1bo": bo1,
            "g1w1": w11, "g1w2": w21, "g1cw": cw1, "g1cb": cb1}


def kernel(**inputs):
    from concourse.bass_utils import run_bass_kernel_spmd as _run
    gi = np.asarray(inputs["L1_attn_global_indices"]).reshape(-1).tolist()
    assert gi == [0, 63], f"kernel specialized to global_indices=[0,63], got {gi}"
    x = np.ascontiguousarray(inputs["L0_attn_hidden_states"], dtype=np.float32)
    W = _pack_all(inputs)
    ncore = 8
    nc = build_fused()
    maps = [dict(W, x=np.ascontiguousarray(x[i])) for i in range(ncore)]
    r = _run(nc, maps, core_ids=list(range(ncore)))
    return np.stack([np.asarray(r.results[i]["xo"]) for i in range(ncore)], axis=0)
